# version 08
# baseline (speedup 1.0000x reference)
import numpy as np
import ml_dtypes
import concourse.bass as bass
import concourse.mybir as mybir
from concourse.bass_utils import run_bass_kernel_spmd
from contextlib import ExitStack

F32 = mybir.dt.float32
BF16 = mybir.dt.bfloat16
ALU = mybir.AluOpType
AF = mybir.ActivationFunctionType
AX = mybir.AxisListType
BF = ml_dtypes.bfloat16

B, L, D, H = 2, 8192, 1024, 4
DK = DV = 256
CH = 32
NG = 4
GA = L // NG
NTT = GA // 128
NCHG = GA // CH
NGI = NCHG // 4
TB = L // 4
NHB = 2
TBH = TB // NHB
EPS_ID, R_EPS = 0.06, 0.025
NE, FF, FSH = 8, 1408, 2816
CCM = 1024
CCS = 6 * NTT

DBG = False
STAGE = 6
NGB = NG
B_STAGE = 2


class Plan:
    ENG = ("tensor", "vector", "scalar", "gpsimd", "sync")

    def __init__(self):
        self.bucket = {e: [] for e in self.ENG}
        self.sems = {}

    def op(self, eng, fn, waits=(), incs=()):
        for s, k in incs:
            self.sems[s] = self.sems.get(s, 0) + k
        self.bucket[eng].append((fn, tuple(w for w in waits if w is not None), tuple(incs)))

    def tok(self, s):
        return (s, self.sems.get(s, 0))


def sbpp(shape, dtype):
    n = 1
    for d in shape[1:]:
        n *= d
    return n * (2 if dtype == BF16 else 4)


class SB:
    def __init__(self, nc, base):
        self.nc, self.off = nc, base

    def al(self, name, shape, dtype):
        off = (self.off + 31) & ~31
        t = self.nc.alloc_sbuf_tensor_at(name, shape, dtype, offset=off)
        self.off = off + sbpp(shape, dtype)
        return t


def build(nc):
    p = Plan()
    dram = {}

    def din(name, shape, dt=F32):
        dram[name] = nc.dram_tensor(name, shape, dt, kind="ExternalInput")
        return dram[name]

    def dout(name, shape, dt=F32):
        dram[name] = nc.dram_tensor(name, shape, dt, kind="ExternalOutput")
        return dram[name]

    din("x_fm", [D, L])
    din("xb_fm", [D, TB])
    din("wq", [D, DK]); din("wk", [D, DK]); din("wv", [D, DV]); din("wb", [128, 8])
    din("convqd", [8 * 128, 128]); din("convkd", [8 * 128, 128]); din("convvd", [8 * 128, 128])
    din("firsd", [6 * 128, 128]); din("firld", [62 * 128, 128])
    din("wid", [D, H]); din("bid", [H, 1]); din("alpha", [H, 1])
    din("wr1", [1152, 2048]); din("br1", [128, 16]); din("wr2", [2048, 12]); din("br2", [12, 1])
    din("ltg", [1, 2]); din("lth", [1, 4]); din("onwc", [128, 2])
    din("wo", [D, D]); din("gwt", [D, NE])
    din("wg", [NE, D, FF]); din("wu", [NE, D, FF]); din("wd", [NE, FF, D])
    din("swg", [D, FSH]); din("swu", [D, FSH]); din("swd", [FSH, D])
    din("identf", [128, 128]); din("identb", [128, 128], BF16)
    din("eye32s", [128, 32], BF16)
    din("masksl", [128, 32], BF16)
    din("masktu", [128, 32], BF16)
    din("onesf", [1, 128]); din("onesb", [1, 128], BF16)
    din("colb1", [128, 1], BF16); din("colb256", [128, 1], BF16); din("colb1024", [128, 1], BF16)

    dout("out", [TB, D])
    dram["ccm_ia"] = nc.dram_tensor("ccm_ia", [NG * CCM, GA], BF16)
    dram["ccm_oa"] = nc.dram_tensor("ccm_oa", [NG * 4 * CCM, GA], BF16)
    dram["ccs_ia"] = nc.dram_tensor("ccs_ia", [128, NG * CCS], BF16)
    dram["ccs_oa"] = nc.dram_tensor("ccs_oa", [4 * 128, NG * CCS], BF16)

    if DBG:
        dout("d_qfm", [128, 2 * GA], BF16); dout("d_kfm", [128, 2 * GA], BF16)
        dout("d_vfm", [128, 2 * GA], BF16); dout("d_knfm", [128, 2 * GA], BF16)
        dout("d_fs", [128, 2 * GA], BF16); dout("d_fl", [128, 2 * GA], BF16)
        dout("d_otm", [128, NTT * 256], BF16); dout("d_dfm", [128, 2 * GA], BF16)
        dout("d_statc", [128, CCS], BF16); dout("d_scols", [128, NTT * 8])
        dout("d_zb", [128, 8 * TBH], BF16)
        dout("d_ptm", [128, 8 * 12]); dout("d_gl8", [128, 8 * 8], BF16)
        dout("d_ofm", [128, 8 * 512], BF16)

    with ExitStack() as ctx:
        pb = [ctx.enter_context(nc.psum_tensor(f"pb{i}", [128, 512], F32)) for i in range(6)]
        pbt = ctx.enter_context(nc.psum_tensor("pbt", [128, 512], BF16))
        pbt2 = ctx.enter_context(nc.psum_tensor("pbt2", [128, 512], BF16))
        semnames = ["gl", "sl", "sw", "gw", "dw", "cc", "te", "ve", "se",
                    "exl", "prw", "crw", "wr1l", "wr1m", "wol", "wom", "wgl", "wul", "wdl"]
        semh = {s: ctx.enter_context(nc.semaphore(s)) for s in semnames}

        base = nc.sbuf_base + 64
        P0 = SB(nc, base)
        identf = P0.al("identf", [128, 128], F32)
        identb = P0.al("identb", [128, 128], BF16)
        eye32s = P0.al("eye32s", [128, 32], BF16)
        masksl = P0.al("masksl", [128, 32], BF16)
        masktu = P0.al("masktu", [128, 32], BF16)
        onesf = P0.al("onesf", [1, 128], F32)
        onesb = P0.al("onesb", [1, 128], BF16)
        colb1 = P0.al("colb1", [128, 1], BF16)
        colb256 = P0.al("colb256", [128, 1], BF16)
        colb1024 = P0.al("colb1024", [128, 1], BF16)
        wqb = P0.al("wqb", [128, 8 * 256], BF16)
        wkb = P0.al("wkb", [128, 8 * 256], BF16)
        wvb = P0.al("wvb", [128, 8 * 256], BF16)
        wbb = P0.al("wbb", [128, 8], BF16)
        PERM_END = P0.off

        A = SB(nc, PERM_END)
        GAp, VAp = GA + 3, GA + 30
        convqd = A.al("convqd", [128, 8 * 128], BF16)
        convkd = A.al("convkd", [128, 8 * 128], BF16)
        convvd = A.al("convvd", [128, 8 * 128], BF16)
        firsd = A.al("firsd", [128, 6 * 128], BF16)
        firld = A.al("firld", [128, 62 * 128], BF16)
        axf_off = (A.off + 31) & ~31
        axf = A.al("axf", [128, 8 * GA], BF16)
        qraw_off = (A.off + 31) & ~31
        qraw = A.al("qraw", [128, 2 * GAp], BF16)
        kraw_off = (A.off + 31) & ~31
        kraw = A.al("kraw", [128, 2 * GAp], BF16)
        vraw_off = (A.off + 31) & ~31
        vraw = A.al("vraw", [128, 2 * GAp], BF16)
        qfm = A.al("qfm", [128, 2 * GA], BF16)
        kfm = A.al("kfm", [128, 2 * GA], BF16)
        vfm = A.al("vfm", [128, 2 * VAp], BF16)
        knfm = A.al("knfm", [128, 2 * GA], BF16)
        fsb = A.al("fsb", [128, 2 * GA], BF16)
        flb = A.al("flb", [128, 2 * GA], BF16)
        sqbuf = A.al("sqbuf", [128, GA], BF16)
        kntm = nc.alloc_sbuf_tensor_at("kntm", [128, NTT * 256], BF16, offset=qraw_off)
        kbntm = nc.alloc_sbuf_tensor_at("kbntm", [128, NTT * 256], BF16, offset=kraw_off)
        vbtm = nc.alloc_sbuf_tensor_at("vbtm", [128, NTT * 256], BF16, offset=vraw_off)
        ubuf = nc.alloc_sbuf_tensor_at("ubuf", [128, NTT * 256], BF16, offset=axf_off)
        wnfm = nc.alloc_sbuf_tensor_at("wnfm", [128, NCHG * 64], BF16, offset=axf_off + 8192)
        otm = nc.alloc_sbuf_tensor_at("otm", [128, NTT * 256], BF16, offset=axf_off + 16384)
        dfm = nc.alloc_sbuf_tensor_at("dfm", [128, 2 * GA], BF16, offset=axf_off + 24576)
        atm = A.al("atm", [128, NGI * 32], BF16)
        Tpk = A.al("Tpk", [128, NGI * 32], BF16)
        TTpk = A.al("TTpk", [128, NGI * 32], BF16)
        Apk = A.al("Apk", [128, NGI * 32], BF16)
        ATpk = A.al("ATpk", [128, NGI * 32], BF16)
        u2sb = A.al("u2sb", [128, 2 * 256], BF16)
        Sbf = A.al("Sbf", [128, 512], BF16)
        S32 = A.al("S32", [128, 512], F32)
        halobuf = A.al("halobuf", [128, 3 * 6], BF16)
        srows = A.al("srows", [8, GA], F32)
        tmpr = A.al("tmpr", [1, GA], F32)
        rkrow = A.al("rkrow", [1, GA], F32)
        rkb = A.al("rkb", [128, 512], F32)
        scols = A.al("scols", [128, NTT * 8], F32)
        rqc = A.al("rqc", [128, NTT], F32)
        statc = A.al("statc", [128, CCS], BF16)
        dmcol = A.al("dmcol", [128, 2 * NTT], F32)
        stdw = A.al("stdw", [128, NTT], F32)
        assert A.off < nc.sbuf_top, f"A region overflow {A.off} > {nc.sbuf_top}"

        Br = SB(nc, PERM_END)
        xz = Br.al("xz", [128, 8 * TBH], F32)
        zb = Br.al("zb", [128, 8 * TBH], BF16)
        wr2b = Br.al("wr2b", [128, 16 * 12], BF16)
        widb = Br.al("widb", [128, 8 * 4], BF16)
        gwtb = Br.al("gwtb", [128, 8 * 8], F32)
        br1c = Br.al("br1c", [128, 16], F32)
        br2c = Br.al("br2c", [12, 1], F32)
        bidc = Br.al("bidc", [4, 1], F32)
        alphac = Br.al("alphac", [4, 1], F32)
        sigac = Br.al("sigac", [4, 1], F32)
        onwcb = Br.al("onwcb", [128, 2], F32)
        ltgr = Br.al("ltgr", [1, 2], F32)
        lthr = Br.al("lthr", [1, 4], F32)
        tau12 = Br.al("tau12", [1, 12], F32)
        tau12r = Br.al("tau12r", [1, 12], F32)
        taub = Br.al("taub", [128, 12], F32)
        logfm = Br.al("logfm", [12, TBH], F32)
        ltm = Br.al("ltm", [128, 8 * 12], F32)
        ptm = Br.al("ptm", [128, 8 * 12], F32)
        p4a = Br.al("p4a", [128, 4], F32)
        pfm12 = Br.al("pfm12", [12, 128], BF16)
        prow_off = (Br.off + 31) & ~31
        prow = Br.al("prow", [1, 12 * 512], BF16)
        irow = Br.al("irow", [1, 4 * 512], BF16)
        idt4 = Br.al("idt4", [4, 512], F32)
        idb4 = Br.al("idb4", [4, 512], BF16)
        combt = Br.al("combt", [128, 8 * 8], BF16)
        comb8 = Br.al("comb8", [8, 128], BF16)
        crow = nc.alloc_sbuf_tensor_at("crow", [1, 8 * TBH], BF16, offset=prow_off)
        g8t = Br.al("g8t", [128, 8], F32)
        gc1 = Br.al("gc1", [128, 1], F32)
        gc2 = Br.al("gc2", [128, 1], F32)
        lnrow = Br.al("lnrow", [1, 2048], F32)
        rmsrow = lnrow
        rbc = Br.al("rbc", [128, 512], F32)
        mrbc = Br.al("mrbc", [128, 512], F32)
        tmpf = Br.al("tmpf", [128, 512], F32)
        obuf = Br.al("obuf", [128, 2 * 512], BF16)
        scr = Br.al("scr", [128, 512], BF16)
        gsil = Br.al("gsil", [128, 512], BF16)
        cbuf = Br.al("cbuf", [128, 5 * 512], BF16)
        ofm = Br.al("ofm", [128, 8 * 512], BF16)
        sstat = Br.al("sstat", [128, 4 * 48], BF16)
        rinstat = Br.al("rinstat", [128, TBH], BF16)
        mucol = Br.al("mucol", [128, 16], F32)
        rcol = Br.al("rcol", [128, 8], F32)
        x2sq = Br.al("x2sq", [128, 1024], F32)
        outb = Br.al("outb", [128, 1024], F32)
        SREG = (Br.off + 31) & ~31
        bxf = nc.alloc_sbuf_tensor_at("bxf", [128, 8 * TBH], BF16, offset=SREG)
        hmid = nc.alloc_sbuf_tensor_at("hmid", [128, 16 * 512], BF16, offset=SREG + 16384)
        exch = nc.alloc_sbuf_tensor_at("exch", [128, 8 * 512], BF16, offset=SREG + 16384)
        exch2 = nc.alloc_sbuf_tensor_at("exch2", [128, 8 * 512], BF16, offset=SREG + 24576)
        wr1b = nc.alloc_sbuf_tensor_at("wr1b", [128, 2 * 9 * 128], BF16, offset=SREG + 32768)
        wob = nc.alloc_sbuf_tensor_at("wob", [128, 2 * 8 * 128], BF16, offset=SREG + 37504)
        wgb = nc.alloc_sbuf_tensor_at("wgb", [128, 8 * FF], BF16, offset=SREG)
        wub = nc.alloc_sbuf_tensor_at("wub", [128, 8 * FF], BF16, offset=SREG + 22528)
        wdb = nc.alloc_sbuf_tensor_at("wdb", [128, 11 * 1024], BF16, offset=SREG + 45056)
        gpr = nc.alloc_sbuf_tensor_at("gpr", [128, 11 * 512], BF16, offset=SREG + 67584)
        x2t8 = nc.alloc_sbuf_tensor_at("x2t8", [128, 8 * 1024], F32, offset=SREG)
        assert SREG + 78848 < nc.sbuf_top, f"B overflow {SREG + 78848}"

        def DMA(eng, sem, out, in_, waits=()):
            p.op(eng, lambda e, o=out, i=in_: e.dma_start(out=o, in_=i), waits, [(sem, 16)])
            return p.tok(sem)

        def MM(out, lhsT, rhs, start, stop, tp=None, waits=(), inc=None):
            p.op("tensor",
                 lambda e, o=out, l=lhsT, r=rhs, s1=start, s2=stop, t=tp:
                     e.matmul(o, l, r, start=s1, stop=s2, tile_position=t),
                 waits, [(inc, 1)] if inc else [])
            return p.tok("te") if inc else None

        def TR(out, in_, ident, waits=(), inc=None):
            p.op("tensor",
                 lambda e, o=out, i=in_, d=ident: e.transpose(o, i, d),
                 waits, [(inc, 1)] if inc else [])
            return p.tok("te") if inc else None

        def V(fn, waits=(), inc=None):
            p.op("vector", fn, list(waits) + [p.tok("ve")], [("ve", 1)])
            return p.tok("ve")

        def SC(fn, waits=(), inc=None):
            p.op("scalar", fn, list(waits) + [p.tok("se")], [("se", 1)])
            return p.tok("se")

        bg = {}

        def bgw(key):
            return [bg.get(key)]

        import dataclasses as _dc
        _scoff = {}

        def _scoff_get(e):
            if id(e) not in _scoff:
                _scoff[id(e)] = e.partition_id() % 4
            return _scoff[id(e)]

        def dynoff(e, ap, elems_per_token):
            return _dc.replace(ap, offset=ap.offset + _scoff_get(e) * (TB * elems_per_token))

        x_fm = dram["x_fm"]

        for nm, buf in [("identf", identf), ("onesf", onesf)]:
            DMA("sync", "sl", buf[:, :], dram[nm].ap())
        tok_const_sl = p.tok("sl")
        for nm, buf in [("identb", identb), ("eye32s", eye32s), ("masksl", masksl),
                        ("masktu", masktu), ("onesb", onesb), ("colb1", colb1),
                        ("colb256", colb256), ("colb1024", colb1024)]:
            DMA("gpsimd", "gl", buf[:, :], dram[nm].ap())
        for nm, buf in [("wq", wqb), ("wk", wkb), ("wv", wvb)]:
            w = dram[nm].ap().rearrange("(c p) n -> p c n", p=128)
            DMA("gpsimd", "gl", buf.ap().rearrange("p (c n) -> p c n", c=8), w)
        DMA("gpsimd", "gl", wbb[:, :], dram["wb"].ap())
        for nm, buf, cc in [("convqd", convqd, 8), ("convkd", convkd, 8),
                            ("convvd", convvd, 8), ("firsd", firsd, 6), ("firld", firld, 62)]:
            DMA("gpsimd", "gl", buf.ap().rearrange("p (c k) -> p c k", c=cc),
                dram[nm].ap().rearrange("(c p) k -> p c k", p=128))
        tok_const_gl = p.tok("gl")

        V(lambda e: e.memset(qraw[:, 0:3], 0.0))
        V(lambda e: e.memset(qraw[:, GAp:GAp + 3], 0.0))
        V(lambda e: e.memset(kraw[:, 0:3], 0.0))
        V(lambda e: e.memset(kraw[:, GAp:GAp + 3], 0.0))
        V(lambda e: e.memset(vraw[:, 0:3], 0.0))
        V(lambda e: e.memset(vraw[:, GAp:GAp + 3], 0.0))
        V(lambda e: e.memset(vfm[:, 0:30], 0.0))
        V(lambda e: e.memset(vfm[:, VAp:VAp + 30], 0.0))
        V(lambda e: e.memset(srows[0:8, :], 0.0))
        V(lambda e: e.memset(S32[:, :], 0.0))
        V(lambda e: e.memset(Sbf[:, :], 0.0), inc="ve")

        def a_group(g, bar):
            t0 = g * GA
            p.op("tensor", lambda e: e.nop(), bar, [("te", 1)])
            p.op("vector", lambda e: e.engine_nop(), bar, [("ve", 1)])
            p.op("scalar", lambda e: e.nop(), bar, [("se", 1)])
            p.op("gpsimd", lambda e: e.engine_nop(), bar, [])
            p.op("sync", lambda e: e.nop(), bar, [])
            barve, barse = p.tok("ve"), p.tok("se")

            def fin():
                if DBG and g == 0:
                    wv, ws = p.tok("ve"), p.tok("se")
                    dumps = [("d_qfm", qfm, 2 * GA, 1), ("d_kfm", kfm, 2 * GA, 1),
                             ("d_fs", fsb, 2 * GA, 2), ("d_fl", flb, 2 * GA, 2),
                             ("d_knfm", knfm, 2 * GA, 3), ("d_scols", scols, NTT * 8, 3),
                             ("d_otm", otm, NTT * 256, 5), ("d_dfm", dfm, 2 * GA, 6),
                             ("d_statc", statc, CCS, 6)]
                    for nm, src, w, st in dumps:
                        if STAGE >= st:
                            DMA("sync", "dw", dram[nm].ap(), src[:, 0:w], waits=[wv, ws])
                    if STAGE >= 1:
                        DMA("sync", "dw", dram["d_vfm"].ap(),
                            vfm.ap().rearrange("p (c t) -> p c t", c=2)[:, :, 30:30 + GA],
                            waits=[wv, ws])
                return [p.tok("te"), p.tok("ve"), p.tok("se"), p.tok("sl"), p.tok("gl"),
                        p.tok("sw"), p.tok("gw"), p.tok("dw")]

            xin = x_fm.ap().rearrange("(c p) t -> p c t", p=128)[:, :, t0:t0 + GA]
            xout = axf.ap().rearrange("p (c t) -> p c t", c=8)
            tok_x = DMA("gpsimd", "gl", xout, xin, waits=[tok_const_gl])
            if g > 0 and STAGE >= 6:
                gp = g - 1
                p.op("gpsimd", lambda e, g_=gp: e.collective_compute(
                    "AllGather", ALU.bypass, replica_groups=[[0, 1, 2, 3], [4, 5, 6, 7]],
                    ins=[dram["ccm_ia"].ap()[g_ * CCM:(g_ + 1) * CCM, :].opt()],
                    outs=[dram["ccm_oa"].ap()[g_ * 4 * CCM:(g_ + 1) * 4 * CCM, :].opt()]),
                    [exch_sw_last[gp]], [("cc", 1)])

            if g > 0:
                for bi, buf in enumerate([qraw, kraw, vraw]):
                    V(lambda e, b=buf, k=bi: e.tensor_copy(
                        b.ap().rearrange("p (c t) -> p c t", c=2)[:, :, 0:3],
                        halobuf.ap().rearrange("p (a c t) -> p a c t", a=3, c=2)[:, k]))
                V(lambda e: e.tensor_copy(
                    vfm.ap().rearrange("p (c t) -> p c t", c=2)[:, :, 0:30],
                    vfm.ap().rearrange("p (c t) -> p c t", c=2)[:, :, VAp - 30:VAp]))
            tok_halo = V(lambda e: e.engine_nop(), inc="ve")

            prtoks = {}
            for pi, (wb_, raw) in enumerate([(wqb, qraw), (wkb, kraw), (wvb, vraw)]):
                for mo in range(2):
                    for n in range(4):
                        bank = 1 + (pi * 8 + mo * 4 + n) % 2
                        for ki in range(8):
                            w0 = ([tok_x, tok_const_gl, barve] + bgw(bank)) if ki == 0 else []
                            tmm = MM(pb[bank][:, 0:512],
                                     wb_[:, ki * 256 + mo * 128:ki * 256 + (mo + 1) * 128],
                                     axf[:, ki * GA + n * 512:ki * GA + (n + 1) * 512],
                                     start=(ki == 0), stop=(ki == 7), waits=w0,
                                     inc="te" if ki == 7 else None)
                        dst = raw[:, mo * GAp + 3 + n * 512:mo * GAp + 3 + (n + 1) * 512]
                        ev = V(lambda e, d=dst, b=bank: e.tensor_copy(d, pb[b][:, 0:512]),
                               waits=[tmm], inc="ve")
                        prtoks[(pi, mo, n)] = ev
                        bg[bank] = ev
            for n in range(4):
                for ki in range(8):
                    w0 = ([tok_x] + bgw(3)) if ki == 0 else []
                    tmm = MM(pb[3][0:1, 0:512], wbb[:, ki:ki + 1],
                             axf[:, ki * GA + n * 512:ki * GA + (n + 1) * 512],
                             start=(ki == 0), stop=(ki == 7), waits=w0,
                             inc="te" if ki == 7 else None)
                ev = V(lambda e, n_=n: e.tensor_copy(tmpr[0:1, n_ * 512:(n_ + 1) * 512],
                                                     pb[3][0:1, 0:512]),
                       waits=[tmm], inc="ve")
                bg[3] = ev
            tok_beta = SC(lambda e: e.activation(tmpr[0:1, 0:GA], tmpr[0:1, 0:GA], AF.Sigmoid),
                          waits=[bg[3], barse], inc="se")
            tok_srow_b = DMA("sync", "sl", srows[1:2, 0:GA], tmpr[0:1, 0:GA], waits=[tok_beta])

            if g < NG - 1:
                for bi, buf in enumerate([qraw, kraw, vraw]):
                    V(lambda e, b=buf, k=bi: e.tensor_copy(
                        halobuf.ap().rearrange("p (a c t) -> p a c t", a=3, c=2)[:, k],
                        b.ap().rearrange("p (c t) -> p c t", c=2)[:, :, GAp - 3:GAp]))

            silu_toks = {}
            for pi, (raw, cwd, dst) in enumerate([(qraw, convqd, qfm), (kraw, convkd, kfm),
                                                  (vraw, convvd, vfm)]):
                for mo in range(2):
                    for n in range(4):
                        bank = 1 + (mo * 4 + n) % 2
                        for j in range(4):
                            w0 = ([prtoks[(pi, mo, nn)] for nn in range(4)] + [tok_halo]
                                  + bgw(bank)) if j == 0 else []
                            tmm = MM(pb[bank][:, 0:512],
                                     cwd[:, (mo * 4 + j) * 128:(mo * 4 + j + 1) * 128],
                                     raw[:, mo * GAp + j + n * 512:mo * GAp + j + n * 512 + 512],
                                     start=(j == 0), stop=(j == 3), waits=w0,
                                     inc="te" if j == 3 else None)
                        if pi == 2:
                            od = dst[:, mo * VAp + 30 + n * 512:mo * VAp + 30 + (n + 1) * 512]
                        else:
                            od = dst[:, mo * GA + n * 512:mo * GA + (n + 1) * 512]
                        ev = SC(lambda e, o=od, b=bank: e.activation(o, pb[b][:, 0:512],
                                                                    AF.Silu),
                                waits=[tmm], inc="se")
                        bg[bank] = ev
                    silu_toks[pi * 2 + mo] = p.tok("se")

            if STAGE <= 1:
                return fin()

            fir_ev = {}
            for si, (dstb, wd, ntap) in enumerate([(fsb, firsd, 3), (flb, firld, 31)]):
                for mo in range(2):
                    for n in range(4):
                        bank = 1 + (mo * 4 + n) % 2
                        for j in range(ntap):
                            off = mo * VAp + (30 - ntap + 1) + j + n * 512
                            w0 = ([silu_toks[4 + mo]] + bgw(bank)) if j == 0 else []
                            tmm = MM(pb[bank][:, 0:512],
                                     wd[:, (mo * ntap + j) * 128:(mo * ntap + j + 1) * 128],
                                     vfm[:, off:off + 512],
                                     start=(j == 0), stop=(j == ntap - 1), waits=w0,
                                     inc="te" if j == ntap - 1 else None)
                        ev = V(lambda e, d=dstb, m=mo, n_=n, b=bank: e.tensor_copy(
                            d[:, m * GA + n_ * 512:m * GA + (n_ + 1) * 512], pb[b][:, 0:512]),
                            waits=[tmm], inc="ve")
                        bg[bank] = ev
                fir_ev[si] = p.tok("ve")
            fir_toks = [(fir_ev[0], fir_ev[1]), (fir_ev[0], fir_ev[1])]

            if STAGE <= 2:
                return fin()

            sq_toks = {}
            for ti, src in enumerate([qfm, kfm]):
                for mo in range(2):
                    sq_toks[(ti, mo)] = SC(
                        lambda e, s=src, m=mo: e.square(sqbuf[:, :], s[:, m * GA:(m + 1) * GA]),
                        waits=[silu_toks[ti * 2 + mo], p.tok("te")], inc="se")
                    dstrow = srows if ti == 0 else rkrow
                    for n in range(4):
                        bank = [4, 5, 1, 2][n]
                        w0 = [sq_toks[(ti, mo)]] + (bgw(bank) if mo == 0 else [])
                        tmm = MM(pb[bank][0:1, 0:512], colb1[:, :],
                                 sqbuf[:, n * 512:(n + 1) * 512],
                                 start=(mo == 0), stop=(mo == 1), waits=w0, inc="te")
                        if mo == 1:
                            ev = V(lambda e, n_=n, b=bank, dr=dstrow:
                                   e.tensor_copy(dr[0:1, n_ * 512:(n_ + 1) * 512],
                                                 pb[b][0:1, 0:512]),
                                   waits=[tmm], inc="ve")
                            bg[bank] = ev
            tok_ssq = p.tok("ve")

            if STAGE <= 2.2:
                return fin()

            t_eps = V(lambda e: e.tensor_scalar(rkrow[0:1, :], rkrow[0:1, :], 1e-6, None,
                                                ALU.add), waits=[tok_ssq], inc="ve")
            tok_rk1 = V(lambda e: e.reciprocal(rkrow[0:1, :], rkrow[0:1, :]),
                        waits=[t_eps], inc="ve")
            tok_rk = SC(lambda e: e.activation(rkrow[0:1, :], rkrow[0:1, :], AF.Sqrt),
                        waits=[tok_rk1], inc="se")

            if STAGE <= 2.3:
                return fin()

            prev_tmpr = tok_srow_b
            stat_dma = []
            for si, src in enumerate([fsb, flb]):
                ftk = [fir_toks[0][si], fir_toks[1][si]]
                for qty in range(2):
                    row = 2 + 2 * si + qty
                    evs = []
                    tsq = None
                    for mo in range(2):
                        if qty == 1:
                            tsq = SC(lambda e, s=src, m=mo: e.square(sqbuf[:, :],
                                                                     s[:, m * GA:(m + 1) * GA]),
                                     waits=ftk + [p.tok("te")], inc="se")
                        for n in range(4):
                            bank = [4, 5, 1, 2][n]
                            if qty == 0:
                                rhs = src[:, mo * GA + n * 512:mo * GA + (n + 1) * 512]
                                w0 = (ftk + bgw(bank)) if mo == 0 else []
                            else:
                                rhs = sqbuf[:, n * 512:(n + 1) * 512]
                                w0 = [tsq] + (bgw(bank) if mo == 0 else [])
                            tmm = MM(pb[bank][0:1, 0:512], colb256[:, :], rhs,
                                     start=(mo == 0), stop=(mo == 1), waits=w0, inc="te")
                            if mo == 1:
                                ev = V(lambda e, n_=n, b=bank: e.tensor_copy(
                                    tmpr[0:1, n_ * 512:(n_ + 1) * 512], pb[b][0:1, 0:512]),
                                    waits=[tmm, prev_tmpr], inc="ve")
                                bg[bank] = ev
                                evs.append(ev)
                    prev_tmpr = DMA("sync", "sl", srows[row:row + 1, 0:GA], tmpr[0:1, 0:GA],
                                    waits=evs)
                    stat_dma.append(prev_tmpr)

            if STAGE <= 2.4:
                return fin()

            tok_knfm = {}
            for n in range(4):
                tmm = MM(pb[3][:, 0:512], onesf[:, :], rkrow[0:1, n * 512:(n + 1) * 512],
                         start=True, stop=True, waits=[tok_rk] + bgw(3), inc="te")
                tok_rkb = V(lambda e: e.tensor_copy(rkb[:, :], pb[3][:, 0:512]),
                            waits=[tmm], inc="ve")
                bg[3] = tok_rkb
                for mo in range(2):
                    tok_knfm[(mo, n)] = V(
                        lambda e, m=mo, n_=n: e.tensor_tensor(
                            knfm[:, m * GA + n_ * 512:m * GA + (n_ + 1) * 512],
                            kfm[:, m * GA + n_ * 512:m * GA + (n_ + 1) * 512],
                            rkb[:, :], ALU.mult),
                        waits=[tok_rkb], inc="ve")

            if STAGE <= 2.5:
                return fin()

            tok_scol = {}
            for tt in range(NTT):
                w0 = ([tok_srow_b, tok_ssq] + stat_dma) if tt == 0 else []
                tmm = TR(pb[2][:, 0:8], srows[0:8, tt * 128:(tt + 1) * 128], identf[0:8, 0:8],
                         waits=w0 + bgw(2), inc="te")
                ev = V(lambda e, t=tt: e.tensor_copy(scols[:, t * 8:(t + 1) * 8], pb[2][:, 0:8]),
                       waits=[tmm], inc="ve")
                tok_scol[tt] = ev
                bg[2] = ev
            sc8 = scols.ap().rearrange("p (t c) -> p t c", c=8)
            tok_scall = p.tok("ve")
            t_eps2 = V(lambda e: e.tensor_scalar(rqc[:, :], sc8[:, :, 0], 1e-6, None, ALU.add),
                       waits=[tok_scall], inc="ve")
            V(lambda e: e.reciprocal(rqc[:, :], rqc[:, :]), waits=[t_eps2], inc="ve")
            tok_rqc = SC(lambda e: e.activation(rqc[:, :], rqc[:, :], AF.Sqrt),
                         waits=[p.tok("ve")], inc="se")

            st6 = statc.ap().rearrange("p (t c) -> p t c", c=6)
            for si in range(2):
                mcol = sc8[:, :, 2 + 2 * si]
                qcol = sc8[:, :, 3 + 2 * si]
                V(lambda e, m=mcol, s=si: e.tensor_copy(st6[:, :, 2 * s], m), waits=[tok_scall])
                t1 = V(lambda e, m=mcol: e.tensor_tensor(stdw[:, :], m, m, ALU.mult), inc="ve")
                t2 = V(lambda e, q=qcol: e.tensor_tensor(stdw[:, :], q, stdw[:, :], ALU.subtract),
                       waits=[t1], inc="ve")
                V(lambda e: e.tensor_scalar(stdw[:, :], stdw[:, :], 0.0, None, ALU.max),
                  waits=[t2], inc="ve")
                tv = SC(lambda e: e.activation(stdw[:, :], stdw[:, :], AF.Sqrt),
                        waits=[p.tok("ve")], inc="se")
                V(lambda e, s=si: e.tensor_copy(st6[:, :, 2 * s + 1], stdw[:, :]),
                  waits=[tv], inc="ve")
            tok_statfl = p.tok("ve")

            if STAGE <= 2.7:
                return fin()

            for tt in range(NTT):
                bcol = scols[:, tt * 8 + 1:tt * 8 + 2]
                for mo in range(2):
                    w0 = [tok_knfm[(m, n)] for m in range(2) for n in range(4)] \
                        if (tt == 0 and mo == 0) else []
                    tmm = TR(pbt[:, 0:128], knfm[:, mo * GA + tt * 128:mo * GA + (tt + 1) * 128],
                             identb[:, :], waits=w0 + bgw("t0"), inc="te")
                    V(lambda e, t=tt, m=mo: e.tensor_copy(
                        kntm[:, t * 256 + m * 128:t * 256 + (m + 1) * 128], pbt[:, 0:128]),
                      waits=[tmm], inc="ve")
                    ev = V(lambda e, t=tt, m=mo, b=bcol: e.tensor_scalar(
                        kbntm[:, t * 256 + m * 128:t * 256 + (m + 1) * 128], pbt[:, 0:128],
                        b, -1.0, ALU.mult, ALU.mult),
                        waits=[tok_scol[tt]], inc="ve")
                    bg["t0"] = ev
                    tmm2 = TR(pbt2[:, 0:128],
                              vfm[:, mo * VAp + 30 + tt * 128:mo * VAp + 30 + (tt + 1) * 128],
                              identb[:, :],
                              waits=([silu_toks[4 + mo]] if tt == 0 else []) + bgw("t1"),
                              inc="te")
                    ev2 = V(lambda e, t=tt, m=mo, b=bcol: e.tensor_scalar(
                        vbtm[:, t * 256 + m * 128:t * 256 + (m + 1) * 128], pbt2[:, 0:128],
                        b, None, ALU.mult),
                        waits=[tmm2, tok_scol[tt]], inc="ve")
                    bg["t1"] = ev2
            tok_tm_all = p.tok("ve")

            if STAGE <= 3:
                return fin()

            for gi in range(NGI):
                for ci in range(4):
                    i = gi * 4 + ci
                    for mo in range(2):
                        w0 = (bgw(4) + [tok_knfm[(m, n)] for m in range(2) for n in range(4)]) \
                            if (gi == 0 and ci == 0 and mo == 0) else []
                        tmm = MM(pb[4][32 * ci:32 * ci + 32, gi * 32:(gi + 1) * 32],
                                 knfm[:, mo * GA + i * 32:mo * GA + (i + 1) * 32],
                                 knfm[:, mo * GA + i * 32:mo * GA + (i + 1) * 32],
                                 start=(mo == 0), stop=(mo == 1), tp=(0, 32 * ci),
                                 waits=w0,
                                 inc="te" if (gi == NGI - 1 and ci == 3 and mo == 1) else None)
            tok_Amm = tmm
            evA = []
            for gi in range(NGI):
                bcol = scols[:, gi * 8 + 1:gi * 8 + 2]
                evA.append(V(lambda e, g_=gi, b=bcol: e.scalar_tensor_tensor(
                    Apk[:, g_ * 32:(g_ + 1) * 32], pb[4][:, g_ * 32:(g_ + 1) * 32], b,
                    masksl[:, :], ALU.mult, ALU.mult),
                    waits=[tok_Amm, tok_scol[gi]], inc="ve"))
            bg[4] = evA[-1]
            eyeb = eye32s.ap().rearrange("p (o c) -> p o c", o=1).broadcast_to([128, NGI, 32])
            tok_T = V(lambda e: e.tensor_tensor(
                Tpk.ap().rearrange("p (g c) -> p g c", g=NGI),
                Apk.ap().rearrange("p (g c) -> p g c", g=NGI), eyeb, ALU.add),
                waits=evA, inc="ve")
            for lvl in range(4):
                tok_at = V(lambda e: e.transpose(ATpk[:, :], Apk[:, :]), waits=[tok_T], inc="ve")
                for gi in range(NGI):
                    for ci in range(4):
                        w0 = ([tok_at] + bgw(4)) if (gi == 0 and ci == 0) else []
                        tmm = MM(pb[4][32 * ci:32 * ci + 32, gi * 32:(gi + 1) * 32],
                                 ATpk[32 * ci:32 * ci + 32, gi * 32:(gi + 1) * 32],
                                 Apk[32 * ci:32 * ci + 32, gi * 32:(gi + 1) * 32],
                                 start=True, stop=True, tp=(32 * ci, 32 * ci),
                                 waits=w0, inc="te" if (gi == NGI - 1 and ci == 3) else None)
                tok_A2 = V(lambda e: e.tensor_copy(Apk[:, :], pb[4][:, :]), waits=[tmm], inc="ve")
                bg[4] = tok_A2
                tok_a2t = V(lambda e: e.transpose(ATpk[:, :], Apk[:, :]), waits=[tok_A2], inc="ve")
                for gi in range(NGI):
                    for ci in range(4):
                        w0 = ([tok_a2t] + bgw(5)) if (gi == 0 and ci == 0) else []
                        tmm = MM(pb[5][32 * ci:32 * ci + 32, gi * 32:(gi + 1) * 32],
                                 ATpk[32 * ci:32 * ci + 32, gi * 32:(gi + 1) * 32],
                                 Tpk[32 * ci:32 * ci + 32, gi * 32:(gi + 1) * 32],
                                 start=True, stop=True, tp=(32 * ci, 32 * ci),
                                 waits=w0, inc="te" if (gi == NGI - 1 and ci == 3) else None)
                tok_T = V(lambda e: e.scalar_tensor_tensor(Tpk[:, :], pb[5][:, :], 1.0, Tpk[:, :],
                                                           ALU.mult, ALU.add),
                          waits=[tmm], inc="ve")
                bg[5] = tok_T
            tok_TT = V(lambda e: e.transpose(TTpk[:, :], Tpk[:, :]), waits=[tok_T], inc="ve")

            for gi in range(NGI):
                for ci in range(4):
                    i = gi * 4 + ci
                    w0 = ([tok_TT, tok_tm_all] + bgw(5) + bgw(3)) if ci == 0 else []
                    MM(pb[5][32 * ci:32 * ci + 32, 0:256],
                       TTpk[32 * ci:32 * ci + 32, gi * 32:(gi + 1) * 32],
                       vbtm[32 * ci:32 * ci + 32, gi * 256:(gi + 1) * 256],
                       start=True, stop=True, tp=(32 * ci, 32 * ci), waits=w0)
                    for mo in range(2):
                        MM(pb[3][32 * ci:32 * ci + 32, 256:288],
                           knfm[:, mo * GA + i * 32:mo * GA + (i + 1) * 32],
                           qfm[:, mo * GA + i * 32:mo * GA + (i + 1) * 32],
                           start=(mo == 0), stop=(mo == 1), tp=(0, 32 * ci))
                    for mo in range(2):
                        tmm = MM(pb[3][:, (ci * 2 + mo) * 32:(ci * 2 + mo + 1) * 32],
                                 kbntm[32 * ci:32 * ci + 32,
                                       gi * 256 + mo * 128:gi * 256 + (mo + 1) * 128],
                                 TTpk[32 * ci:32 * ci + 32, gi * 32:(gi + 1) * 32],
                                 start=True, stop=True, tp=(32 * ci, 0),
                                 inc="te" if (ci == 3 and mo == 1) else None)
                e1 = V(lambda e, g_=gi: e.tensor_copy(ubuf[:, g_ * 256:(g_ + 1) * 256],
                                                      pb[5][:, 0:256]), waits=[tmm], inc="ve")
                V(lambda e, g_=gi: e.tensor_tensor(atm[:, g_ * 32:(g_ + 1) * 32],
                                                   pb[3][:, 256:288], masktu[:, :], ALU.mult),
                  inc="ve")
                e3 = V(lambda e, g_=gi: e.tensor_copy(wnfm[:, g_ * 256:(g_ + 1) * 256],
                                                      pb[3][:, 0:256]), inc="ve")
                bg[5], bg[3] = e1, e3
            tok_batch = p.tok("ve")

            if STAGE <= 4:
                return fin()

            prev_sevac = tok_batch
            prev_oevac = {0: None, 1: None, 2: None, 3: None}
            for i in range(NCHG):
                gi, ci = i // 4, i % 4
                gchunk = g * NCHG + i
                slot = (i // 4) % 2
                first = (gchunk == 0)
                w0 = [tok_batch, prev_sevac] + bgw("u2" + str(slot))
                t_u2mm = MM(pb[5][32 * ci:32 * ci + 32, 256:512],
                            eye32s[32 * ci:32 * ci + 32, :],
                            ubuf[32 * ci:32 * ci + 32, gi * 256:(gi + 1) * 256],
                            start=True, stop=first, tp=(32 * ci, 32 * ci), waits=w0,
                            inc="te" if first else None)
                if not first:
                    for mo in range(2):
                        t_u2mm = MM(pb[5][32 * ci:32 * ci + 32, 256:512],
                                    wnfm[:, i * 64 + mo * 32:i * 64 + (mo + 1) * 32],
                                    Sbf[:, mo * 256:(mo + 1) * 256],
                                    start=False, stop=(mo == 1), tp=(0, 32 * ci),
                                    inc="te" if mo == 1 else None)
                t_u2 = SC(lambda e, c=ci, s=slot: e.copy(
                    u2sb[32 * c:32 * c + 32, s * 256:(s + 1) * 256],
                    pb[5][32 * c:32 * c + 32, 256:512]), waits=[t_u2mm], inc="se")
                bg["u2" + str(slot)] = t_u2
                if not first:
                    for mo in range(2):
                        MM(pb[2][32 * ci:32 * ci + 32, 0:256],
                           qfm[:, mo * GA + i * 32:mo * GA + (i + 1) * 32],
                           Sbf[:, mo * 256:(mo + 1) * 256],
                           start=(mo == 0), stop=False, tp=(0, 32 * ci),
                           waits=(([prev_oevac[ci]] if prev_oevac[ci] else bgw(2))
                                  if mo == 0 else ()))
                MM(pb[2][32 * ci:32 * ci + 32, 0:256],
                   atm[32 * ci:32 * ci + 32, gi * 32:(gi + 1) * 32],
                   u2sb[32 * ci:32 * ci + 32, slot * 256:(slot + 1) * 256],
                   start=first, stop=True, tp=(32 * ci, 32 * ci),
                   waits=[t_u2] + (([prev_oevac[ci]] if prev_oevac[ci] else bgw(2))
                                   if first else []))
                for mo in range(2):
                    tmm = MM(pb[0][:, mo * 256:(mo + 1) * 256],
                             kntm[32 * ci:32 * ci + 32,
                                  gi * 256 + mo * 128:gi * 256 + (mo + 1) * 128],
                             u2sb[32 * ci:32 * ci + 32, slot * 256:(slot + 1) * 256],
                             start=True, stop=True, tp=(32 * ci, 0),
                             inc="te" if mo == 1 else None)
                t_smm = tmm
                prev_oevac[ci] = SC(lambda e, c=ci, g_=gi: e.activation(
                    otm[32 * c:32 * c + 32, g_ * 256:(g_ + 1) * 256],
                    pb[2][32 * c:32 * c + 32, 0:256], AF.Copy,
                    scale=rqc[32 * c:32 * c + 32, g_:g_ + 1]),
                    waits=[t_smm] + ([tok_rqc] if i == 0 else []), inc="se")
                V(lambda e: e.tensor_tensor(S32[:, :], pb[0][:, :], S32[:, :], ALU.add),
                  waits=[t_smm])
                prev_sevac = V(lambda e: e.tensor_copy(Sbf[:, :], S32[:, :]), inc="ve")
            tok_scan = p.tok("ve")

            if STAGE <= 5:
                return fin()

            dstt = None
            for tt in range(NTT):
                for mo in range(2):
                    w0 = [tok_scan] if (tt == 0 and mo == 0) else []
                    tmm = TR(pbt[:, 0:128], otm[:, tt * 256 + mo * 128:tt * 256 + (mo + 1) * 128],
                             identb[:, :], waits=w0 + bgw("t0"), inc="te")
                    ev = V(lambda e, t=tt, m=mo: e.tensor_copy(
                        dfm[:, m * GA + t * 128:m * GA + (t + 1) * 128], pbt[:, 0:128]),
                        waits=[tmm], inc="ve")
                    bg["t0"] = ev
                V(lambda e, t=tt: e.tensor_reduce(dmcol[:, t:t + 1], otm[:, t * 256:(t + 1) * 256],
                                                  AX.X, ALU.add),
                  waits=[tok_scan, tok_statfl] if tt == 0 else ())
                dstt = V(lambda e, t=tt: e.scalar_tensor_tensor(
                    sqbuf[:, 0:256], otm[:, t * 256:(t + 1) * 256], 1.0,
                    otm[:, t * 256:(t + 1) * 256], ALU.mult, ALU.mult,
                    accum_out=dmcol[:, NTT + t:NTT + t + 1]), inc="ve")
            td1 = V(lambda e: e.tensor_scalar(dmcol[:, :], dmcol[:, :], 1.0 / 256, None,
                                              ALU.mult), waits=[dstt], inc="ve")
            V(lambda e: e.tensor_copy(st6[:, :, 4], dmcol[:, 0:NTT]), waits=[td1])
            td2 = V(lambda e: e.tensor_tensor(stdw[:, :], dmcol[:, 0:NTT], dmcol[:, 0:NTT],
                                              ALU.mult), waits=[td1], inc="ve")
            td3 = V(lambda e: e.tensor_tensor(stdw[:, :], dmcol[:, NTT:2 * NTT], stdw[:, :],
                                              ALU.subtract), waits=[td2], inc="ve")
            V(lambda e: e.tensor_scalar(stdw[:, :], stdw[:, :], 0.0, None, ALU.max),
              waits=[td3], inc="ve")
            tds = SC(lambda e: e.activation(stdw[:, :], stdw[:, :], AF.Sqrt),
                     waits=[p.tok("ve")], inc="se")
            tok_statc = V(lambda e: e.tensor_copy(st6[:, :, 5], stdw[:, :]),
                          waits=[tds], inc="ve")
            tok_dfm = p.tok("ve")

            ccm = dram["ccm_ia"]
            ccs = dram["ccs_ia"]
            for rowbase, src, pitch, off in [(0, fsb, GA, 0), (256, flb, GA, 0),
                                             (512, dfm, GA, 0), (768, vfm, VAp, 30)]:
                for mo in range(2):
                    exch_wtoks.append(DMA(
                        "sync", "sw",
                        ccm.ap()[g * CCM + rowbase + mo * 128:
                                 g * CCM + rowbase + (mo + 1) * 128, :],
                        src[:, mo * pitch + off:mo * pitch + off + GA],
                        waits=[tok_dfm]))
            exch_wtoks.append(DMA("gpsimd", "gw",
                                  ccs.ap()[:, g * CCS:(g + 1) * CCS],
                                  statc[:, :], waits=[tok_statc]))
            exch_sw_last[g] = p.tok("sw")

            return fin()

        exch_wtoks = []
        exch_sw_last = {}
        bar = [p.tok("ve"), tok_const_sl, tok_const_gl]
        for g in range(NGB):
            bar = a_group(g, bar)
        if STAGE >= 6:
            p.op("gpsimd", lambda e: e.collective_compute(
                "AllGather", ALU.bypass, replica_groups=[[0, 1, 2, 3], [4, 5, 6, 7]],
                ins=[dram["ccs_ia"].ap().opt()], outs=[dram["ccs_oa"].ap().opt()]),
                [p.tok("gw")], [("cc", 1)])
            tok_cc_stats = p.tok("cc")
            gl_ = NGB - 1
            p.op("gpsimd", lambda e, g_=gl_: e.collective_compute(
                "AllGather", ALU.bypass, replica_groups=[[0, 1, 2, 3], [4, 5, 6, 7]],
                ins=[dram["ccm_ia"].ap()[g_ * CCM:(g_ + 1) * CCM, :].opt()],
                outs=[dram["ccm_oa"].ap()[g_ * 4 * CCM:(g_ + 1) * 4 * CCM, :].opt()]),
                [exch_sw_last[gl_]], [("cc", 1)])
        tok_cc_all = p.tok("cc") if p.sems.get("cc") else None
        if not p.sems.get("cc"):
            tok_cc_stats = None

        BSTAGE = B_STAGE
        bstart = bar
        p.op("tensor", lambda e: e.nop(), bstart, [("te", 1)])
        p.op("vector", lambda e: e.engine_nop(), bstart, [("ve", 1)])
        p.op("scalar", lambda e: e.nop(), bstart, [("se", 1)])
        p.op("gpsimd", lambda e: e.engine_nop(), bstart, [])
        p.op("sync", lambda e: e.nop(), bstart, [])

        if BSTAGE == 0 or STAGE < 6:
            tok_z = V(lambda e: e.memset(outb[:, :], 0.0), inc="ve")
            for r in range(TB // 128):
                DMA("sync", "sw", dram["out"].ap()[r * 128:(r + 1) * 128, :], outb[:, :],
                    waits=[tok_z])
        else:
            xb = dram["xb_fm"]
            t_wr2 = DMA("gpsimd", "gl",
                        wr2b.ap().rearrange("p (c n) -> p c n", c=16),
                        dram["wr2"].ap().rearrange("(c p) n -> p c n", p=128))
            t_wid = DMA("gpsimd", "gl",
                        widb.ap().rearrange("p (c n) -> p c n", c=8),
                        dram["wid"].ap().rearrange("(c p) n -> p c n", p=128))
            t_gwt = DMA("sync", "sl",
                        gwtb.ap().rearrange("p (c n) -> p c n", c=8),
                        dram["gwt"].ap().rearrange("(c p) n -> p c n", p=128))
            for nm, buf in [("br1", br1c), ("br2", br2c), ("bid", bidc), ("alpha", alphac),
                            ("onwc", onwcb), ("ltg", ltgr), ("lth", lthr)]:
                DMA("sync", "sl", buf[:, :], dram[nm].ap())
            t_bload = p.tok("sl")
            t_gload = p.tok("gl")
            t_sa = SC(lambda e: e.activation(sigac[:, :], alphac[:, :], AF.Sigmoid),
                      waits=[t_bload], inc="se")
            t_eg = SC(lambda e: e.activation(ltgr[:, :], ltgr[:, :], AF.Exp), inc="se")
            t_eh = SC(lambda e: e.activation(lthr[:, :], lthr[:, :], AF.Exp), inc="se")
            t_t1 = V(lambda e: e.tensor_copy(
                tau12.ap().rearrange("o (g x) -> o g x", g=2),
                ltgr.ap().rearrange("o (g x) -> o g x", x=1).broadcast_to([1, 2, 6])),
                waits=[t_eg], inc="ve")
            t_t2 = V(lambda e: e.scalar_tensor_tensor(
                tau12.ap().rearrange("o (g x) -> o g x", g=4),
                lthr.ap().rearrange("o (g x) -> o g x", x=1).broadcast_to([1, 4, 3]),
                0.0, tau12.ap().rearrange("o (g x) -> o g x", g=4), ALU.mult, ALU.add),
                waits=[t_eh, t_t1], inc="ve")
            t_t3 = V(lambda e: e.reciprocal(tau12r[:, :], tau12[:, :]), waits=[t_t2], inc="ve")
            tmm = MM(pb[5][:, 0:12], onesf[:, :], tau12r[0:1, 0:12], start=True, stop=True,
                     waits=[t_t3] + bgw(5), inc="te")
            t_taub = V(lambda e: e.tensor_copy(taub[:, :], pb[5][:, 0:12]), waits=[tmm], inc="ve")
            bg[5] = t_taub

            wr1c = [0]
            woc = [0]
            ec = [0]
            wr1guard = {0: None, 1: None}
            woguard = {0: None, 1: None}

            def b_half(hb):
                thc = hb * TBH
                t_bxf = DMA("gpsimd", "gl",
                            bxf.ap().rearrange("p (c t) -> p c t", c=8),
                            xb.ap().rearrange("(c p) t -> p c t", p=128)[:, :, thc:thc + TBH],
                            waits=[p.tok("ve"), t_gload])
                t_xz = DMA("sync", "sl",
                           xz.ap().rearrange("p (c t) -> p c t", c=8),
                           xb.ap().rearrange("(c p) t -> p c t", p=128)[:, :, thc:thc + TBH],
                           waits=[p.tok("ve")])
                sst_src = dram["ccs_oa"].ap().rearrange("(j p) c -> p j c", p=128)[
                    :, :, hb * 48:(hb + 1) * 48]
                p.op("sync",
                     lambda e, o=sstat.ap().rearrange("p (j c) -> p j c", j=4), i=sst_src:
                         e.dma_start(out=o, in_=_dc.replace(
                             i, offset=i.offset + _scoff_get(e) * (NG * CCS // 4) * 4
                             if False else i.offset + _scoff_get(e) * CCS)),
                     [t_xz, tok_cc_stats], [("sl", 16)])
                t_sst = p.tok("sl")
                t_rz = V(lambda e: e.memset(rinstat[:, :], 0.0), waits=[p.tok("te")], inc="ve")
                for j in range(4):
                    for tt in range(8):
                        w0 = [t_sst, t_rz] + bgw("t0")
                        tmm = TR(pbt[0:6, 0:128],
                                 sstat[:, j * 48 + tt * 6:j * 48 + (tt + 1) * 6],
                                 identb[:, :], waits=w0, inc="te")
                        ev = V(lambda e, j_=j, t=tt: e.tensor_copy(
                            rinstat[32 * j_:32 * j_ + 6, t * 128:(t + 1) * 128],
                            pbt[0:6, 0:128]), waits=[tmm], inc="ve")
                        bg["t0"] = ev
                tok_rin = p.tok("ve")

                for sg in range(2):
                    for mo in range(16):
                        slot = mo % 2
                        wr1c[0] += 1
                        t_w = DMA("gpsimd", "wr1l" if slot == 0 else "wr1m",
                                  wr1b.ap()[:, slot * 1152:(slot + 1) * 1152]
                                  .rearrange("p (c n) -> p c n", c=9),
                                  dram["wr1"].ap().rearrange("(c p) n -> p c n", p=128)
                                  [:, :, mo * 128:(mo + 1) * 128],
                                  waits=[wr1guard[slot]])
                        bank = 1 + mo % 2
                        for ki in range(9):
                            rhs = bxf[:, ki * TBH + sg * 512:ki * TBH + (sg + 1) * 512]                                 if ki < 8 else rinstat[:, sg * 512:(sg + 1) * 512]
                            w0 = ([t_w, t_bxf, tok_rin] + bgw(bank)) if ki == 0 else []
                            tmm = MM(pb[bank][:, 0:512],
                                     wr1b[:, slot * 1152 + ki * 128:slot * 1152 + (ki + 1) * 128],
                                     rhs, start=(ki == 0), stop=(ki == 8), waits=w0,
                                     inc="te" if ki == 8 else None)
                        wr1guard[slot] = tmm
                        ev = SC(lambda e, m=mo, s=sg, b=bank: e.activation(
                            hmid[:, m * 512:(m + 1) * 512], pb[b][:, 0:512], AF.Gelu,
                            bias=br1c[:, m:m + 1]), waits=[tmm, t_bload], inc="se")
                        bg[bank] = ev
                    t_hm = p.tok("se")
                    for ki in range(16):
                        w0 = ([t_hm, t_gload] + bgw(3)) if ki == 0 else []
                        tmm = MM(pb[3][0:12, 0:512], wr2b[:, ki * 12:(ki + 1) * 12],
                                 hmid[:, ki * 512:(ki + 1) * 512],
                                 start=(ki == 0), stop=(ki == 15), waits=w0,
                                 inc="te" if ki == 15 else None)
                    ev = V(lambda e, s=sg: e.tensor_scalar(
                        logfm[0:12, s * 512:(s + 1) * 512], pb[3][0:12, 0:512],
                        br2c[0:12, 0:1], None, ALU.add), waits=[tmm, t_bload], inc="ve")
                    bg[3] = ev
                tok_log = p.tok("ve")

                for tt in range(8):
                    w0 = ([tok_log] if tt == 0 else []) + bgw(2)
                    tmm = TR(pb[2][:, 0:12], logfm[0:12, tt * 128:(tt + 1) * 128],
                             identf[0:12, 0:12], waits=w0, inc="te")
                    ev = V(lambda e, t=tt: e.tensor_copy(ltm[:, t * 12:(t + 1) * 12],
                                                         pb[2][:, 0:12]), waits=[tmm], inc="ve")
                    bg[2] = ev
                for tt in range(8):
                    pv = ptm.ap().rearrange("p (t c) -> p t c", c=12)[:, tt]
                    pv3 = ptm.ap().rearrange("p (t h c) -> p t h c", t=8, h=4)[:, tt]
                    t1 = V(lambda e, t=tt: e.tensor_tensor(
                        ptm[:, t * 12:(t + 1) * 12], ltm[:, t * 12:(t + 1) * 12],
                        taub[:, :], ALU.mult), waits=[t_taub], inc="ve")
                    t2 = V(lambda e, v3=pv3: e.tensor_reduce(p4a[:, :], v3, AX.X, ALU.max,
                                                             negate=True),
                           waits=[t1], inc="ve")
                    t3 = V(lambda e, v3=pv3: e.tensor_tensor(
                        v3, v3, p4a.ap().rearrange("p (h x) -> p h x", x=1)
                        .broadcast_to([128, 4, 3]), ALU.add), waits=[t2], inc="ve")
                    t4 = SC(lambda e, t=tt: e.activation(ptm[:, t * 12:(t + 1) * 12],
                                                         ptm[:, t * 12:(t + 1) * 12], AF.Exp),
                            waits=[t3], inc="se")
                    t5 = V(lambda e, v3=pv3: e.tensor_reduce(p4a[:, :], v3, AX.X, ALU.add),
                           waits=[t4], inc="ve")
                    t6 = V(lambda e: e.reciprocal(p4a[:, :], p4a[:, :]), waits=[t5], inc="ve")
                    t7 = V(lambda e, v3=pv3: e.tensor_tensor(
                        v3, v3, p4a.ap().rearrange("p (h x) -> p h x", x=1)
                        .broadcast_to([128, 4, 3]), ALU.mult), waits=[t6], inc="ve")
                    t8 = V(lambda e, t=tt: e.tensor_scalar(
                        ptm[:, t * 12:(t + 1) * 12], ptm[:, t * 12:(t + 1) * 12],
                        1.0 - 3.0 * R_EPS, R_EPS, ALU.mult, ALU.add), waits=[t7], inc="ve")
                tok_pfm = p.tok("ve")
                if DBG and hb == 0:
                    bg["dwc"] = DMA("sync", "dw", dram["d_ptm"].ap(), ptm[:, :],
                                    waits=[tok_pfm, bg.get("dwc")])

                exj = [exch, exch2]
                t_exl = {}
                prev_exl = [None]
                for sg in range(2):
                    rowdma = []
                    for tl in range(4):
                        tt = sg * 4 + tl
                        w0 = ([tok_pfm] if tl == 0 else []) + bgw(2)
                        tmm = TR(pb[2][0:12, 0:128], ptm[:, tt * 12:(tt + 1) * 12],
                                 identf[0:128, 0:128], waits=w0, inc="te")
                        ev = V(lambda e: e.tensor_copy(pfm12[0:12, 0:128], pb[2][0:12, 0:128]),
                               waits=[tmm, rowdma[-1] if rowdma else None], inc="ve")
                        bg[2] = ev
                        rowdma.append(DMA(
                            "sync", "prw",
                            prow.ap().rearrange("o (r t) -> o r t", r=12)[:, :, tl * 128:(tl + 1) * 128],
                            pfm12.ap()[0:12, 0:128], waits=[ev]))
                    for ki in range(8):
                        w0 = ([t_gload, t_bxf] + bgw(4)) if ki == 0 else []
                        tmm = MM(pb[4][0:4, 0:512], widb[:, ki * 4:(ki + 1) * 4],
                                 bxf[:, ki * TBH + sg * 512:ki * TBH + (sg + 1) * 512],
                                 start=(ki == 0), stop=(ki == 7), waits=w0,
                                 inc="te" if ki == 7 else None)
                    t_id1 = SC(lambda e: e.activation(idt4[:, :], pb[4][0:4, 0:512], AF.Sigmoid,
                                                      bias=bidc[0:4, 0:1]),
                               waits=[tmm, t_bload, t_sa], inc="se")
                    ev = V(lambda e: e.tensor_scalar(
                        idb4[0:4, 0:512], idt4[:, :], sigac[0:4, 0:1],
                        EPS_ID, ALU.mult, ALU.add), waits=[t_id1], inc="ve")
                    bg[4] = ev
                    rowdma.append(DMA("sync", "prw",
                                      irow.ap().rearrange("o (r t) -> o r t", r=4),
                                      idb4.ap()[0:4, 0:512], waits=[ev]))
                    tok_rows = p.tok("prw")
                    for j in range(4):
                        eb = exj[j % 2]
                        guard = bg.get("ex" + str(j % 2))
                        src_ap = dram["ccm_oa"].ap().rearrange("(c p) t -> p c t", p=128)[
                            :, 8 * j:8 * j + 8,
                            hb * TBH + sg * 512:hb * TBH + (sg + 1) * 512]
                        p.op("sync",
                             lambda e, o=eb.ap().rearrange("p (c t) -> p c t", c=8), i=src_ap:
                                 e.dma_start(out=o, in_=dynoff(e, i, 4 * CCM)),
                             [guard, prev_exl[0], tok_cc_all,
                              tok_pfm if (sg == 0 and j == 0) else None], [("exl", 16)])
                        t_exl[(sg, j)] = p.tok("exl")
                        prev_exl[0] = t_exl[(sg, j)]
                        cbev = []
                        for r in range(3):
                            tmm = MM(pb[3][:, 0:512], onesb[:, :],
                                     prow[0:1, (3 * j + r) * 512:(3 * j + r + 1) * 512],
                                     start=True, stop=True,
                                     waits=[tok_rows] + bgw(3), inc="te")
                            ev = V(lambda e, r_=r: e.tensor_copy(
                                cbuf[:, r_ * 512:(r_ + 1) * 512], pb[3][:, 0:512]),
                                waits=[tmm], inc="ve")
                            bg[3] = ev
                            cbev.append(ev)
                        tmm = MM(pb[4][:, 0:512], onesb[:, :],
                                 irow[0:1, j * 512:(j + 1) * 512],
                                 start=True, stop=True, waits=[tok_rows] + bgw(4), inc="te")
                        ev = V(lambda e: e.tensor_copy(cbuf[:, 3 * 512:4 * 512], pb[4][:, 0:512]),
                               waits=[tmm], inc="ve")
                        bg[4] = ev
                        cbev.append(ev)
                        for mo in range(2):
                            ts0 = V(lambda e, b=eb, m=mo: e.tensor_tensor(
                                obuf[:, m * 512:(m + 1) * 512],
                                b[:, (0 + m) * 512:(1 + m) * 512], cbuf[:, 0:512], ALU.mult),
                                waits=cbev + [t_exl[(sg, j)]], inc="ve")
                            for t_i, cb_i in [(2, 1), (4, 2), (6, 3)]:
                                ts1 = V(lambda e, b=eb, m=mo, ti=t_i, ci=cb_i: e.tensor_tensor(
                                    scr[:, 0:512], b[:, (ti + m) * 512:(ti + 1 + m) * 512],
                                    cbuf[:, ci * 512:(ci + 1) * 512], ALU.mult),
                                    inc="ve")
                                ts0 = V(lambda e, m=mo: e.tensor_tensor(
                                    obuf[:, m * 512:(m + 1) * 512],
                                    obuf[:, m * 512:(m + 1) * 512], scr[:, 0:512], ALU.add),
                                    waits=[ts1, ts0], inc="ve")
                        tok_o = ts0
                        for mo in range(2):
                            tsq = SC(lambda e, m=mo: e.square(scr[:, 0:512],
                                                              obuf[:, m * 512:(m + 1) * 512]),
                                     waits=[tok_o, p.tok("te")], inc="se")
                            tmm = MM(pb[5][0:1, 0:512], colb256[:, :], scr[:, 0:512],
                                     start=(mo == 0), stop=(mo == 1),
                                     waits=[tsq] + (bgw(5) if mo == 0 else []), inc="te")
                        tr1 = V(lambda e: e.tensor_scalar(rmsrow[0:1, 0:512], pb[5][0:1, 0:512],
                                                          1e-5, None, ALU.add),
                                waits=[tmm], inc="ve")
                        bg[5] = tr1
                        tr2 = V(lambda e: e.reciprocal(rmsrow[0:1, 0:512], rmsrow[0:1, 0:512]),
                                waits=[tr1], inc="ve")
                        tr3 = SC(lambda e: e.activation(rmsrow[0:1, 0:512], rmsrow[0:1, 0:512], AF.Sqrt),
                                 waits=[tr2], inc="se")
                        tmm = MM(pb[5][:, 0:512], onesf[:, :], rmsrow[0:1, 0:512],
                                 start=True, stop=True, waits=[tr3] + bgw(5), inc="te")
                        tr4 = V(lambda e: e.tensor_copy(rbc[:, :], pb[5][:, 0:512]),
                                waits=[tmm], inc="ve")
                        bg[5] = tr4
                        for mo in range(2):
                            ev = V(lambda e, j_=j, m=mo: e.scalar_tensor_tensor(
                                ofm[:, (j_ * 2 + m) * 512:(j_ * 2 + m + 1) * 512],
                                obuf[:, m * 512:(m + 1) * 512], onwcb[:, m:m + 1],
                                rbc[:, :], ALU.mult, ALU.mult),
                                waits=[tr4, bg.get("ofmdump")], inc="ve")
                        bg["ex" + str(j % 2)] = ev
                    tok_ofm = p.tok("ve")
                    if DBG and hb == 0 and sg == 0:
                        bg["ofmdump"] = bg["dwc"] = DMA(
                            "sync", "dw", dram["d_ofm"].ap(), ofm[:, :],
                            waits=[tok_ofm, bg.get("dwc")])
                    for mo in range(8):
                        slot = mo % 2
                        woc[0] += 1
                        t_w = DMA("gpsimd", "wol" if slot == 0 else "wom",
                                  wob.ap()[:, slot * 1024:(slot + 1) * 1024]
                                  .rearrange("p (c n) -> p c n", c=8),
                                  dram["wo"].ap().rearrange("(c p) n -> p c n", p=128)
                                  [:, :, mo * 128:(mo + 1) * 128],
                                  waits=[woguard[slot]])
                        bank = 1 + mo % 2
                        for ki in range(8):
                            w0 = ([t_w, tok_ofm] + bgw(bank)) if ki == 0 else []
                            tmm = MM(pb[bank][:, 0:512],
                                     wob[:, slot * 1024 + ki * 128:slot * 1024 + (ki + 1) * 128],
                                     ofm[:, ki * 512:(ki + 1) * 512],
                                     start=(ki == 0), stop=(ki == 7), waits=w0,
                                     inc="te" if ki == 7 else None)
                        woguard[slot] = tmm
                        ev = V(lambda e, m=mo, s=sg, b=bank: e.tensor_tensor(
                            xz[:, m * TBH + s * 512:m * TBH + (s + 1) * 512],
                            pb[b][:, 0:512],
                            xz[:, m * TBH + s * 512:m * TBH + (s + 1) * 512], ALU.add),
                            waits=[tmm, t_xz], inc="ve")
                        bg[bank] = ev
                tok_x1 = p.tok("ve")

                for sg in range(2):
                    cts = []
                    for ki in range(8):
                        cts.append(V(lambda e, k=ki, s=sg: e.tensor_copy(
                            bxf[:, k * TBH + s * 512:k * TBH + (s + 1) * 512],
                            xz[:, k * TBH + s * 512:k * TBH + (s + 1) * 512]),
                            waits=[tok_x1] if ki == 0 else [], inc="ve"))
                    for ki in range(8):
                        w0m = ([cts[-1]] + bgw(4)) if ki == 0 else []
                        tmm_mu = MM(pb[4][0:1, 0:512], colb1024[:, :],
                                    bxf[:, ki * TBH + sg * 512:ki * TBH + (sg + 1) * 512],
                                    start=(ki == 0), stop=(ki == 7), waits=w0m,
                                    inc="te" if ki == 7 else None)
                    for ki in range(8):
                        tsq = SC(lambda e, k=ki, s=sg: e.square(
                            scr[:, 0:512], bxf[:, k * TBH + s * 512:k * TBH + (s + 1) * 512]),
                            waits=[cts[-1], p.tok("te")], inc="se")
                        tmm_m2 = MM(pb[5][0:1, 0:512], colb1024[:, :], scr[:, 0:512],
                                    start=(ki == 0), stop=(ki == 7),
                                    waits=[tsq] + (bgw(5) if ki == 0 else []),
                                    inc="te")
                    l0 = V(lambda e: e.tensor_copy(lnrow[0:1, 0:512], pb[4][0:1, 0:512]),
                           waits=[tmm_mu, tmm_m2], inc="ve")
                    bg[4] = l0
                    l1 = V(lambda e: e.tensor_copy(lnrow[0:1, 512:1024], pb[5][0:1, 0:512]),
                           waits=[tmm_m2], inc="ve")
                    bg[5] = l1
                    l2 = V(lambda e: e.tensor_tensor(lnrow[0:1, 1024:1536], lnrow[0:1, 0:512],
                                                     lnrow[0:1, 0:512], ALU.mult),
                           waits=[l0, l1], inc="ve")
                    l3 = V(lambda e: e.tensor_tensor(lnrow[0:1, 1024:1536],
                                                     lnrow[0:1, 512:1024],
                                                     lnrow[0:1, 1024:1536], ALU.subtract),
                           waits=[l2], inc="ve")
                    l4 = V(lambda e: e.tensor_scalar(lnrow[0:1, 1024:1536],
                                                     lnrow[0:1, 1024:1536], 1e-5, None, ALU.add),
                           waits=[l3], inc="ve")
                    l5 = V(lambda e: e.reciprocal(lnrow[0:1, 1024:1536], lnrow[0:1, 1024:1536]),
                           waits=[l4], inc="ve")
                    l6 = SC(lambda e: e.activation(lnrow[0:1, 1024:1536], lnrow[0:1, 1024:1536],
                                                   AF.Sqrt), waits=[l5], inc="se")
                    l7 = V(lambda e: e.tensor_scalar(lnrow[0:1, 1536:2048], lnrow[0:1, 0:512],
                                                     -1.0, None, ALU.mult), waits=[l6], inc="ve")
                    tmm = MM(pb[4][:, 0:512], onesf[:, :], lnrow[0:1, 1024:1536],
                             start=True, stop=True, waits=[l6] + bgw(4), inc="te")
                    b1 = V(lambda e: e.tensor_copy(rbc[:, :], pb[4][:, 0:512]),
                           waits=[tmm], inc="ve")
                    bg[4] = b1
                    tmm = MM(pb[5][:, 0:512], onesf[:, :], lnrow[0:1, 1536:2048],
                             start=True, stop=True, waits=[l7] + bgw(5), inc="te")
                    b2 = V(lambda e: e.tensor_copy(mrbc[:, :], pb[5][:, 0:512]),
                           waits=[tmm], inc="ve")
                    bg[5] = b2
                    for ki in range(8):
                        a1 = V(lambda e, k=ki, s=sg: e.tensor_tensor(
                            tmpf[:, :], xz[:, k * TBH + s * 512:k * TBH + (s + 1) * 512],
                            mrbc[:, :], ALU.add), waits=[b1, b2], inc="ve")
                        a2 = V(lambda e, k=ki, s=sg: e.tensor_tensor(
                            xz[:, k * TBH + s * 512:k * TBH + (s + 1) * 512], tmpf[:, :],
                            rbc[:, :], ALU.mult), waits=[a1], inc="ve")
                        V(lambda e, k=ki, s=sg: e.tensor_copy(
                            zb[:, k * TBH + s * 512:k * TBH + (s + 1) * 512],
                            xz[:, k * TBH + s * 512:k * TBH + (s + 1) * 512]),
                            waits=[a2, bg.get("zbdump")], inc="ve")
                tok_z = p.tok("ve")
                if DBG and hb == 0:
                    bg["zbdump"] = bg["dwc"] = DMA("sync", "dw", dram["d_zb"].ap(), zb[:, :],
                                                   waits=[tok_z, bg.get("dwc")])

                for tt in range(8):
                    for ki in range(8):
                        w0 = ([tok_z, t_bload] + bgw(2)) if ki == 0 else []
                        tmm = MM(pb[2][:, 0:8],
                                 xz[:, ki * TBH + tt * 128:ki * TBH + (tt + 1) * 128],
                                 gwtb[:, ki * 8:(ki + 1) * 8],
                                 start=(ki == 0), stop=(ki == 7), waits=w0,
                                 inc="te" if ki == 7 else None)
                    gl = V(lambda e: e.tensor_copy(g8t[:, :], pb[2][:, 0:8]),
                           waits=[tmm], inc="ve")
                    bg[2] = gl
                    s1 = V(lambda e: e.tensor_reduce(gc1[:, :], g8t[:, :], AX.X, ALU.max,
                                                     negate=True), waits=[gl], inc="ve")
                    s2 = V(lambda e: e.tensor_scalar(g8t[:, :], g8t[:, :], gc1[:, 0:1], None,
                                                     ALU.add), waits=[s1], inc="ve")
                    s3 = SC(lambda e: e.activation(g8t[:, :], g8t[:, :], AF.Exp),
                            waits=[s2], inc="se")
                    s4 = V(lambda e: e.tensor_reduce(gc1[:, :], g8t[:, :], AX.X, ALU.add),
                           waits=[s3], inc="ve")
                    s5 = V(lambda e: e.reciprocal(gc1[:, :], gc1[:, :]), waits=[s4], inc="ve")
                    s6 = V(lambda e: e.tensor_scalar(g8t[:, :], g8t[:, :], gc1[:, 0:1], None,
                                                     ALU.mult), waits=[s5], inc="ve")
                    s7 = V(lambda e: e.tensor_reduce(gc1[:, :], g8t[:, :], AX.X, ALU.max),
                           waits=[s6], inc="ve")
                    s8 = V(lambda e: e.tensor_scalar(tmpf[:, 0:8], g8t[:, :], gc1[:, 0:1],
                                                     -1e30, ALU.is_ge, ALU.mult),
                           waits=[s7], inc="ve")
                    s9 = V(lambda e: e.tensor_tensor(tmpf[:, 0:8], g8t[:, :], tmpf[:, 0:8],
                                                     ALU.add), waits=[s8], inc="ve")
                    sa_ = V(lambda e: e.tensor_reduce(gc2[:, :], tmpf[:, 0:8], AX.X, ALU.max),
                            waits=[s9], inc="ve")
                    sb_ = V(lambda e: e.tensor_scalar(tmpf[:, 0:8], g8t[:, :], gc2[:, 0:1],
                                                      None, ALU.is_ge), waits=[sa_], inc="ve")
                    sc_ = V(lambda e: e.tensor_tensor(gc1[:, :], gc1[:, :], gc2[:, :], ALU.add),
                            waits=[sa_], inc="ve")
                    sd_ = V(lambda e: e.tensor_scalar(gc1[:, :], gc1[:, :], 1e-20, None,
                                                      ALU.add), waits=[sc_], inc="ve")
                    se_ = V(lambda e: e.reciprocal(gc1[:, :], gc1[:, :]), waits=[sd_], inc="ve")
                    sf_ = V(lambda e: e.tensor_scalar(gc1[:, :], gc1[:, :], 2.5, None, ALU.mult),
                            waits=[se_], inc="ve")
                    sg_ = V(lambda e: e.tensor_tensor(tmpf[:, 0:8], g8t[:, :], tmpf[:, 0:8],
                                                      ALU.mult), waits=[sb_, sf_], inc="ve")
                    sh_ = V(lambda e, t=tt: e.tensor_scalar(combt[:, t * 8:(t + 1) * 8],
                                                            tmpf[:, 0:8], gc1[:, 0:1], None,
                                                            ALU.mult),
                            waits=[sg_, bg.get("dwc") if hb == 1 else None], inc="ve")
                    if DBG and hb == 0:
                        bg["dwc"] = DMA("sync", "dw",
                                        dram["d_gl8"].ap()[:, tt * 8:(tt + 1) * 8],
                                        combt[:, tt * 8:(tt + 1) * 8],
                                        waits=[sh_, bg.get("dwc")])
                    tmm = TR(pbt[0:8, 0:128], combt[:, tt * 8:(tt + 1) * 8], identb[:, :],
                             waits=[sh_] + bgw("t0"), inc="te")
                    ev = V(lambda e: e.tensor_copy(comb8[0:8, 0:128], pbt[0:8, 0:128]),
                           waits=[tmm, bg.get("c8")], inc="ve")
                    bg["t0"] = ev
                    bg["c8"] = DMA("sync", "crw",
                                   crow.ap().rearrange("o (r t) -> o r t", r=8)
                                   [:, :, tt * 128:(tt + 1) * 128],
                                   comb8.ap()[0:8, 0:128], waits=[ev])
                tok_comb = p.tok("ve")
                tok_crow = p.tok("crw")

                if BSTAGE < 2:
                    tok_zz = V(lambda e: e.memset(outb[:, :], 0.0), waits=[tok_comb], inc="ve")
                    for r in range(TBH // 128):
                        DMA("sync", "sw",
                            dram["out"].ap()[hb * TBH + r * 128:hb * TBH + (r + 1) * 128, :],
                            outb[:, :], waits=[tok_zz])
                    return

                def wsrc(e_, which):
                    if which == 0:
                        return dram["wg"].ap()[e_] if e_ < 8 else                             dram["swg"].ap()[:, (e_ - 8) * FF:(e_ - 7) * FF]
                    if which == 1:
                        return dram["wu"].ap()[e_] if e_ < 8 else                             dram["swu"].ap()[:, (e_ - 8) * FF:(e_ - 7) * FF]
                    return dram["wd"].ap()[e_] if e_ < 8 else                         dram["swd"].ap()[(e_ - 8) * FF:(e_ - 7) * FF, :]

                wguard = {"g": None, "u": None, "d": None}
                for e_ in range(10):
                    ec[0] += 1
                    fw = [tok_comb, p.tok("te"), p.tok("se")] if e_ == 0 else []
                    t_wg = DMA("gpsimd", "wgl",
                               wgb.ap().rearrange("p (c f) -> p c f", c=8),
                               wsrc(e_, 0).rearrange("(c p) f -> p c f", p=128),
                               waits=[wguard["g"]] + fw)
                    t_wu = DMA("gpsimd", "wul",
                               wub.ap().rearrange("p (c f) -> p c f", c=8),
                               wsrc(e_, 1).rearrange("(c p) f -> p c f", p=128),
                               waits=[wguard["u"]] + fw)
                    t_wd = DMA("gpsimd", "wdl",
                               wdb.ap().rearrange("p (c f) -> p c f", c=11),
                               wsrc(e_, 2).rearrange("(c p) f -> p c f", p=128),
                               waits=[wguard["d"]] + fw)
                    for sg in range(2):
                        if e_ < 8:
                            tmm = MM(pb[3][:, 0:512], onesb[:, :],
                                     crow[0:1, e_ * TBH + sg * 512:e_ * TBH + (sg + 1) * 512],
                                     start=True, stop=True,
                                     waits=[tok_comb, tok_crow] + bgw(3), inc="te")
                            cb = V(lambda e: e.tensor_copy(cbuf[:, 4 * 512:5 * 512],
                                                           pb[3][:, 0:512]),
                                   waits=[tmm], inc="ve")
                            bg[3] = cb
                        lastg = None
                        for fo in range(11):
                            gbank = [1, 0][fo % 2]
                            ubank = [2, 5][fo % 2]
                            for ki in range(8):
                                w0 = ([t_wg, tok_z] + bgw(gbank)) if ki == 0 else []
                                tmm = MM(pb[gbank][:, 0:512],
                                         wgb[:, ki * FF + fo * 128:ki * FF + (fo + 1) * 128],
                                         zb[:, ki * TBH + sg * 512:ki * TBH + (sg + 1) * 512],
                                         start=(ki == 0), stop=(ki == 7), waits=w0,
                                         inc="te" if ki == 7 else None)
                            if sg == 1 and fo == 10:
                                wguard["g"] = tmm
                            tsil = SC(lambda e, b=gbank, f=fo: e.activation(
                                gsil[:, (f % 2) * 256:(f % 2) * 256 + 512]
                                if False else gsil[:, 0:512],
                                pb[b][:, 0:512], AF.Silu),
                                waits=[tmm, bg.get("gsilr")], inc="se")
                            bg[gbank] = tsil
                            for ki in range(8):
                                w0 = ([t_wu] + bgw(ubank)) if ki == 0 else []
                                tmm = MM(pb[ubank][:, 0:512],
                                         wub[:, ki * FF + fo * 128:ki * FF + (fo + 1) * 128],
                                         zb[:, ki * TBH + sg * 512:ki * TBH + (sg + 1) * 512],
                                         start=(ki == 0), stop=(ki == 7), waits=w0,
                                         inc="te" if ki == 7 else None)
                            if sg == 1 and fo == 10:
                                wguard["u"] = tmm
                            tg1 = V(lambda e, b=ubank: e.tensor_tensor(
                                scr[:, 0:512], gsil[:, 0:512], pb[b][:, 0:512], ALU.mult),
                                waits=[tmm, tsil], inc="ve")
                            bg[ubank] = tg1
                            bg["gsilr"] = tg1
                            if e_ < 8:
                                lastg = V(lambda e, f=fo: e.tensor_tensor(
                                    gpr[:, f * 512:(f + 1) * 512], scr[:, 0:512],
                                    cbuf[:, 4 * 512:5 * 512], ALU.mult),
                                    waits=[tg1, cb], inc="ve")
                            else:
                                lastg = V(lambda e, f=fo: e.tensor_copy(
                                    gpr[:, f * 512:(f + 1) * 512], scr[:, 0:512]),
                                    waits=[tg1], inc="ve")
                        for dm in range(8):
                            bank = 3 + dm % 2
                            for ki in range(11):
                                w0 = ([t_wd, lastg] + bgw(bank)) if ki == 0 else []
                                tmm = MM(pb[bank][:, 0:512],
                                         wdb[:, ki * 1024 + dm * 128:ki * 1024 + (dm + 1) * 128],
                                         gpr[:, ki * 512:(ki + 1) * 512],
                                         start=(ki == 0), stop=(ki == 10), waits=w0,
                                         inc="te" if ki == 10 else None)
                            if sg == 1 and dm == 7:
                                wguard["d"] = tmm
                            ev = V(lambda e, d=dm, s=sg, b=bank: e.tensor_tensor(
                                xz[:, d * TBH + s * 512:d * TBH + (s + 1) * 512],
                                pb[b][:, 0:512],
                                xz[:, d * TBH + s * 512:d * TBH + (s + 1) * 512], ALU.add),
                                waits=[tmm], inc="ve")
                            bg[bank] = ev
                tok_ffn = p.tok("ve")

                for tt in range(8):
                    for mo8 in range(8):
                        w0 = ([tok_ffn] if (tt == 0 and mo8 == 0) else []) + bgw(5)
                        tmm = TR(pb[5][:, 0:128],
                                 xz[:, mo8 * TBH + tt * 128:mo8 * TBH + (tt + 1) * 128],
                                 identf[:, :], waits=w0, inc="te")
                        ev = V(lambda e, t=tt, m=mo8: e.tensor_copy(
                            x2t8[:, t * 1024 + m * 128:t * 1024 + (m + 1) * 128],
                            pb[5][:, 0:128]), waits=[tmm], inc="ve")
                        bg[5] = ev
                    r1 = V(lambda e, t=tt: e.tensor_reduce(
                        mucol[:, t:t + 1], x2t8[:, t * 1024:(t + 1) * 1024], AX.X, ALU.add),
                        inc="ve")
                    V(lambda e, t=tt: e.scalar_tensor_tensor(
                        x2sq[:, :], x2t8[:, t * 1024:(t + 1) * 1024], 1.0,
                        x2t8[:, t * 1024:(t + 1) * 1024], ALU.mult, ALU.mult,
                        accum_out=mucol[:, 8 + t:8 + t + 1]), waits=[r1], inc="ve")
                m0 = V(lambda e: e.tensor_scalar(mucol[:, :], mucol[:, :], 1.0 / 1024, None,
                                                 ALU.mult), inc="ve")
                m1 = V(lambda e: e.tensor_tensor(rcol[:, :], mucol[:, 0:8], mucol[:, 0:8],
                                                 ALU.mult), waits=[m0], inc="ve")
                m2_ = V(lambda e: e.tensor_tensor(rcol[:, :], mucol[:, 8:16], rcol[:, :],
                                                  ALU.subtract), waits=[m1], inc="ve")
                m3 = V(lambda e: e.tensor_scalar(rcol[:, :], rcol[:, :], 1e-5, None, ALU.add),
                       waits=[m2_], inc="ve")
                m4 = V(lambda e: e.reciprocal(rcol[:, :], rcol[:, :]), waits=[m3], inc="ve")
                m5 = SC(lambda e: e.activation(rcol[:, :], rcol[:, :], AF.Sqrt),
                        waits=[m4], inc="se")
                for tt in range(8):
                    ap_ = V(lambda e, t=tt: e.tensor_scalar(
                        outb[:, :], x2t8[:, t * 1024:(t + 1) * 1024],
                        mucol[:, t:t + 1], rcol[:, t:t + 1], ALU.subtract, ALU.mult),
                        waits=[m5, bg.get("outd")], inc="ve")
                    bg["outd"] = DMA(
                        "sync", "sw",
                        dram["out"].ap()[hb * TBH + tt * 128:hb * TBH + (tt + 1) * 128, :],
                        outb[:, :], waits=[ap_])

            for hb in range(NHB):
                b_half(hb)

        p.op("sync", lambda e: e.nop(),
             [p.tok("sw"), p.tok("gw"), p.tok("dw"), p.tok("cc")], [])

        block = ctx.enter_context(nc.Block())

        def make_runner(name):
            prog = p.bucket[name]

            def runner(eng):
                waited = {}
                for fn, waits, incs in prog:
                    need = {}
                    for s, v in waits:
                        if v > need.get(s, 0):
                            need[s] = v
                    for s, v in need.items():
                        if v > 0 and waited.get(s, 0) < v:
                            eng.wait_ge(semh[s], v)
                            waited[s] = v
                    inst = fn(eng)
                    for s, k in incs:
                        inst.then_inc(semh[s], k)
            return runner

        block.tensor(make_runner("tensor"))
        block.vector(make_runner("vector"))
        block.scalar(make_runner("scalar"))
        block.gpsimd(make_runner("gpsimd"))
        block.sync(make_runner("sync"))

    return nc


def prep_inputs(inp):
    f32 = np.float32
    x = np.asarray(inp["L0_attn_hidden_states"], f32)
    g = lambda n: np.asarray(inp[n], f32)
    Wr1 = g("L0_attn_Wr1")
    wr1p = np.zeros((1152, 2048), f32)
    wr1p[:1024] = Wr1[:1024]
    for h in range(H):
        for j in range(6):
            wr1p[1024 + h * 32 + j] = Wr1[1024 + j * 4 + h]
    consts = dict(
        identf=np.eye(128, dtype=f32),
        identb=np.eye(128, dtype=f32).astype(BF),
        eye32s=np.tile(np.eye(32, dtype=f32), (4, 1)).astype(BF),
        masksl=np.tile(-np.tril(np.ones((32, 32), f32), -1), (4, 1)).astype(BF),
        masktu=np.tile(np.triu(np.ones((32, 32), f32), 0), (4, 1)).astype(BF),
        onesf=np.ones((1, 128), f32),
        onesb=np.ones((1, 128), f32).astype(BF),
        colb1=np.ones((128, 1), f32).astype(BF),
        colb256=np.full((128, 1), 1.0 / 256, f32).astype(BF),
        colb1024=np.full((128, 1), 1.0 / 1024, f32).astype(BF),
    )
    shared = dict(
        wid=g("L0_attn_Wid"), bid=g("L0_attn_bid").reshape(H, 1),
        alpha=g("L0_attn_alpha_id").reshape(H, 1),
        wr1=wr1p, br1=np.ascontiguousarray(g("L0_attn_br1").reshape(16, 128).T),
        wr2=g("L0_attn_Wr2"), br2=g("L0_attn_br2").reshape(12, 1),
        ltg=g("L0_attn_log_tau_group").reshape(1, 2),
        lth=g("L0_attn_log_tau_head").reshape(1, 4),
        onwc=np.ascontiguousarray(g("L0_attn_o_norm_w").reshape(2, 128).T),
        wo=g("L0_attn_Wo"),
        gwt=np.ascontiguousarray(g("L0_ffn_gate_w").T),
        wg=g("L0_ffn_w_gate"), wu=g("L0_ffn_w_up"), wd=g("L0_ffn_w_down"),
        swg=g("L0_ffn_sw_gate"), swu=g("L0_ffn_sw_up"), swd=g("L0_ffn_sw_down"),
        **consts,
    )
    xT = [np.ascontiguousarray(x[b_].T) for b_ in range(B)]
    cores = []
    for c in range(8):
        b_, h_ = c // 4, c % 4
        cs, ce = h_ * DK, (h_ + 1) * DK
        d = dict(shared)
        d["x_fm"] = xT[b_]
        d["xb_fm"] = np.ascontiguousarray(xT[b_][:, h_ * TB:(h_ + 1) * TB])
        d["wq"] = np.ascontiguousarray(g("L0_attn_Wq")[:, cs:ce])
        d["wk"] = np.ascontiguousarray(g("L0_attn_Wk")[:, cs:ce])
        d["wv"] = np.ascontiguousarray(g("L0_attn_Wv")[:, cs:ce])
        d["wb"] = np.ascontiguousarray(g("L0_attn_Wb")[:, h_].reshape(8, 128).T)
        def diag_expand(w):
            K = w.shape[1]
            out = np.zeros((2 * K * 128, 128), f32)
            for mo in range(2):
                for j in range(K):
                    blk = (mo * K + j) * 128
                    np.fill_diagonal(out[blk:blk + 128], w[mo * 128:(mo + 1) * 128, j])
            return out
        d["convqd"] = diag_expand(g("L0_attn_conv_q")[cs:ce])
        d["convkd"] = diag_expand(g("L0_attn_conv_k")[cs:ce])
        d["convvd"] = diag_expand(g("L0_attn_conv_v")[cs:ce])
        d["firsd"] = diag_expand(g("L0_attn_fir_short")[h_])
        d["firld"] = diag_expand(g("L0_attn_fir_long")[h_])
        cores.append(d)
    return cores


_CACHE = {}


def kernel(**inputs):
    cores = prep_inputs(inputs)
    if "nc" not in _CACHE:
        nc = bass.Bass("TRN2", target_bir_lowering=False)
        _CACHE["nc"] = build(nc)
    nc = _CACHE["nc"]
    res = run_bass_kernel_spmd(nc, cores, core_ids=list(range(8)))
    out = np.zeros((B, L, D), np.float32)
    for c in range(8):
        b_, s_ = c // 4, c % 4
        out[b_, s_ * TB:(s_ + 1) * TB] = np.asarray(res.results[c]["out"], np.float32)
    kernel.results = res.results
    return out




# version 11
# speedup vs baseline: 1.1613x; 1.1613x over previous
import numpy as np
import ml_dtypes
import concourse.bass as bass
import concourse.mybir as mybir
from concourse.bass_utils import run_bass_kernel_spmd
from contextlib import ExitStack

F32 = mybir.dt.float32
BF16 = mybir.dt.bfloat16
ALU = mybir.AluOpType
AF = mybir.ActivationFunctionType
AX = mybir.AxisListType
BF = ml_dtypes.bfloat16

B, L, D, H = 2, 8192, 1024, 4
DK = DV = 256
CH = 32
NG = 4
GA = L // NG
NTT = GA // 128
NCHG = GA // CH
NGI = NCHG // 4
TB = L // 4
NHB = 2
TBH = TB // NHB
EPS_ID, R_EPS = 0.06, 0.025
NE, FF, FSH = 8, 1408, 2816
CCM = 1024
CCS = 6 * NTT

DBG = False
STAGE = 6
NGB = NG
B_STAGE = 2


class Plan:
    ENG = ("tensor", "vector", "scalar", "gpsimd", "sync")

    def __init__(self):
        self.bucket = {e: [] for e in self.ENG}
        self.sems = {}

    def op(self, eng, fn, waits=(), incs=()):
        for s, k in incs:
            self.sems[s] = self.sems.get(s, 0) + k
        self.bucket[eng].append((fn, tuple(w for w in waits if w is not None), tuple(incs)))

    def tok(self, s):
        return (s, self.sems.get(s, 0))


def sbpp(shape, dtype):
    n = 1
    for d in shape[1:]:
        n *= d
    return n * (2 if dtype == BF16 else 4)


class SB:
    def __init__(self, nc, base):
        self.nc, self.off = nc, base

    def al(self, name, shape, dtype):
        off = (self.off + 31) & ~31
        t = self.nc.alloc_sbuf_tensor_at(name, shape, dtype, offset=off)
        self.off = off + sbpp(shape, dtype)
        return t


def build(nc):
    p = Plan()
    dram = {}

    def din(name, shape, dt=F32):
        dram[name] = nc.dram_tensor(name, shape, dt, kind="ExternalInput")
        return dram[name]

    def dout(name, shape, dt=F32):
        dram[name] = nc.dram_tensor(name, shape, dt, kind="ExternalOutput")
        return dram[name]

    din("x_fm", [D, L])
    din("xb_fm", [D, TB])
    din("wq", [D, DK]); din("wk", [D, DK]); din("wv", [D, DV]); din("wb", [128, 8])
    din("convqd", [8 * 128, 128]); din("convkd", [8 * 128, 128]); din("convvd", [8 * 128, 128])
    din("firsd", [6 * 128, 128]); din("firld", [62 * 128, 128])
    din("wid", [D, H]); din("bid", [H, 1]); din("alpha", [H, 1])
    din("wr1", [1152, 2048]); din("br1", [128, 16]); din("wr2", [2048, 12]); din("br2", [12, 1])
    din("ltg", [1, 2]); din("lth", [1, 4]); din("onwc", [128, 2])
    din("wo", [D, D]); din("gwt", [D, NE])
    din("wg", [NE, D, FF]); din("wu", [NE, D, FF]); din("wd", [NE, FF, D])
    din("swg", [D, FSH]); din("swu", [D, FSH]); din("swd", [FSH, D])
    din("identf", [128, 128]); din("identb", [128, 128], BF16)
    din("eye32s", [128, 32], BF16)
    din("masksl", [128, 32], BF16)
    din("masktu", [128, 32], BF16)
    din("onesf", [1, 128]); din("onesb", [1, 128], BF16)
    din("colb1", [128, 1], BF16); din("colb256", [128, 1], BF16); din("colb1024", [128, 1], BF16)

    dout("out", [TB, D])
    dram["ccm_ia"] = nc.dram_tensor("ccm_ia", [NG * CCM, GA], BF16)
    dram["ccm_oa"] = nc.dram_tensor("ccm_oa", [NG * 4 * CCM, GA], BF16)
    dram["ccs_ia"] = nc.dram_tensor("ccs_ia", [128, NG * CCS], BF16)
    dram["ccs_oa"] = nc.dram_tensor("ccs_oa", [4 * 128, NG * CCS], BF16)

    if DBG:
        dout("d_qfm", [128, 2 * GA], BF16); dout("d_kfm", [128, 2 * GA], BF16)
        dout("d_vfm", [128, 2 * GA], BF16); dout("d_knfm", [128, 2 * GA], BF16)
        dout("d_fs", [128, 2 * GA], BF16); dout("d_fl", [128, 2 * GA], BF16)
        dout("d_otm", [128, NTT * 256], BF16); dout("d_dfm", [128, 2 * GA], BF16)
        dout("d_statc", [128, CCS], BF16); dout("d_scols", [128, NTT * 8])
        dout("d_zb", [128, 8 * TBH], BF16)
        dout("d_ptm", [128, 8 * 12]); dout("d_gl8", [128, 8 * 8], BF16)
        dout("d_ofm", [128, 8 * 512], BF16)

    with ExitStack() as ctx:
        pb = [ctx.enter_context(nc.psum_tensor(f"pb{i}", [128, 512], F32)) for i in range(6)]
        pbt = ctx.enter_context(nc.psum_tensor("pbt", [128, 512], BF16))
        pbt2 = ctx.enter_context(nc.psum_tensor("pbt2", [128, 512], BF16))
        semnames = ["gl", "sl", "sw", "gw", "dw", "cc", "te", "ve", "se",
                    "exl", "prw", "crw", "wr1l", "wr1m", "wol", "wom", "wgl", "wul", "wdl"]
        semh = {s: ctx.enter_context(nc.semaphore(s)) for s in semnames}

        base = nc.sbuf_base + 64
        P0 = SB(nc, base)
        identf = P0.al("identf", [128, 128], F32)
        identb = P0.al("identb", [128, 128], BF16)
        eye32s = P0.al("eye32s", [128, 32], BF16)
        masksl = P0.al("masksl", [128, 32], BF16)
        masktu = P0.al("masktu", [128, 32], BF16)
        onesf = P0.al("onesf", [1, 128], F32)
        onesb = P0.al("onesb", [1, 128], BF16)
        colb1 = P0.al("colb1", [128, 1], BF16)
        colb256 = P0.al("colb256", [128, 1], BF16)
        colb1024 = P0.al("colb1024", [128, 1], BF16)
        wqb = P0.al("wqb", [128, 8 * 256], BF16)
        wkb = P0.al("wkb", [128, 8 * 256], BF16)
        wvb = P0.al("wvb", [128, 8 * 256], BF16)
        wbb = P0.al("wbb", [128, 8], BF16)
        PERM_END = P0.off

        A = SB(nc, PERM_END)
        GAp, VAp = GA + 3, GA + 30
        convqd = A.al("convqd", [128, 8 * 128], BF16)
        convkd = A.al("convkd", [128, 8 * 128], BF16)
        convvd = A.al("convvd", [128, 8 * 128], BF16)
        firsd = A.al("firsd", [128, 6 * 128], BF16)
        firld = A.al("firld", [128, 62 * 128], BF16)
        axf_off = (A.off + 31) & ~31
        axf = A.al("axf", [128, 8 * GA], BF16)
        qraw_off = (A.off + 31) & ~31
        qraw = A.al("qraw", [128, 2 * GAp], BF16)
        kraw_off = (A.off + 31) & ~31
        kraw = A.al("kraw", [128, 2 * GAp], BF16)
        vraw_off = (A.off + 31) & ~31
        vraw = A.al("vraw", [128, 2 * GAp], BF16)
        qfm = A.al("qfm", [128, 2 * GA], BF16)
        kfm = A.al("kfm", [128, 2 * GA], BF16)
        vfm = A.al("vfm", [128, 2 * VAp], BF16)
        knfm = A.al("knfm", [128, 2 * GA], BF16)
        fsb = A.al("fsb", [128, 2 * GA], BF16)
        flb = A.al("flb", [128, 2 * GA], BF16)
        sqbuf = A.al("sqbuf", [128, GA], BF16)
        kntm = nc.alloc_sbuf_tensor_at("kntm", [128, NTT * 256], BF16, offset=qraw_off)
        kbntm = nc.alloc_sbuf_tensor_at("kbntm", [128, NTT * 256], BF16, offset=kraw_off)
        vbtm = nc.alloc_sbuf_tensor_at("vbtm", [128, NTT * 256], BF16, offset=vraw_off)
        ubuf = nc.alloc_sbuf_tensor_at("ubuf", [128, NTT * 256], BF16, offset=axf_off)
        wnfm = nc.alloc_sbuf_tensor_at("wnfm", [128, NCHG * 64], BF16, offset=axf_off + 8192)
        otm = nc.alloc_sbuf_tensor_at("otm", [128, NTT * 256], BF16, offset=axf_off + 16384)
        dfm = nc.alloc_sbuf_tensor_at("dfm", [128, 2 * GA], BF16, offset=axf_off + 24576)
        atm = A.al("atm", [128, NGI * 32], BF16)
        Tpk = A.al("Tpk", [128, NGI * 32], BF16)
        TTpk = A.al("TTpk", [128, NGI * 32], BF16)
        Apk = A.al("Apk", [128, NGI * 32], BF16)
        ATpk = A.al("ATpk", [128, NGI * 32], BF16)
        u2sb = A.al("u2sb", [128, 2 * 256], BF16)
        Sbf = A.al("Sbf", [128, 512], BF16)
        S32 = A.al("S32", [128, 512], F32)
        halobuf = A.al("halobuf", [128, 3 * 6], BF16)
        srows = A.al("srows", [8, GA], F32)
        tmpr = A.al("tmpr", [1, GA], F32)
        rkrow = A.al("rkrow", [1, GA], F32)
        rkb = A.al("rkb", [128, 512], F32)
        scols = A.al("scols", [128, NTT * 8], F32)
        rqc = A.al("rqc", [128, NTT], F32)
        statc = A.al("statc", [128, CCS], BF16)
        dmcol = A.al("dmcol", [128, 2 * NTT], F32)
        stdw = A.al("stdw", [128, NTT], F32)
        assert A.off < nc.sbuf_top, f"A region overflow {A.off} > {nc.sbuf_top}"

        Br = SB(nc, PERM_END)
        xz = Br.al("xz", [128, 8 * TBH], F32)
        zb = Br.al("zb", [128, 8 * TBH], BF16)
        wr2b = Br.al("wr2b", [128, 16 * 12], BF16)
        widb = Br.al("widb", [128, 8 * 4], BF16)
        gwtb = Br.al("gwtb", [128, 8 * 8], F32)
        br1c = Br.al("br1c", [128, 16], F32)
        br2c = Br.al("br2c", [12, 1], F32)
        bidc = Br.al("bidc", [4, 1], F32)
        alphac = Br.al("alphac", [4, 1], F32)
        sigac = Br.al("sigac", [4, 1], F32)
        onwcb = Br.al("onwcb", [128, 2], F32)
        ltgr = Br.al("ltgr", [1, 2], F32)
        lthr = Br.al("lthr", [1, 4], F32)
        tau12 = Br.al("tau12", [1, 12], F32)
        tau12r = Br.al("tau12r", [1, 12], F32)
        taub = Br.al("taub", [128, 12], F32)
        logfm = Br.al("logfm", [12, TBH], F32)
        ltm = Br.al("ltm", [128, 8 * 12], F32)
        ptm = Br.al("ptm", [128, 8 * 12], F32)
        p4a = Br.al("p4a", [128, 4], F32)
        pfm12 = Br.al("pfm12", [12, 128], BF16)
        prow_off = (Br.off + 31) & ~31
        prow = Br.al("prow", [1, 12 * 512], BF16)
        irow = Br.al("irow", [1, 4 * 512], BF16)
        idt4 = Br.al("idt4", [4, 512], F32)
        idb4 = Br.al("idb4", [4, 512], BF16)
        combt = Br.al("combt", [128, 8 * 8], BF16)
        comb8 = Br.al("comb8", [8, 128], BF16)
        crow = nc.alloc_sbuf_tensor_at("crow", [1, 8 * TBH], BF16, offset=prow_off)
        g8t = Br.al("g8t", [128, 8], F32)
        gc1 = Br.al("gc1", [128, 1], F32)
        gc2 = Br.al("gc2", [128, 1], F32)
        lnrow = Br.al("lnrow", [1, 2048], F32)
        rmsrow = lnrow
        rbc = Br.al("rbc", [128, 512], F32)
        mrbc = Br.al("mrbc", [128, 512], F32)
        tmpf = Br.al("tmpf", [128, 512], F32)
        obuf = Br.al("obuf", [128, 2 * 512], BF16)
        scr = Br.al("scr", [128, 512], BF16)
        gsil = Br.al("gsil", [128, 512], BF16)
        cbuf = Br.al("cbuf", [128, 5 * 512], BF16)
        ofm = Br.al("ofm", [128, 8 * 512], BF16)
        sstat = Br.al("sstat", [128, 4 * 48], BF16)
        rinstat = Br.al("rinstat", [128, TBH], BF16)
        mucol = Br.al("mucol", [128, 16], F32)
        rcol = Br.al("rcol", [128, 8], F32)
        x2sq = Br.al("x2sq", [128, 1024], F32)
        outb = Br.al("outb", [128, 1024], F32)
        SREG = (Br.off + 31) & ~31
        bxf = nc.alloc_sbuf_tensor_at("bxf", [128, 8 * TBH], BF16, offset=SREG)
        hmid = nc.alloc_sbuf_tensor_at("hmid", [128, 16 * 512], BF16, offset=SREG + 16384)
        exch = nc.alloc_sbuf_tensor_at("exch", [128, 8 * 512], BF16, offset=SREG + 16384)
        exch2 = nc.alloc_sbuf_tensor_at("exch2", [128, 8 * 512], BF16, offset=SREG + 24576)
        wr1b = nc.alloc_sbuf_tensor_at("wr1b", [128, 2 * 9 * 128], BF16, offset=SREG + 32768)
        wob = nc.alloc_sbuf_tensor_at("wob", [128, 2 * 8 * 128], BF16, offset=SREG + 37504)
        wgb = nc.alloc_sbuf_tensor_at("wgb", [128, 8 * FF], BF16, offset=SREG)
        wub = nc.alloc_sbuf_tensor_at("wub", [128, 8 * FF], BF16, offset=SREG + 22528)
        wdb = nc.alloc_sbuf_tensor_at("wdb", [128, 11 * 1024], BF16, offset=SREG + 45056)
        gpr = nc.alloc_sbuf_tensor_at("gpr", [128, 11 * 512], BF16, offset=SREG + 67584)
        x2t8 = nc.alloc_sbuf_tensor_at("x2t8", [128, 8 * 1024], F32, offset=SREG)
        assert SREG + 78848 < nc.sbuf_top, f"B overflow {SREG + 78848}"

        def DMA(eng, sem, out, in_, waits=()):
            p.op(eng, lambda e, o=out, i=in_: e.dma_start(out=o, in_=i), waits, [(sem, 16)])
            return p.tok(sem)

        def MM(out, lhsT, rhs, start, stop, tp=None, waits=(), inc=None):
            p.op("tensor",
                 lambda e, o=out, l=lhsT, r=rhs, s1=start, s2=stop, t=tp:
                     e.matmul(o, l, r, start=s1, stop=s2, tile_position=t),
                 waits, [(inc, 1)] if inc else [])
            return p.tok("te") if inc else None

        def TR(out, in_, ident, waits=(), inc=None):
            p.op("tensor",
                 lambda e, o=out, i=in_, d=ident: e.transpose(o, i, d),
                 waits, [(inc, 1)] if inc else [])
            return p.tok("te") if inc else None

        def V(fn, waits=(), inc=None):
            p.op("vector", fn, list(waits) + [p.tok("ve")], [("ve", 1)])
            return p.tok("ve")

        def SC(fn, waits=(), inc=None):
            p.op("scalar", fn, list(waits) + [p.tok("se")], [("se", 1)])
            return p.tok("se")

        bg = {}

        def bgw(key):
            return [bg.get(key)]

        import dataclasses as _dc
        _scoff = {}

        def _scoff_get(e):
            if id(e) not in _scoff:
                _scoff[id(e)] = e.partition_id() % 4
            return _scoff[id(e)]

        def dynoff(e, ap, elems_per_token):
            return _dc.replace(ap, offset=ap.offset + _scoff_get(e) * (TB * elems_per_token))

        x_fm = dram["x_fm"]

        for nm, buf in [("identf", identf), ("onesf", onesf)]:
            DMA("sync", "sl", buf[:, :], dram[nm].ap())
        tok_const_sl = p.tok("sl")
        for nm, buf in [("identb", identb), ("eye32s", eye32s), ("masksl", masksl),
                        ("masktu", masktu), ("onesb", onesb), ("colb1", colb1),
                        ("colb256", colb256), ("colb1024", colb1024)]:
            DMA("gpsimd", "gl", buf[:, :], dram[nm].ap())
        for nm, buf in [("wq", wqb), ("wk", wkb), ("wv", wvb)]:
            w = dram[nm].ap().rearrange("(c p) n -> p c n", p=128)
            DMA("gpsimd", "gl", buf.ap().rearrange("p (c n) -> p c n", c=8), w)
        DMA("gpsimd", "gl", wbb[:, :], dram["wb"].ap())
        for nm, buf, cc in [("convqd", convqd, 8), ("convkd", convkd, 8),
                            ("convvd", convvd, 8), ("firsd", firsd, 6), ("firld", firld, 62)]:
            DMA("gpsimd", "gl", buf.ap().rearrange("p (c k) -> p c k", c=cc),
                dram[nm].ap().rearrange("(c p) k -> p c k", p=128))
        tok_const_gl = p.tok("gl")

        V(lambda e: e.memset(qraw[:, 0:3], 0.0))
        V(lambda e: e.memset(qraw[:, GAp:GAp + 3], 0.0))
        V(lambda e: e.memset(kraw[:, 0:3], 0.0))
        V(lambda e: e.memset(kraw[:, GAp:GAp + 3], 0.0))
        V(lambda e: e.memset(vraw[:, 0:3], 0.0))
        V(lambda e: e.memset(vraw[:, GAp:GAp + 3], 0.0))
        V(lambda e: e.memset(vfm[:, 0:30], 0.0))
        V(lambda e: e.memset(vfm[:, VAp:VAp + 30], 0.0))
        V(lambda e: e.memset(srows[0:8, :], 0.0))
        V(lambda e: e.memset(S32[:, :], 0.0))
        V(lambda e: e.memset(Sbf[:, :], 0.0), inc="ve")

        def a_group(g, bar):
            t0 = g * GA
            p.op("tensor", lambda e: e.nop(), bar, [("te", 1)])
            p.op("vector", lambda e: e.engine_nop(), bar, [("ve", 1)])
            p.op("scalar", lambda e: e.nop(), bar, [("se", 1)])
            p.op("gpsimd", lambda e: e.engine_nop(), bar, [])
            p.op("sync", lambda e: e.nop(), bar, [])
            barve, barse = p.tok("ve"), p.tok("se")

            def fin():
                if DBG and g == 0:
                    wv, ws = p.tok("ve"), p.tok("se")
                    dumps = [("d_qfm", qfm, 2 * GA, 1), ("d_kfm", kfm, 2 * GA, 1),
                             ("d_fs", fsb, 2 * GA, 2), ("d_fl", flb, 2 * GA, 2),
                             ("d_knfm", knfm, 2 * GA, 3), ("d_scols", scols, NTT * 8, 3),
                             ("d_otm", otm, NTT * 256, 5), ("d_dfm", dfm, 2 * GA, 6),
                             ("d_statc", statc, CCS, 6)]
                    for nm, src, w, st in dumps:
                        if STAGE >= st:
                            DMA("sync", "dw", dram[nm].ap(), src[:, 0:w], waits=[wv, ws])
                    if STAGE >= 1:
                        DMA("sync", "dw", dram["d_vfm"].ap(),
                            vfm.ap().rearrange("p (c t) -> p c t", c=2)[:, :, 30:30 + GA],
                            waits=[wv, ws])
                return [p.tok("te"), p.tok("ve"), p.tok("se"), p.tok("sl"), p.tok("gl"),
                        p.tok("sw"), p.tok("gw"), p.tok("dw")]

            xin = x_fm.ap().rearrange("(c p) t -> p c t", p=128)[:, :, t0:t0 + GA]
            xout = axf.ap().rearrange("p (c t) -> p c t", c=8)
            tok_x = DMA("gpsimd", "gl", xout, xin, waits=[tok_const_gl])
            if g > 0 and STAGE >= 6:
                gp = g - 1
                p.op("gpsimd", lambda e, g_=gp: e.collective_compute(
                    "AllGather", ALU.bypass, replica_groups=[[0, 1, 2, 3], [4, 5, 6, 7]],
                    ins=[dram["ccm_ia"].ap()[g_ * CCM:(g_ + 1) * CCM, :].opt()],
                    outs=[dram["ccm_oa"].ap()[g_ * 4 * CCM:(g_ + 1) * 4 * CCM, :].opt()]),
                    [exch_sw_last[gp]], [("cc", 1)])

            if g > 0:
                for bi, buf in enumerate([qraw, kraw, vraw]):
                    V(lambda e, b=buf, k=bi: e.tensor_copy(
                        b.ap().rearrange("p (c t) -> p c t", c=2)[:, :, 0:3],
                        halobuf.ap().rearrange("p (a c t) -> p a c t", a=3, c=2)[:, k]))
                V(lambda e: e.tensor_copy(
                    vfm.ap().rearrange("p (c t) -> p c t", c=2)[:, :, 0:30],
                    vfm.ap().rearrange("p (c t) -> p c t", c=2)[:, :, VAp - 30:VAp]))
            tok_halo = V(lambda e: e.engine_nop(), inc="ve")

            prtoks = {}
            for pi, (wb_, raw) in enumerate([(wqb, qraw), (wkb, kraw), (wvb, vraw)]):
                for mo in range(2):
                    for n in range(4):
                        bank = 1 + (pi * 8 + mo * 4 + n) % 2
                        for ki in range(8):
                            w0 = ([tok_x, tok_const_gl, barve] + bgw(bank)) if ki == 0 else []
                            tmm = MM(pb[bank][:, 0:512],
                                     wb_[:, ki * 256 + mo * 128:ki * 256 + (mo + 1) * 128],
                                     axf[:, ki * GA + n * 512:ki * GA + (n + 1) * 512],
                                     start=(ki == 0), stop=(ki == 7), waits=w0,
                                     inc="te" if ki == 7 else None)
                        dst = raw[:, mo * GAp + 3 + n * 512:mo * GAp + 3 + (n + 1) * 512]
                        ev = V(lambda e, d=dst, b=bank: e.tensor_copy(d, pb[b][:, 0:512]),
                               waits=[tmm], inc="ve")
                        prtoks[(pi, mo, n)] = ev
                        bg[bank] = ev
            for n in range(4):
                for ki in range(8):
                    w0 = ([tok_x] + bgw(3)) if ki == 0 else []
                    tmm = MM(pb[3][0:1, 0:512], wbb[:, ki:ki + 1],
                             axf[:, ki * GA + n * 512:ki * GA + (n + 1) * 512],
                             start=(ki == 0), stop=(ki == 7), waits=w0,
                             inc="te" if ki == 7 else None)
                ev = V(lambda e, n_=n: e.tensor_copy(tmpr[0:1, n_ * 512:(n_ + 1) * 512],
                                                     pb[3][0:1, 0:512]),
                       waits=[tmm], inc="ve")
                bg[3] = ev
            tok_beta = SC(lambda e: e.activation(tmpr[0:1, 0:GA], tmpr[0:1, 0:GA], AF.Sigmoid),
                          waits=[bg[3], barse], inc="se")
            tok_srow_b = DMA("sync", "sl", srows[1:2, 0:GA], tmpr[0:1, 0:GA], waits=[tok_beta])

            if g < NG - 1:
                for bi, buf in enumerate([qraw, kraw, vraw]):
                    V(lambda e, b=buf, k=bi: e.tensor_copy(
                        halobuf.ap().rearrange("p (a c t) -> p a c t", a=3, c=2)[:, k],
                        b.ap().rearrange("p (c t) -> p c t", c=2)[:, :, GAp - 3:GAp]))

            silu_toks = {}
            for pi, (raw, cwd, dst) in enumerate([(qraw, convqd, qfm), (kraw, convkd, kfm),
                                                  (vraw, convvd, vfm)]):
                for mo in range(2):
                    for n in range(4):
                        bank = 1 + (mo * 4 + n) % 2
                        for j in range(4):
                            w0 = ([prtoks[(pi, mo, nn)] for nn in range(4)] + [tok_halo]
                                  + bgw(bank)) if j == 0 else []
                            tmm = MM(pb[bank][:, 0:512],
                                     cwd[:, (mo * 4 + j) * 128:(mo * 4 + j + 1) * 128],
                                     raw[:, mo * GAp + j + n * 512:mo * GAp + j + n * 512 + 512],
                                     start=(j == 0), stop=(j == 3), waits=w0,
                                     inc="te" if j == 3 else None)
                        if pi == 2:
                            od = dst[:, mo * VAp + 30 + n * 512:mo * VAp + 30 + (n + 1) * 512]
                        else:
                            od = dst[:, mo * GA + n * 512:mo * GA + (n + 1) * 512]
                        ev = SC(lambda e, o=od, b=bank: e.activation(o, pb[b][:, 0:512],
                                                                    AF.Silu),
                                waits=[tmm], inc="se")
                        bg[bank] = ev
                    silu_toks[pi * 2 + mo] = p.tok("se")

            if STAGE <= 1:
                return fin()

            fir_ev = {}
            for si, (dstb, wd, ntap) in enumerate([(fsb, firsd, 3), (flb, firld, 31)]):
                for mo in range(2):
                    for n in range(4):
                        bank = 1 + (mo * 4 + n) % 2
                        for j in range(ntap):
                            off = mo * VAp + (30 - ntap + 1) + j + n * 512
                            w0 = ([silu_toks[4 + mo]] + bgw(bank)) if j == 0 else []
                            tmm = MM(pb[bank][:, 0:512],
                                     wd[:, (mo * ntap + j) * 128:(mo * ntap + j + 1) * 128],
                                     vfm[:, off:off + 512],
                                     start=(j == 0), stop=(j == ntap - 1), waits=w0,
                                     inc="te" if j == ntap - 1 else None)
                        ev = V(lambda e, d=dstb, m=mo, n_=n, b=bank: e.tensor_copy(
                            d[:, m * GA + n_ * 512:m * GA + (n_ + 1) * 512], pb[b][:, 0:512]),
                            waits=[tmm], inc="ve")
                        bg[bank] = ev
                fir_ev[si] = p.tok("ve")
            fir_toks = [(fir_ev[0], fir_ev[1]), (fir_ev[0], fir_ev[1])]

            if STAGE <= 2:
                return fin()

            sq_toks = {}
            for ti, src in enumerate([qfm, kfm]):
                for mo in range(2):
                    sq_toks[(ti, mo)] = SC(
                        lambda e, s=src, m=mo: e.square(sqbuf[:, :], s[:, m * GA:(m + 1) * GA]),
                        waits=[silu_toks[ti * 2 + mo], p.tok("te")], inc="se")
                    dstrow = srows if ti == 0 else rkrow
                    for n in range(4):
                        bank = [4, 5, 1, 2][n]
                        w0 = [sq_toks[(ti, mo)]] + (bgw(bank) if mo == 0 else [])
                        tmm = MM(pb[bank][0:1, 0:512], colb1[:, :],
                                 sqbuf[:, n * 512:(n + 1) * 512],
                                 start=(mo == 0), stop=(mo == 1), waits=w0, inc="te")
                        if mo == 1:
                            ev = V(lambda e, n_=n, b=bank, dr=dstrow:
                                   e.tensor_copy(dr[0:1, n_ * 512:(n_ + 1) * 512],
                                                 pb[b][0:1, 0:512]),
                                   waits=[tmm], inc="ve")
                            bg[bank] = ev
            tok_ssq = p.tok("ve")

            if STAGE <= 2.2:
                return fin()

            t_eps = V(lambda e: e.tensor_scalar(rkrow[0:1, :], rkrow[0:1, :], 1e-6, None,
                                                ALU.add), waits=[tok_ssq], inc="ve")
            tok_rk1 = V(lambda e: e.reciprocal(rkrow[0:1, :], rkrow[0:1, :]),
                        waits=[t_eps], inc="ve")
            tok_rk = SC(lambda e: e.activation(rkrow[0:1, :], rkrow[0:1, :], AF.Sqrt),
                        waits=[tok_rk1], inc="se")

            if STAGE <= 2.3:
                return fin()

            prev_tmpr = tok_srow_b
            stat_dma = []
            for si, src in enumerate([fsb, flb]):
                ftk = [fir_toks[0][si], fir_toks[1][si]]
                for qty in range(2):
                    row = 2 + 2 * si + qty
                    evs = []
                    tsq = None
                    for mo in range(2):
                        if qty == 1:
                            tsq = SC(lambda e, s=src, m=mo: e.square(sqbuf[:, :],
                                                                     s[:, m * GA:(m + 1) * GA]),
                                     waits=ftk + [p.tok("te")], inc="se")
                        for n in range(4):
                            bank = [4, 5, 1, 2][n]
                            if qty == 0:
                                rhs = src[:, mo * GA + n * 512:mo * GA + (n + 1) * 512]
                                w0 = (ftk + bgw(bank)) if mo == 0 else []
                            else:
                                rhs = sqbuf[:, n * 512:(n + 1) * 512]
                                w0 = [tsq] + (bgw(bank) if mo == 0 else [])
                            tmm = MM(pb[bank][0:1, 0:512], colb256[:, :], rhs,
                                     start=(mo == 0), stop=(mo == 1), waits=w0, inc="te")
                            if mo == 1:
                                ev = V(lambda e, n_=n, b=bank: e.tensor_copy(
                                    tmpr[0:1, n_ * 512:(n_ + 1) * 512], pb[b][0:1, 0:512]),
                                    waits=[tmm, prev_tmpr], inc="ve")
                                bg[bank] = ev
                                evs.append(ev)
                    prev_tmpr = DMA("sync", "sl", srows[row:row + 1, 0:GA], tmpr[0:1, 0:GA],
                                    waits=evs)
                    stat_dma.append(prev_tmpr)

            if STAGE <= 2.4:
                return fin()

            tok_knfm = {}
            for n in range(4):
                tmm = MM(pb[3][:, 0:512], onesf[:, :], rkrow[0:1, n * 512:(n + 1) * 512],
                         start=True, stop=True, waits=[tok_rk] + bgw(3), inc="te")
                tok_rkb = V(lambda e: e.tensor_copy(rkb[:, :], pb[3][:, 0:512]),
                            waits=[tmm], inc="ve")
                bg[3] = tok_rkb
                for mo in range(2):
                    tok_knfm[(mo, n)] = V(
                        lambda e, m=mo, n_=n: e.tensor_tensor(
                            knfm[:, m * GA + n_ * 512:m * GA + (n_ + 1) * 512],
                            kfm[:, m * GA + n_ * 512:m * GA + (n_ + 1) * 512],
                            rkb[:, :], ALU.mult),
                        waits=[tok_rkb], inc="ve")

            if STAGE <= 2.5:
                return fin()

            tok_scol = {}
            for tt in range(NTT):
                w0 = ([tok_srow_b, tok_ssq] + stat_dma) if tt == 0 else []
                tmm = TR(pb[2][:, 0:8], srows[0:8, tt * 128:(tt + 1) * 128], identf[0:8, 0:8],
                         waits=w0 + bgw(2), inc="te")
                ev = V(lambda e, t=tt: e.tensor_copy(scols[:, t * 8:(t + 1) * 8], pb[2][:, 0:8]),
                       waits=[tmm], inc="ve")
                tok_scol[tt] = ev
                bg[2] = ev
            sc8 = scols.ap().rearrange("p (t c) -> p t c", c=8)
            tok_scall = p.tok("ve")
            t_eps2 = V(lambda e: e.tensor_scalar(rqc[:, :], sc8[:, :, 0], 1e-6, None, ALU.add),
                       waits=[tok_scall], inc="ve")
            V(lambda e: e.reciprocal(rqc[:, :], rqc[:, :]), waits=[t_eps2], inc="ve")
            tok_rqc = SC(lambda e: e.activation(rqc[:, :], rqc[:, :], AF.Sqrt),
                         waits=[p.tok("ve")], inc="se")

            st6 = statc.ap().rearrange("p (t c) -> p t c", c=6)
            for si in range(2):
                mcol = sc8[:, :, 2 + 2 * si]
                qcol = sc8[:, :, 3 + 2 * si]
                V(lambda e, m=mcol, s=si: e.tensor_copy(st6[:, :, 2 * s], m), waits=[tok_scall])
                t1 = V(lambda e, m=mcol: e.tensor_tensor(stdw[:, :], m, m, ALU.mult), inc="ve")
                t2 = V(lambda e, q=qcol: e.tensor_tensor(stdw[:, :], q, stdw[:, :], ALU.subtract),
                       waits=[t1], inc="ve")
                V(lambda e: e.tensor_scalar(stdw[:, :], stdw[:, :], 0.0, None, ALU.max),
                  waits=[t2], inc="ve")
                tv = SC(lambda e: e.activation(stdw[:, :], stdw[:, :], AF.Sqrt),
                        waits=[p.tok("ve")], inc="se")
                V(lambda e, s=si: e.tensor_copy(st6[:, :, 2 * s + 1], stdw[:, :]),
                  waits=[tv], inc="ve")
            tok_statfl = p.tok("ve")

            if STAGE <= 2.7:
                return fin()

            for tt in range(NTT):
                bcol = scols[:, tt * 8 + 1:tt * 8 + 2]
                for mo in range(2):
                    w0 = [tok_knfm[(m, n)] for m in range(2) for n in range(4)] \
                        if (tt == 0 and mo == 0) else []
                    tmm = TR(pbt[:, 0:128], knfm[:, mo * GA + tt * 128:mo * GA + (tt + 1) * 128],
                             identb[:, :], waits=w0 + bgw("t0"), inc="te")
                    V(lambda e, t=tt, m=mo: e.tensor_copy(
                        kntm[:, t * 256 + m * 128:t * 256 + (m + 1) * 128], pbt[:, 0:128]),
                      waits=[tmm], inc="ve")
                    ev = V(lambda e, t=tt, m=mo, b=bcol: e.tensor_scalar(
                        kbntm[:, t * 256 + m * 128:t * 256 + (m + 1) * 128], pbt[:, 0:128],
                        b, -1.0, ALU.mult, ALU.mult),
                        waits=[tok_scol[tt]], inc="ve")
                    bg["t0"] = ev
                    tmm2 = TR(pbt2[:, 0:128],
                              vfm[:, mo * VAp + 30 + tt * 128:mo * VAp + 30 + (tt + 1) * 128],
                              identb[:, :],
                              waits=([silu_toks[4 + mo]] if tt == 0 else []) + bgw("t1"),
                              inc="te")
                    ev2 = V(lambda e, t=tt, m=mo, b=bcol: e.tensor_scalar(
                        vbtm[:, t * 256 + m * 128:t * 256 + (m + 1) * 128], pbt2[:, 0:128],
                        b, None, ALU.mult),
                        waits=[tmm2, tok_scol[tt]], inc="ve")
                    bg["t1"] = ev2
            tok_tm_all = p.tok("ve")

            if STAGE <= 3:
                return fin()

            for gi in range(NGI):
                for ci in range(4):
                    i = gi * 4 + ci
                    for mo in range(2):
                        w0 = (bgw(4) + [tok_knfm[(m, n)] for m in range(2) for n in range(4)]) \
                            if (gi == 0 and ci == 0 and mo == 0) else []
                        tmm = MM(pb[4][32 * ci:32 * ci + 32, gi * 32:(gi + 1) * 32],
                                 knfm[:, mo * GA + i * 32:mo * GA + (i + 1) * 32],
                                 knfm[:, mo * GA + i * 32:mo * GA + (i + 1) * 32],
                                 start=(mo == 0), stop=(mo == 1), tp=(0, 32 * ci),
                                 waits=w0,
                                 inc="te" if (gi == NGI - 1 and ci == 3 and mo == 1) else None)
            tok_Amm = tmm
            evA = []
            for gi in range(NGI):
                bcol = scols[:, gi * 8 + 1:gi * 8 + 2]
                evA.append(V(lambda e, g_=gi, b=bcol: e.scalar_tensor_tensor(
                    Apk[:, g_ * 32:(g_ + 1) * 32], pb[4][:, g_ * 32:(g_ + 1) * 32], b,
                    masksl[:, :], ALU.mult, ALU.mult),
                    waits=[tok_Amm, tok_scol[gi]], inc="ve"))
            bg[4] = evA[-1]
            eyeb = eye32s.ap().rearrange("p (o c) -> p o c", o=1).broadcast_to([128, NGI, 32])
            tok_T = V(lambda e: e.tensor_tensor(
                Tpk.ap().rearrange("p (g c) -> p g c", g=NGI),
                Apk.ap().rearrange("p (g c) -> p g c", g=NGI), eyeb, ALU.add),
                waits=evA, inc="ve")
            for lvl in range(4):
                tok_at = V(lambda e: e.transpose(ATpk[:, :], Apk[:, :]), waits=[tok_T], inc="ve")
                for gi in range(NGI):
                    for ci in range(4):
                        w0 = ([tok_at] + bgw(4)) if (gi == 0 and ci == 0) else []
                        tmm = MM(pb[4][32 * ci:32 * ci + 32, gi * 32:(gi + 1) * 32],
                                 ATpk[32 * ci:32 * ci + 32, gi * 32:(gi + 1) * 32],
                                 Apk[32 * ci:32 * ci + 32, gi * 32:(gi + 1) * 32],
                                 start=True, stop=True, tp=(32 * ci, 32 * ci),
                                 waits=w0, inc="te" if (gi == NGI - 1 and ci == 3) else None)
                tok_A2 = V(lambda e: e.tensor_copy(Apk[:, :], pb[4][:, :]), waits=[tmm], inc="ve")
                bg[4] = tok_A2
                tok_a2t = V(lambda e: e.transpose(ATpk[:, :], Apk[:, :]), waits=[tok_A2], inc="ve")
                for gi in range(NGI):
                    for ci in range(4):
                        w0 = ([tok_a2t] + bgw(5)) if (gi == 0 and ci == 0) else []
                        tmm = MM(pb[5][32 * ci:32 * ci + 32, gi * 32:(gi + 1) * 32],
                                 ATpk[32 * ci:32 * ci + 32, gi * 32:(gi + 1) * 32],
                                 Tpk[32 * ci:32 * ci + 32, gi * 32:(gi + 1) * 32],
                                 start=True, stop=True, tp=(32 * ci, 32 * ci),
                                 waits=w0, inc="te" if (gi == NGI - 1 and ci == 3) else None)
                tok_T = V(lambda e: e.scalar_tensor_tensor(Tpk[:, :], pb[5][:, :], 1.0, Tpk[:, :],
                                                           ALU.mult, ALU.add),
                          waits=[tmm], inc="ve")
                bg[5] = tok_T
            tok_TT = V(lambda e: e.transpose(TTpk[:, :], Tpk[:, :]), waits=[tok_T], inc="ve")

            for gi in range(NGI):
                for ci in range(4):
                    i = gi * 4 + ci
                    w0 = ([tok_TT, tok_tm_all] + bgw(5) + bgw(3)) if ci == 0 else []
                    MM(pb[5][32 * ci:32 * ci + 32, 0:256],
                       TTpk[32 * ci:32 * ci + 32, gi * 32:(gi + 1) * 32],
                       vbtm[32 * ci:32 * ci + 32, gi * 256:(gi + 1) * 256],
                       start=True, stop=True, tp=(32 * ci, 32 * ci), waits=w0)
                    for mo in range(2):
                        MM(pb[3][32 * ci:32 * ci + 32, 256:288],
                           knfm[:, mo * GA + i * 32:mo * GA + (i + 1) * 32],
                           qfm[:, mo * GA + i * 32:mo * GA + (i + 1) * 32],
                           start=(mo == 0), stop=(mo == 1), tp=(0, 32 * ci))
                    for mo in range(2):
                        tmm = MM(pb[3][:, (ci * 2 + mo) * 32:(ci * 2 + mo + 1) * 32],
                                 kbntm[32 * ci:32 * ci + 32,
                                       gi * 256 + mo * 128:gi * 256 + (mo + 1) * 128],
                                 TTpk[32 * ci:32 * ci + 32, gi * 32:(gi + 1) * 32],
                                 start=True, stop=True, tp=(32 * ci, 0),
                                 inc="te" if (ci == 3 and mo == 1) else None)
                e1 = V(lambda e, g_=gi: e.tensor_copy(ubuf[:, g_ * 256:(g_ + 1) * 256],
                                                      pb[5][:, 0:256]), waits=[tmm], inc="ve")
                V(lambda e, g_=gi: e.tensor_tensor(atm[:, g_ * 32:(g_ + 1) * 32],
                                                   pb[3][:, 256:288], masktu[:, :], ALU.mult),
                  inc="ve")
                e3 = V(lambda e, g_=gi: e.tensor_copy(wnfm[:, g_ * 256:(g_ + 1) * 256],
                                                      pb[3][:, 0:256]), inc="ve")
                bg[5], bg[3] = e1, e3
            tok_batch = p.tok("ve")

            if STAGE <= 4:
                return fin()

            prev_sevac = tok_batch
            prev_oevac = {0: None, 1: None, 2: None, 3: None}
            for i in range(NCHG):
                gi, ci = i // 4, i % 4
                gchunk = g * NCHG + i
                slot = (i // 4) % 2
                first = (gchunk == 0)
                w0 = [tok_batch, prev_sevac] + bgw("u2" + str(slot))
                t_u2mm = MM(pb[5][32 * ci:32 * ci + 32, 256:512],
                            eye32s[32 * ci:32 * ci + 32, :],
                            ubuf[32 * ci:32 * ci + 32, gi * 256:(gi + 1) * 256],
                            start=True, stop=first, tp=(32 * ci, 32 * ci), waits=w0,
                            inc="te" if first else None)
                if not first:
                    for mo in range(2):
                        t_u2mm = MM(pb[5][32 * ci:32 * ci + 32, 256:512],
                                    wnfm[:, i * 64 + mo * 32:i * 64 + (mo + 1) * 32],
                                    Sbf[:, mo * 256:(mo + 1) * 256],
                                    start=False, stop=(mo == 1), tp=(0, 32 * ci),
                                    inc="te" if mo == 1 else None)
                t_u2 = SC(lambda e, c=ci, s=slot: e.copy(
                    u2sb[32 * c:32 * c + 32, s * 256:(s + 1) * 256],
                    pb[5][32 * c:32 * c + 32, 256:512]), waits=[t_u2mm], inc="se")
                bg["u2" + str(slot)] = t_u2
                if not first:
                    for mo in range(2):
                        MM(pb[2][32 * ci:32 * ci + 32, 0:256],
                           qfm[:, mo * GA + i * 32:mo * GA + (i + 1) * 32],
                           Sbf[:, mo * 256:(mo + 1) * 256],
                           start=(mo == 0), stop=False, tp=(0, 32 * ci),
                           waits=(([prev_oevac[ci]] if prev_oevac[ci] else bgw(2))
                                  if mo == 0 else ()))
                MM(pb[2][32 * ci:32 * ci + 32, 0:256],
                   atm[32 * ci:32 * ci + 32, gi * 32:(gi + 1) * 32],
                   u2sb[32 * ci:32 * ci + 32, slot * 256:(slot + 1) * 256],
                   start=first, stop=True, tp=(32 * ci, 32 * ci),
                   waits=[t_u2] + (([prev_oevac[ci]] if prev_oevac[ci] else bgw(2))
                                   if first else []))
                for mo in range(2):
                    tmm = MM(pb[0][:, mo * 256:(mo + 1) * 256],
                             kntm[32 * ci:32 * ci + 32,
                                  gi * 256 + mo * 128:gi * 256 + (mo + 1) * 128],
                             u2sb[32 * ci:32 * ci + 32, slot * 256:(slot + 1) * 256],
                             start=True, stop=True, tp=(32 * ci, 0),
                             inc="te" if mo == 1 else None)
                t_smm = tmm
                prev_oevac[ci] = SC(lambda e, c=ci, g_=gi: e.activation(
                    otm[32 * c:32 * c + 32, g_ * 256:(g_ + 1) * 256],
                    pb[2][32 * c:32 * c + 32, 0:256], AF.Copy,
                    scale=rqc[32 * c:32 * c + 32, g_:g_ + 1]),
                    waits=[t_smm] + ([tok_rqc] if i == 0 else []), inc="se")
                V(lambda e: e.tensor_tensor(S32[:, :], pb[0][:, :], S32[:, :], ALU.add),
                  waits=[t_smm])
                prev_sevac = V(lambda e: e.tensor_copy(Sbf[:, :], S32[:, :]), inc="ve")
            tok_scan = p.tok("ve")
            tok_scan_se = p.tok("se")

            if STAGE <= 5:
                return fin()

            dstt = None
            for tt in range(NTT):
                for mo in range(2):
                    w0 = [tok_scan, tok_scan_se] if (tt == 0 and mo == 0) else []
                    tmm = TR(pbt[:, 0:128], otm[:, tt * 256 + mo * 128:tt * 256 + (mo + 1) * 128],
                             identb[:, :], waits=w0 + bgw("t0"), inc="te")
                    ev = V(lambda e, t=tt, m=mo: e.tensor_copy(
                        dfm[:, m * GA + t * 128:m * GA + (t + 1) * 128], pbt[:, 0:128]),
                        waits=[tmm], inc="ve")
                    bg["t0"] = ev
                V(lambda e, t=tt: e.tensor_reduce(dmcol[:, t:t + 1], otm[:, t * 256:(t + 1) * 256],
                                                  AX.X, ALU.add),
                  waits=[tok_scan, tok_scan_se, tok_statfl] if tt == 0 else ())
                dstt = V(lambda e, t=tt: e.scalar_tensor_tensor(
                    sqbuf[:, 0:256], otm[:, t * 256:(t + 1) * 256], 1.0,
                    otm[:, t * 256:(t + 1) * 256], ALU.mult, ALU.mult,
                    accum_out=dmcol[:, NTT + t:NTT + t + 1]), inc="ve")
            td1 = V(lambda e: e.tensor_scalar(dmcol[:, :], dmcol[:, :], 1.0 / 256, None,
                                              ALU.mult), waits=[dstt], inc="ve")
            V(lambda e: e.tensor_copy(st6[:, :, 4], dmcol[:, 0:NTT]), waits=[td1])
            td2 = V(lambda e: e.tensor_tensor(stdw[:, :], dmcol[:, 0:NTT], dmcol[:, 0:NTT],
                                              ALU.mult), waits=[td1], inc="ve")
            td3 = V(lambda e: e.tensor_tensor(stdw[:, :], dmcol[:, NTT:2 * NTT], stdw[:, :],
                                              ALU.subtract), waits=[td2], inc="ve")
            V(lambda e: e.tensor_scalar(stdw[:, :], stdw[:, :], 0.0, None, ALU.max),
              waits=[td3], inc="ve")
            tds = SC(lambda e: e.activation(stdw[:, :], stdw[:, :], AF.Sqrt),
                     waits=[p.tok("ve")], inc="se")
            tok_statc = V(lambda e: e.tensor_copy(st6[:, :, 5], stdw[:, :]),
                          waits=[tds], inc="ve")
            tok_dfm = p.tok("ve")

            ccm = dram["ccm_ia"]
            ccs = dram["ccs_ia"]
            for rowbase, src, pitch, off in [(0, fsb, GA, 0), (256, flb, GA, 0),
                                             (512, dfm, GA, 0), (768, vfm, VAp, 30)]:
                for mo in range(2):
                    exch_wtoks.append(DMA(
                        "sync", "sw",
                        ccm.ap()[g * CCM + rowbase + mo * 128:
                                 g * CCM + rowbase + (mo + 1) * 128, :],
                        src[:, mo * pitch + off:mo * pitch + off + GA],
                        waits=[tok_dfm]))
            exch_wtoks.append(DMA("gpsimd", "gw",
                                  ccs.ap()[:, g * CCS:(g + 1) * CCS],
                                  statc[:, :], waits=[tok_statc]))
            exch_sw_last[g] = p.tok("sw")

            return fin()

        exch_wtoks = []
        exch_sw_last = {}
        bar = [p.tok("ve"), tok_const_sl, tok_const_gl]
        for g in range(NGB):
            bar = a_group(g, bar)
        if STAGE >= 6:
            p.op("gpsimd", lambda e: e.collective_compute(
                "AllGather", ALU.bypass, replica_groups=[[0, 1, 2, 3], [4, 5, 6, 7]],
                ins=[dram["ccs_ia"].ap().opt()], outs=[dram["ccs_oa"].ap().opt()]),
                [p.tok("gw")], [("cc", 1)])
            tok_cc_stats = p.tok("cc")
            gl_ = NGB - 1
            p.op("gpsimd", lambda e, g_=gl_: e.collective_compute(
                "AllGather", ALU.bypass, replica_groups=[[0, 1, 2, 3], [4, 5, 6, 7]],
                ins=[dram["ccm_ia"].ap()[g_ * CCM:(g_ + 1) * CCM, :].opt()],
                outs=[dram["ccm_oa"].ap()[g_ * 4 * CCM:(g_ + 1) * 4 * CCM, :].opt()]),
                [exch_sw_last[gl_]], [("cc", 1)])
        tok_cc_all = p.tok("cc") if p.sems.get("cc") else None
        if not p.sems.get("cc"):
            tok_cc_stats = None

        BSTAGE = B_STAGE
        bstart = bar
        p.op("tensor", lambda e: e.nop(), bstart, [("te", 1)])
        p.op("vector", lambda e: e.engine_nop(), bstart, [("ve", 1)])
        p.op("scalar", lambda e: e.nop(), bstart, [("se", 1)])
        p.op("gpsimd", lambda e: e.engine_nop(), bstart, [])
        p.op("sync", lambda e: e.nop(), bstart, [])

        if BSTAGE == 0 or STAGE < 6:
            tok_z = V(lambda e: e.memset(outb[:, :], 0.0), inc="ve")
            for r in range(TB // 128):
                DMA("sync", "sw", dram["out"].ap()[r * 128:(r + 1) * 128, :], outb[:, :],
                    waits=[tok_z])
        else:
            xb = dram["xb_fm"]
            t_wr2 = DMA("gpsimd", "gl",
                        wr2b.ap().rearrange("p (c n) -> p c n", c=16),
                        dram["wr2"].ap().rearrange("(c p) n -> p c n", p=128))
            t_wid = DMA("gpsimd", "gl",
                        widb.ap().rearrange("p (c n) -> p c n", c=8),
                        dram["wid"].ap().rearrange("(c p) n -> p c n", p=128))
            t_gwt = DMA("sync", "sl",
                        gwtb.ap().rearrange("p (c n) -> p c n", c=8),
                        dram["gwt"].ap().rearrange("(c p) n -> p c n", p=128))
            for nm, buf in [("br1", br1c), ("br2", br2c), ("bid", bidc), ("alpha", alphac),
                            ("onwc", onwcb), ("ltg", ltgr), ("lth", lthr)]:
                DMA("sync", "sl", buf[:, :], dram[nm].ap())
            t_bload = p.tok("sl")
            t_gload = p.tok("gl")
            t_sa = SC(lambda e: e.activation(sigac[:, :], alphac[:, :], AF.Sigmoid),
                      waits=[t_bload], inc="se")
            t_eg = SC(lambda e: e.activation(ltgr[:, :], ltgr[:, :], AF.Exp), inc="se")
            t_eh = SC(lambda e: e.activation(lthr[:, :], lthr[:, :], AF.Exp), inc="se")
            t_t1 = V(lambda e: e.tensor_copy(
                tau12.ap().rearrange("o (g x) -> o g x", g=2),
                ltgr.ap().rearrange("o (g x) -> o g x", x=1).broadcast_to([1, 2, 6])),
                waits=[t_eg], inc="ve")
            t_t2 = V(lambda e: e.scalar_tensor_tensor(
                tau12.ap().rearrange("o (g x) -> o g x", g=4),
                lthr.ap().rearrange("o (g x) -> o g x", x=1).broadcast_to([1, 4, 3]),
                0.0, tau12.ap().rearrange("o (g x) -> o g x", g=4), ALU.mult, ALU.add),
                waits=[t_eh, t_t1], inc="ve")
            t_t3 = V(lambda e: e.reciprocal(tau12r[:, :], tau12[:, :]), waits=[t_t2], inc="ve")
            tmm = MM(pb[5][:, 0:12], onesf[:, :], tau12r[0:1, 0:12], start=True, stop=True,
                     waits=[t_t3] + bgw(5), inc="te")
            t_taub = V(lambda e: e.tensor_copy(taub[:, :], pb[5][:, 0:12]), waits=[tmm], inc="ve")
            bg[5] = t_taub

            wr1c = [0]
            woc = [0]
            ec = [0]
            wr1guard = {0: None, 1: None}
            woguard = {0: None, 1: None}

            def b_half(hb):
                thc = hb * TBH
                t_bxf = DMA("gpsimd", "gl",
                            bxf.ap().rearrange("p (c t) -> p c t", c=8),
                            xb.ap().rearrange("(c p) t -> p c t", p=128)[:, :, thc:thc + TBH],
                            waits=[p.tok("ve"), t_gload])
                t_xz = DMA("sync", "sl",
                           xz.ap().rearrange("p (c t) -> p c t", c=8),
                           xb.ap().rearrange("(c p) t -> p c t", p=128)[:, :, thc:thc + TBH],
                           waits=[p.tok("ve")])
                sst_src = dram["ccs_oa"].ap().rearrange("(j p) c -> p j c", p=128)[
                    :, :, hb * 48:(hb + 1) * 48]
                p.op("sync",
                     lambda e, o=sstat.ap().rearrange("p (j c) -> p j c", j=4), i=sst_src:
                         e.dma_start(out=o, in_=_dc.replace(
                             i, offset=i.offset + _scoff_get(e) * (NG * CCS // 4) * 4
                             if False else i.offset + _scoff_get(e) * CCS)),
                     [t_xz, tok_cc_stats], [("sl", 16)])
                t_sst = p.tok("sl")
                t_rz = V(lambda e: e.memset(rinstat[:, :], 0.0), waits=[p.tok("te")], inc="ve")
                for j in range(4):
                    for tt in range(8):
                        w0 = [t_sst, t_rz] + bgw("t0")
                        tmm = TR(pbt[0:6, 0:128],
                                 sstat[:, j * 48 + tt * 6:j * 48 + (tt + 1) * 6],
                                 identb[:, :], waits=w0, inc="te")
                        ev = V(lambda e, j_=j, t=tt: e.tensor_copy(
                            rinstat[32 * j_:32 * j_ + 6, t * 128:(t + 1) * 128],
                            pbt[0:6, 0:128]), waits=[tmm], inc="ve")
                        bg["t0"] = ev
                tok_rin = p.tok("ve")

                for sg in range(2):
                    for mo in range(16):
                        slot = mo % 2
                        wr1c[0] += 1
                        t_w = DMA("gpsimd", "wr1l" if slot == 0 else "wr1m",
                                  wr1b.ap()[:, slot * 1152:(slot + 1) * 1152]
                                  .rearrange("p (c n) -> p c n", c=9),
                                  dram["wr1"].ap().rearrange("(c p) n -> p c n", p=128)
                                  [:, :, mo * 128:(mo + 1) * 128],
                                  waits=[wr1guard[slot]])
                        bank = 1 + mo % 2
                        for ki in range(9):
                            rhs = bxf[:, ki * TBH + sg * 512:ki * TBH + (sg + 1) * 512]                                 if ki < 8 else rinstat[:, sg * 512:(sg + 1) * 512]
                            w0 = ([t_w, t_bxf, tok_rin] + bgw(bank)) if ki == 0 else []
                            tmm = MM(pb[bank][:, 0:512],
                                     wr1b[:, slot * 1152 + ki * 128:slot * 1152 + (ki + 1) * 128],
                                     rhs, start=(ki == 0), stop=(ki == 8), waits=w0,
                                     inc="te" if ki == 8 else None)
                        wr1guard[slot] = tmm
                        ev = SC(lambda e, m=mo, s=sg, b=bank: e.activation(
                            hmid[:, m * 512:(m + 1) * 512], pb[b][:, 0:512], AF.Gelu,
                            bias=br1c[:, m:m + 1]), waits=[tmm, t_bload], inc="se")
                        bg[bank] = ev
                    t_hm = p.tok("se")
                    for ki in range(16):
                        w0 = ([t_hm, t_gload] + bgw(3)) if ki == 0 else []
                        tmm = MM(pb[3][0:12, 0:512], wr2b[:, ki * 12:(ki + 1) * 12],
                                 hmid[:, ki * 512:(ki + 1) * 512],
                                 start=(ki == 0), stop=(ki == 15), waits=w0,
                                 inc="te" if ki == 15 else None)
                    ev = V(lambda e, s=sg: e.tensor_scalar(
                        logfm[0:12, s * 512:(s + 1) * 512], pb[3][0:12, 0:512],
                        br2c[0:12, 0:1], None, ALU.add), waits=[tmm, t_bload], inc="ve")
                    bg[3] = ev
                tok_log = p.tok("ve")

                for tt in range(8):
                    w0 = ([tok_log] if tt == 0 else []) + bgw(2)
                    tmm = TR(pb[2][:, 0:12], logfm[0:12, tt * 128:(tt + 1) * 128],
                             identf[0:12, 0:12], waits=w0, inc="te")
                    ev = V(lambda e, t=tt: e.tensor_copy(ltm[:, t * 12:(t + 1) * 12],
                                                         pb[2][:, 0:12]), waits=[tmm], inc="ve")
                    bg[2] = ev
                for tt in range(8):
                    pv = ptm.ap().rearrange("p (t c) -> p t c", c=12)[:, tt]
                    pv3 = ptm.ap().rearrange("p (t h c) -> p t h c", t=8, h=4)[:, tt]
                    t1 = V(lambda e, t=tt: e.tensor_tensor(
                        ptm[:, t * 12:(t + 1) * 12], ltm[:, t * 12:(t + 1) * 12],
                        taub[:, :], ALU.mult), waits=[t_taub], inc="ve")
                    t2 = V(lambda e, v3=pv3: e.tensor_reduce(p4a[:, :], v3, AX.X, ALU.max,
                                                             negate=True),
                           waits=[t1], inc="ve")
                    t3 = V(lambda e, v3=pv3: e.tensor_tensor(
                        v3, v3, p4a.ap().rearrange("p (h x) -> p h x", x=1)
                        .broadcast_to([128, 4, 3]), ALU.add), waits=[t2], inc="ve")
                    t4 = SC(lambda e, t=tt: e.activation(ptm[:, t * 12:(t + 1) * 12],
                                                         ptm[:, t * 12:(t + 1) * 12], AF.Exp),
                            waits=[t3], inc="se")
                    t5 = V(lambda e, v3=pv3: e.tensor_reduce(p4a[:, :], v3, AX.X, ALU.add),
                           waits=[t4], inc="ve")
                    t6 = V(lambda e: e.reciprocal(p4a[:, :], p4a[:, :]), waits=[t5], inc="ve")
                    t7 = V(lambda e, v3=pv3: e.tensor_tensor(
                        v3, v3, p4a.ap().rearrange("p (h x) -> p h x", x=1)
                        .broadcast_to([128, 4, 3]), ALU.mult), waits=[t6], inc="ve")
                    t8 = V(lambda e, t=tt: e.tensor_scalar(
                        ptm[:, t * 12:(t + 1) * 12], ptm[:, t * 12:(t + 1) * 12],
                        1.0 - 3.0 * R_EPS, R_EPS, ALU.mult, ALU.add), waits=[t7], inc="ve")
                tok_pfm = p.tok("ve")
                if DBG and hb == 0:
                    bg["dwc"] = DMA("sync", "dw", dram["d_ptm"].ap(), ptm[:, :],
                                    waits=[tok_pfm, bg.get("dwc")])

                exj = [exch, exch2]
                t_exl = {}
                prev_exl = [None]
                for sg in range(2):
                    rowdma = []
                    for tl in range(4):
                        tt = sg * 4 + tl
                        w0 = ([tok_pfm] if tl == 0 else []) + bgw(2)
                        tmm = TR(pb[2][0:12, 0:128], ptm[:, tt * 12:(tt + 1) * 12],
                                 identf[0:128, 0:128], waits=w0, inc="te")
                        ev = V(lambda e: e.tensor_copy(pfm12[0:12, 0:128], pb[2][0:12, 0:128]),
                               waits=[tmm, rowdma[-1] if rowdma else None], inc="ve")
                        bg[2] = ev
                        rowdma.append(DMA(
                            "sync", "prw",
                            prow.ap().rearrange("o (r t) -> o r t", r=12)[:, :, tl * 128:(tl + 1) * 128],
                            pfm12.ap()[0:12, 0:128], waits=[ev]))
                    for ki in range(8):
                        w0 = ([t_gload, t_bxf] + bgw(4)) if ki == 0 else []
                        tmm = MM(pb[4][0:4, 0:512], widb[:, ki * 4:(ki + 1) * 4],
                                 bxf[:, ki * TBH + sg * 512:ki * TBH + (sg + 1) * 512],
                                 start=(ki == 0), stop=(ki == 7), waits=w0,
                                 inc="te" if ki == 7 else None)
                    t_id1 = SC(lambda e: e.activation(idt4[:, :], pb[4][0:4, 0:512], AF.Sigmoid,
                                                      bias=bidc[0:4, 0:1]),
                               waits=[tmm, t_bload, t_sa], inc="se")
                    ev = V(lambda e: e.tensor_scalar(
                        idb4[0:4, 0:512], idt4[:, :], sigac[0:4, 0:1],
                        EPS_ID, ALU.mult, ALU.add), waits=[t_id1], inc="ve")
                    bg[4] = ev
                    rowdma.append(DMA("sync", "prw",
                                      irow.ap().rearrange("o (r t) -> o r t", r=4),
                                      idb4.ap()[0:4, 0:512], waits=[ev]))
                    tok_rows = p.tok("prw")
                    for j in range(4):
                        eb = exj[j % 2]
                        guard = bg.get("ex" + str(j % 2))
                        src_ap = dram["ccm_oa"].ap().rearrange("(c p) t -> p c t", p=128)[
                            :, 8 * j:8 * j + 8,
                            hb * TBH + sg * 512:hb * TBH + (sg + 1) * 512]
                        p.op("sync",
                             lambda e, o=eb.ap().rearrange("p (c t) -> p c t", c=8), i=src_ap:
                                 e.dma_start(out=o, in_=dynoff(e, i, 4 * CCM)),
                             [guard, prev_exl[0], tok_cc_all,
                              tok_pfm if (sg == 0 and j == 0) else None], [("exl", 16)])
                        t_exl[(sg, j)] = p.tok("exl")
                        prev_exl[0] = t_exl[(sg, j)]
                        cbev = []
                        for r in range(3):
                            tmm = MM(pb[3][:, 0:512], onesb[:, :],
                                     prow[0:1, (3 * j + r) * 512:(3 * j + r + 1) * 512],
                                     start=True, stop=True,
                                     waits=[tok_rows] + bgw(3), inc="te")
                            ev = V(lambda e, r_=r: e.tensor_copy(
                                cbuf[:, r_ * 512:(r_ + 1) * 512], pb[3][:, 0:512]),
                                waits=[tmm], inc="ve")
                            bg[3] = ev
                            cbev.append(ev)
                        tmm = MM(pb[4][:, 0:512], onesb[:, :],
                                 irow[0:1, j * 512:(j + 1) * 512],
                                 start=True, stop=True, waits=[tok_rows] + bgw(4), inc="te")
                        ev = V(lambda e: e.tensor_copy(cbuf[:, 3 * 512:4 * 512], pb[4][:, 0:512]),
                               waits=[tmm], inc="ve")
                        bg[4] = ev
                        cbev.append(ev)
                        for mo in range(2):
                            ts0 = V(lambda e, b=eb, m=mo: e.tensor_tensor(
                                obuf[:, m * 512:(m + 1) * 512],
                                b[:, (0 + m) * 512:(1 + m) * 512], cbuf[:, 0:512], ALU.mult),
                                waits=cbev + [t_exl[(sg, j)]], inc="ve")
                            for t_i, cb_i in [(2, 1), (4, 2), (6, 3)]:
                                ts1 = V(lambda e, b=eb, m=mo, ti=t_i, ci=cb_i: e.tensor_tensor(
                                    scr[:, 0:512], b[:, (ti + m) * 512:(ti + 1 + m) * 512],
                                    cbuf[:, ci * 512:(ci + 1) * 512], ALU.mult),
                                    inc="ve")
                                ts0 = V(lambda e, m=mo: e.tensor_tensor(
                                    obuf[:, m * 512:(m + 1) * 512],
                                    obuf[:, m * 512:(m + 1) * 512], scr[:, 0:512], ALU.add),
                                    waits=[ts1, ts0], inc="ve")
                        tok_o = ts0
                        for mo in range(2):
                            tsq = SC(lambda e, m=mo: e.square(scr[:, 0:512],
                                                              obuf[:, m * 512:(m + 1) * 512]),
                                     waits=[tok_o, p.tok("te")], inc="se")
                            tmm = MM(pb[5][0:1, 0:512], colb256[:, :], scr[:, 0:512],
                                     start=(mo == 0), stop=(mo == 1),
                                     waits=[tsq] + (bgw(5) if mo == 0 else []), inc="te")
                        tr1 = V(lambda e: e.tensor_scalar(rmsrow[0:1, 0:512], pb[5][0:1, 0:512],
                                                          1e-5, None, ALU.add),
                                waits=[tmm], inc="ve")
                        bg[5] = tr1
                        tr2 = V(lambda e: e.reciprocal(rmsrow[0:1, 0:512], rmsrow[0:1, 0:512]),
                                waits=[tr1], inc="ve")
                        tr3 = SC(lambda e: e.activation(rmsrow[0:1, 0:512], rmsrow[0:1, 0:512], AF.Sqrt),
                                 waits=[tr2], inc="se")
                        tmm = MM(pb[5][:, 0:512], onesf[:, :], rmsrow[0:1, 0:512],
                                 start=True, stop=True, waits=[tr3] + bgw(5), inc="te")
                        tr4 = V(lambda e: e.tensor_copy(rbc[:, :], pb[5][:, 0:512]),
                                waits=[tmm], inc="ve")
                        bg[5] = tr4
                        for mo in range(2):
                            ev = V(lambda e, j_=j, m=mo: e.scalar_tensor_tensor(
                                ofm[:, (j_ * 2 + m) * 512:(j_ * 2 + m + 1) * 512],
                                obuf[:, m * 512:(m + 1) * 512], onwcb[:, m:m + 1],
                                rbc[:, :], ALU.mult, ALU.mult),
                                waits=[tr4, bg.get("ofmdump")], inc="ve")
                        bg["ex" + str(j % 2)] = ev
                    tok_ofm = p.tok("ve")
                    if DBG and hb == 0 and sg == 0:
                        bg["ofmdump"] = bg["dwc"] = DMA(
                            "sync", "dw", dram["d_ofm"].ap(), ofm[:, :],
                            waits=[tok_ofm, bg.get("dwc")])
                    for mo in range(8):
                        slot = mo % 2
                        woc[0] += 1
                        t_w = DMA("gpsimd", "wol" if slot == 0 else "wom",
                                  wob.ap()[:, slot * 1024:(slot + 1) * 1024]
                                  .rearrange("p (c n) -> p c n", c=8),
                                  dram["wo"].ap().rearrange("(c p) n -> p c n", p=128)
                                  [:, :, mo * 128:(mo + 1) * 128],
                                  waits=[woguard[slot]])
                        bank = 1 + mo % 2
                        for ki in range(8):
                            w0 = ([t_w, tok_ofm] + bgw(bank)) if ki == 0 else []
                            tmm = MM(pb[bank][:, 0:512],
                                     wob[:, slot * 1024 + ki * 128:slot * 1024 + (ki + 1) * 128],
                                     ofm[:, ki * 512:(ki + 1) * 512],
                                     start=(ki == 0), stop=(ki == 7), waits=w0,
                                     inc="te" if ki == 7 else None)
                        woguard[slot] = tmm
                        ev = V(lambda e, m=mo, s=sg, b=bank: e.tensor_tensor(
                            xz[:, m * TBH + s * 512:m * TBH + (s + 1) * 512],
                            pb[b][:, 0:512],
                            xz[:, m * TBH + s * 512:m * TBH + (s + 1) * 512], ALU.add),
                            waits=[tmm, t_xz], inc="ve")
                        bg[bank] = ev
                tok_x1 = p.tok("ve")

                for sg in range(2):
                    cts = []
                    for ki in range(8):
                        cts.append(V(lambda e, k=ki, s=sg: e.tensor_copy(
                            bxf[:, k * TBH + s * 512:k * TBH + (s + 1) * 512],
                            xz[:, k * TBH + s * 512:k * TBH + (s + 1) * 512]),
                            waits=[tok_x1] if ki == 0 else [], inc="ve"))
                    for ki in range(8):
                        w0m = ([cts[-1]] + bgw(4)) if ki == 0 else []
                        tmm_mu = MM(pb[4][0:1, 0:512], colb1024[:, :],
                                    bxf[:, ki * TBH + sg * 512:ki * TBH + (sg + 1) * 512],
                                    start=(ki == 0), stop=(ki == 7), waits=w0m,
                                    inc="te" if ki == 7 else None)
                    for ki in range(8):
                        tsq = SC(lambda e, k=ki, s=sg: e.square(
                            scr[:, 0:512], bxf[:, k * TBH + s * 512:k * TBH + (s + 1) * 512]),
                            waits=[cts[-1], p.tok("te")], inc="se")
                        tmm_m2 = MM(pb[5][0:1, 0:512], colb1024[:, :], scr[:, 0:512],
                                    start=(ki == 0), stop=(ki == 7),
                                    waits=[tsq] + (bgw(5) if ki == 0 else []),
                                    inc="te")
                    l0 = V(lambda e: e.tensor_copy(lnrow[0:1, 0:512], pb[4][0:1, 0:512]),
                           waits=[tmm_mu, tmm_m2], inc="ve")
                    bg[4] = l0
                    l1 = V(lambda e: e.tensor_copy(lnrow[0:1, 512:1024], pb[5][0:1, 0:512]),
                           waits=[tmm_m2], inc="ve")
                    bg[5] = l1
                    l2 = V(lambda e: e.tensor_tensor(lnrow[0:1, 1024:1536], lnrow[0:1, 0:512],
                                                     lnrow[0:1, 0:512], ALU.mult),
                           waits=[l0, l1], inc="ve")
                    l3 = V(lambda e: e.tensor_tensor(lnrow[0:1, 1024:1536],
                                                     lnrow[0:1, 512:1024],
                                                     lnrow[0:1, 1024:1536], ALU.subtract),
                           waits=[l2], inc="ve")
                    l4 = V(lambda e: e.tensor_scalar(lnrow[0:1, 1024:1536],
                                                     lnrow[0:1, 1024:1536], 1e-5, None, ALU.add),
                           waits=[l3], inc="ve")
                    l5 = V(lambda e: e.reciprocal(lnrow[0:1, 1024:1536], lnrow[0:1, 1024:1536]),
                           waits=[l4], inc="ve")
                    l6 = SC(lambda e: e.activation(lnrow[0:1, 1024:1536], lnrow[0:1, 1024:1536],
                                                   AF.Sqrt), waits=[l5], inc="se")
                    l7 = V(lambda e: e.tensor_scalar(lnrow[0:1, 1536:2048], lnrow[0:1, 0:512],
                                                     -1.0, None, ALU.mult), waits=[l6], inc="ve")
                    tmm = MM(pb[4][:, 0:512], onesf[:, :], lnrow[0:1, 1024:1536],
                             start=True, stop=True, waits=[l6] + bgw(4), inc="te")
                    b1 = V(lambda e: e.tensor_copy(rbc[:, :], pb[4][:, 0:512]),
                           waits=[tmm], inc="ve")
                    bg[4] = b1
                    tmm = MM(pb[5][:, 0:512], onesf[:, :], lnrow[0:1, 1536:2048],
                             start=True, stop=True, waits=[l7] + bgw(5), inc="te")
                    b2 = V(lambda e: e.tensor_copy(mrbc[:, :], pb[5][:, 0:512]),
                           waits=[tmm], inc="ve")
                    bg[5] = b2
                    for ki in range(8):
                        a1 = V(lambda e, k=ki, s=sg: e.tensor_tensor(
                            tmpf[:, :], xz[:, k * TBH + s * 512:k * TBH + (s + 1) * 512],
                            mrbc[:, :], ALU.add), waits=[b1, b2], inc="ve")
                        a2 = V(lambda e, k=ki, s=sg: e.tensor_tensor(
                            xz[:, k * TBH + s * 512:k * TBH + (s + 1) * 512], tmpf[:, :],
                            rbc[:, :], ALU.mult), waits=[a1], inc="ve")
                        V(lambda e, k=ki, s=sg: e.tensor_copy(
                            zb[:, k * TBH + s * 512:k * TBH + (s + 1) * 512],
                            xz[:, k * TBH + s * 512:k * TBH + (s + 1) * 512]),
                            waits=[a2, bg.get("zbdump")], inc="ve")
                tok_z = p.tok("ve")
                if DBG and hb == 0:
                    bg["zbdump"] = bg["dwc"] = DMA("sync", "dw", dram["d_zb"].ap(), zb[:, :],
                                                   waits=[tok_z, bg.get("dwc")])

                for tt in range(8):
                    for ki in range(8):
                        w0 = ([tok_z, t_bload] + bgw(2)) if ki == 0 else []
                        tmm = MM(pb[2][:, 0:8],
                                 xz[:, ki * TBH + tt * 128:ki * TBH + (tt + 1) * 128],
                                 gwtb[:, ki * 8:(ki + 1) * 8],
                                 start=(ki == 0), stop=(ki == 7), waits=w0,
                                 inc="te" if ki == 7 else None)
                    gl = V(lambda e: e.tensor_copy(g8t[:, :], pb[2][:, 0:8]),
                           waits=[tmm], inc="ve")
                    bg[2] = gl
                    s1 = V(lambda e: e.tensor_reduce(gc1[:, :], g8t[:, :], AX.X, ALU.max,
                                                     negate=True), waits=[gl], inc="ve")
                    s2 = V(lambda e: e.tensor_scalar(g8t[:, :], g8t[:, :], gc1[:, 0:1], None,
                                                     ALU.add), waits=[s1], inc="ve")
                    s3 = SC(lambda e: e.activation(g8t[:, :], g8t[:, :], AF.Exp),
                            waits=[s2], inc="se")
                    s4 = V(lambda e: e.tensor_reduce(gc1[:, :], g8t[:, :], AX.X, ALU.add),
                           waits=[s3], inc="ve")
                    s5 = V(lambda e: e.reciprocal(gc1[:, :], gc1[:, :]), waits=[s4], inc="ve")
                    s6 = V(lambda e: e.tensor_scalar(g8t[:, :], g8t[:, :], gc1[:, 0:1], None,
                                                     ALU.mult), waits=[s5], inc="ve")
                    s7 = V(lambda e: e.tensor_reduce(gc1[:, :], g8t[:, :], AX.X, ALU.max),
                           waits=[s6], inc="ve")
                    s8 = V(lambda e: e.tensor_scalar(tmpf[:, 0:8], g8t[:, :], gc1[:, 0:1],
                                                     -1e30, ALU.is_ge, ALU.mult),
                           waits=[s7], inc="ve")
                    s9 = V(lambda e: e.tensor_tensor(tmpf[:, 0:8], g8t[:, :], tmpf[:, 0:8],
                                                     ALU.add), waits=[s8], inc="ve")
                    sa_ = V(lambda e: e.tensor_reduce(gc2[:, :], tmpf[:, 0:8], AX.X, ALU.max),
                            waits=[s9], inc="ve")
                    sb_ = V(lambda e: e.tensor_scalar(tmpf[:, 0:8], g8t[:, :], gc2[:, 0:1],
                                                      None, ALU.is_ge), waits=[sa_], inc="ve")
                    sc_ = V(lambda e: e.tensor_tensor(gc1[:, :], gc1[:, :], gc2[:, :], ALU.add),
                            waits=[sa_], inc="ve")
                    sd_ = V(lambda e: e.tensor_scalar(gc1[:, :], gc1[:, :], 1e-20, None,
                                                      ALU.add), waits=[sc_], inc="ve")
                    se_ = V(lambda e: e.reciprocal(gc1[:, :], gc1[:, :]), waits=[sd_], inc="ve")
                    sf_ = V(lambda e: e.tensor_scalar(gc1[:, :], gc1[:, :], 2.5, None, ALU.mult),
                            waits=[se_], inc="ve")
                    sg_ = V(lambda e: e.tensor_tensor(tmpf[:, 0:8], g8t[:, :], tmpf[:, 0:8],
                                                      ALU.mult), waits=[sb_, sf_], inc="ve")
                    sh_ = V(lambda e, t=tt: e.tensor_scalar(combt[:, t * 8:(t + 1) * 8],
                                                            tmpf[:, 0:8], gc1[:, 0:1], None,
                                                            ALU.mult),
                            waits=[sg_, bg.get("dwc") if hb == 1 else None], inc="ve")
                    if DBG and hb == 0:
                        bg["dwc"] = DMA("sync", "dw",
                                        dram["d_gl8"].ap()[:, tt * 8:(tt + 1) * 8],
                                        combt[:, tt * 8:(tt + 1) * 8],
                                        waits=[sh_, bg.get("dwc")])
                    tmm = TR(pbt[0:8, 0:128], combt[:, tt * 8:(tt + 1) * 8], identb[:, :],
                             waits=[sh_] + bgw("t0"), inc="te")
                    ev = V(lambda e: e.tensor_copy(comb8[0:8, 0:128], pbt[0:8, 0:128]),
                           waits=[tmm, bg.get("c8")], inc="ve")
                    bg["t0"] = ev
                    bg["c8"] = DMA("sync", "crw",
                                   crow.ap().rearrange("o (r t) -> o r t", r=8)
                                   [:, :, tt * 128:(tt + 1) * 128],
                                   comb8.ap()[0:8, 0:128], waits=[ev])
                tok_comb = p.tok("ve")
                tok_crow = p.tok("crw")

                if BSTAGE < 2:
                    tok_zz = V(lambda e: e.memset(outb[:, :], 0.0), waits=[tok_comb], inc="ve")
                    for r in range(TBH // 128):
                        DMA("sync", "sw",
                            dram["out"].ap()[hb * TBH + r * 128:hb * TBH + (r + 1) * 128, :],
                            outb[:, :], waits=[tok_zz])
                    return

                def wsrc(e_, which):
                    if which == 0:
                        return dram["wg"].ap()[e_] if e_ < 8 else                             dram["swg"].ap()[:, (e_ - 8) * FF:(e_ - 7) * FF]
                    if which == 1:
                        return dram["wu"].ap()[e_] if e_ < 8 else                             dram["swu"].ap()[:, (e_ - 8) * FF:(e_ - 7) * FF]
                    return dram["wd"].ap()[e_] if e_ < 8 else                         dram["swd"].ap()[(e_ - 8) * FF:(e_ - 7) * FF, :]

                wguard = {"g": None, "u": None, "d": None}
                for e_ in range(10):
                    ec[0] += 1
                    fw = [tok_comb, p.tok("te"), p.tok("se")] if e_ == 0 else []
                    t_wg = DMA("gpsimd", "wgl",
                               wgb.ap().rearrange("p (c f) -> p c f", c=8),
                               wsrc(e_, 0).rearrange("(c p) f -> p c f", p=128),
                               waits=[wguard["g"]] + fw)
                    t_wu = DMA("gpsimd", "wul",
                               wub.ap().rearrange("p (c f) -> p c f", c=8),
                               wsrc(e_, 1).rearrange("(c p) f -> p c f", p=128),
                               waits=[wguard["u"]] + fw)
                    t_wd = DMA("gpsimd", "wdl",
                               wdb.ap().rearrange("p (c f) -> p c f", c=11),
                               wsrc(e_, 2).rearrange("(c p) f -> p c f", p=128),
                               waits=[wguard["d"]] + fw)
                    for sg in range(2):
                        if e_ < 8:
                            tmm = MM(pb[3][:, 0:512], onesb[:, :],
                                     crow[0:1, e_ * TBH + sg * 512:e_ * TBH + (sg + 1) * 512],
                                     start=True, stop=True,
                                     waits=[tok_comb, tok_crow] + bgw(3), inc="te")
                            cb = V(lambda e: e.tensor_copy(cbuf[:, 4 * 512:5 * 512],
                                                           pb[3][:, 0:512]),
                                   waits=[tmm], inc="ve")
                            bg[3] = cb
                        lastg = None
                        for fo in range(11):
                            gbank = [1, 0][fo % 2]
                            ubank = [2, 5][fo % 2]
                            for ki in range(8):
                                w0 = ([t_wg, tok_z] + bgw(gbank)) if ki == 0 else []
                                tmm = MM(pb[gbank][:, 0:512],
                                         wgb[:, ki * FF + fo * 128:ki * FF + (fo + 1) * 128],
                                         zb[:, ki * TBH + sg * 512:ki * TBH + (sg + 1) * 512],
                                         start=(ki == 0), stop=(ki == 7), waits=w0,
                                         inc="te" if ki == 7 else None)
                            if sg == 1 and fo == 10:
                                wguard["g"] = tmm
                            tsil = SC(lambda e, b=gbank, f=fo: e.activation(
                                gsil[:, (f % 2) * 256:(f % 2) * 256 + 512]
                                if False else gsil[:, 0:512],
                                pb[b][:, 0:512], AF.Silu),
                                waits=[tmm, bg.get("gsilr")], inc="se")
                            bg[gbank] = tsil
                            for ki in range(8):
                                w0 = ([t_wu] + bgw(ubank)) if ki == 0 else []
                                tmm = MM(pb[ubank][:, 0:512],
                                         wub[:, ki * FF + fo * 128:ki * FF + (fo + 1) * 128],
                                         zb[:, ki * TBH + sg * 512:ki * TBH + (sg + 1) * 512],
                                         start=(ki == 0), stop=(ki == 7), waits=w0,
                                         inc="te" if ki == 7 else None)
                            if sg == 1 and fo == 10:
                                wguard["u"] = tmm
                            tg1 = V(lambda e, b=ubank: e.tensor_tensor(
                                scr[:, 0:512], gsil[:, 0:512], pb[b][:, 0:512], ALU.mult),
                                waits=[tmm, tsil], inc="ve")
                            bg[ubank] = tg1
                            bg["gsilr"] = tg1
                            if e_ < 8:
                                lastg = V(lambda e, f=fo: e.tensor_tensor(
                                    gpr[:, f * 512:(f + 1) * 512], scr[:, 0:512],
                                    cbuf[:, 4 * 512:5 * 512], ALU.mult),
                                    waits=[tg1, cb], inc="ve")
                            else:
                                lastg = V(lambda e, f=fo: e.tensor_copy(
                                    gpr[:, f * 512:(f + 1) * 512], scr[:, 0:512]),
                                    waits=[tg1], inc="ve")
                        for dm in range(8):
                            bank = 3 + dm % 2
                            for ki in range(11):
                                w0 = ([t_wd, lastg] + bgw(bank)) if ki == 0 else []
                                tmm = MM(pb[bank][:, 0:512],
                                         wdb[:, ki * 1024 + dm * 128:ki * 1024 + (dm + 1) * 128],
                                         gpr[:, ki * 512:(ki + 1) * 512],
                                         start=(ki == 0), stop=(ki == 10), waits=w0,
                                         inc="te" if ki == 10 else None)
                            if sg == 1 and dm == 7:
                                wguard["d"] = tmm
                            ev = V(lambda e, d=dm, s=sg, b=bank: e.tensor_tensor(
                                xz[:, d * TBH + s * 512:d * TBH + (s + 1) * 512],
                                pb[b][:, 0:512],
                                xz[:, d * TBH + s * 512:d * TBH + (s + 1) * 512], ALU.add),
                                waits=[tmm], inc="ve")
                            bg[bank] = ev
                tok_ffn = p.tok("ve")

                for tt in range(8):
                    for mo8 in range(8):
                        w0 = ([tok_ffn] if (tt == 0 and mo8 == 0) else []) + bgw(5)
                        tmm = TR(pb[5][:, 0:128],
                                 xz[:, mo8 * TBH + tt * 128:mo8 * TBH + (tt + 1) * 128],
                                 identf[:, :], waits=w0, inc="te")
                        ev = V(lambda e, t=tt, m=mo8: e.tensor_copy(
                            x2t8[:, t * 1024 + m * 128:t * 1024 + (m + 1) * 128],
                            pb[5][:, 0:128]), waits=[tmm], inc="ve")
                        bg[5] = ev
                    r1 = V(lambda e, t=tt: e.tensor_reduce(
                        mucol[:, t:t + 1], x2t8[:, t * 1024:(t + 1) * 1024], AX.X, ALU.add),
                        inc="ve")
                    V(lambda e, t=tt: e.scalar_tensor_tensor(
                        x2sq[:, :], x2t8[:, t * 1024:(t + 1) * 1024], 1.0,
                        x2t8[:, t * 1024:(t + 1) * 1024], ALU.mult, ALU.mult,
                        accum_out=mucol[:, 8 + t:8 + t + 1]), waits=[r1], inc="ve")
                m0 = V(lambda e: e.tensor_scalar(mucol[:, :], mucol[:, :], 1.0 / 1024, None,
                                                 ALU.mult), inc="ve")
                m1 = V(lambda e: e.tensor_tensor(rcol[:, :], mucol[:, 0:8], mucol[:, 0:8],
                                                 ALU.mult), waits=[m0], inc="ve")
                m2_ = V(lambda e: e.tensor_tensor(rcol[:, :], mucol[:, 8:16], rcol[:, :],
                                                  ALU.subtract), waits=[m1], inc="ve")
                m3 = V(lambda e: e.tensor_scalar(rcol[:, :], rcol[:, :], 1e-5, None, ALU.add),
                       waits=[m2_], inc="ve")
                m4 = V(lambda e: e.reciprocal(rcol[:, :], rcol[:, :]), waits=[m3], inc="ve")
                m5 = SC(lambda e: e.activation(rcol[:, :], rcol[:, :], AF.Sqrt),
                        waits=[m4], inc="se")
                for tt in range(8):
                    ap_ = V(lambda e, t=tt: e.tensor_scalar(
                        outb[:, :], x2t8[:, t * 1024:(t + 1) * 1024],
                        mucol[:, t:t + 1], rcol[:, t:t + 1], ALU.subtract, ALU.mult),
                        waits=[m5, bg.get("outd")], inc="ve")
                    bg["outd"] = DMA(
                        "sync", "sw",
                        dram["out"].ap()[hb * TBH + tt * 128:hb * TBH + (tt + 1) * 128, :],
                        outb[:, :], waits=[ap_])

            for hb in range(NHB):
                b_half(hb)

        p.op("sync", lambda e: e.nop(),
             [p.tok("sw"), p.tok("gw"), p.tok("dw"), p.tok("cc")], [])

        block = ctx.enter_context(nc.Block())

        def make_runner(name):
            prog = p.bucket[name]

            def runner(eng):
                waited = {}
                for fn, waits, incs in prog:
                    need = {}
                    for s, v in waits:
                        if v > need.get(s, 0):
                            need[s] = v
                    for s, v in need.items():
                        if v > 0 and waited.get(s, 0) < v:
                            eng.wait_ge(semh[s], v)
                            waited[s] = v
                    inst = fn(eng)
                    for s, k in incs:
                        inst.then_inc(semh[s], k)
            return runner

        block.tensor(make_runner("tensor"))
        block.vector(make_runner("vector"))
        block.scalar(make_runner("scalar"))
        block.gpsimd(make_runner("gpsimd"))
        block.sync(make_runner("sync"))

    return nc


def prep_inputs(inp):
    f32 = np.float32
    x = np.asarray(inp["L0_attn_hidden_states"], f32)
    g = lambda n: np.asarray(inp[n], f32)
    Wr1 = g("L0_attn_Wr1")
    wr1p = np.zeros((1152, 2048), f32)
    wr1p[:1024] = Wr1[:1024]
    for h in range(H):
        for j in range(6):
            wr1p[1024 + h * 32 + j] = Wr1[1024 + j * 4 + h]
    consts = dict(
        identf=np.eye(128, dtype=f32),
        identb=np.eye(128, dtype=f32).astype(BF),
        eye32s=np.tile(np.eye(32, dtype=f32), (4, 1)).astype(BF),
        masksl=np.tile(-np.tril(np.ones((32, 32), f32), -1), (4, 1)).astype(BF),
        masktu=np.tile(np.triu(np.ones((32, 32), f32), 0), (4, 1)).astype(BF),
        onesf=np.ones((1, 128), f32),
        onesb=np.ones((1, 128), f32).astype(BF),
        colb1=np.ones((128, 1), f32).astype(BF),
        colb256=np.full((128, 1), 1.0 / 256, f32).astype(BF),
        colb1024=np.full((128, 1), 1.0 / 1024, f32).astype(BF),
    )
    shared = dict(
        wid=g("L0_attn_Wid"), bid=g("L0_attn_bid").reshape(H, 1),
        alpha=g("L0_attn_alpha_id").reshape(H, 1),
        wr1=wr1p, br1=np.ascontiguousarray(g("L0_attn_br1").reshape(16, 128).T),
        wr2=g("L0_attn_Wr2"), br2=g("L0_attn_br2").reshape(12, 1),
        ltg=g("L0_attn_log_tau_group").reshape(1, 2),
        lth=g("L0_attn_log_tau_head").reshape(1, 4),
        onwc=np.ascontiguousarray(g("L0_attn_o_norm_w").reshape(2, 128).T),
        wo=g("L0_attn_Wo"),
        gwt=np.ascontiguousarray(g("L0_ffn_gate_w").T),
        wg=g("L0_ffn_w_gate"), wu=g("L0_ffn_w_up"), wd=g("L0_ffn_w_down"),
        swg=g("L0_ffn_sw_gate"), swu=g("L0_ffn_sw_up"), swd=g("L0_ffn_sw_down"),
        **consts,
    )
    xT = [np.ascontiguousarray(x[b_].T) for b_ in range(B)]
    cores = []
    for c in range(8):
        b_, h_ = c // 4, c % 4
        cs, ce = h_ * DK, (h_ + 1) * DK
        d = dict(shared)
        d["x_fm"] = xT[b_]
        d["xb_fm"] = np.ascontiguousarray(xT[b_][:, h_ * TB:(h_ + 1) * TB])
        d["wq"] = np.ascontiguousarray(g("L0_attn_Wq")[:, cs:ce])
        d["wk"] = np.ascontiguousarray(g("L0_attn_Wk")[:, cs:ce])
        d["wv"] = np.ascontiguousarray(g("L0_attn_Wv")[:, cs:ce])
        d["wb"] = np.ascontiguousarray(g("L0_attn_Wb")[:, h_].reshape(8, 128).T)
        def diag_expand(w):
            K = w.shape[1]
            out = np.zeros((2 * K * 128, 128), f32)
            for mo in range(2):
                for j in range(K):
                    blk = (mo * K + j) * 128
                    np.fill_diagonal(out[blk:blk + 128], w[mo * 128:(mo + 1) * 128, j])
            return out
        d["convqd"] = diag_expand(g("L0_attn_conv_q")[cs:ce])
        d["convkd"] = diag_expand(g("L0_attn_conv_k")[cs:ce])
        d["convvd"] = diag_expand(g("L0_attn_conv_v")[cs:ce])
        d["firsd"] = diag_expand(g("L0_attn_fir_short")[h_])
        d["firld"] = diag_expand(g("L0_attn_fir_long")[h_])
        cores.append(d)
    return cores


_CACHE = {}


def kernel(**inputs):
    cores = prep_inputs(inputs)
    if "nc" not in _CACHE:
        nc = bass.Bass("TRN2", target_bir_lowering=False)
        _CACHE["nc"] = build(nc)
    nc = _CACHE["nc"]
    res = run_bass_kernel_spmd(nc, cores, core_ids=list(range(8)))
    out = np.zeros((B, L, D), np.float32)
    for c in range(8):
        b_, s_ = c // 4, c % 4
        out[b_, s_ * TB:(s_ + 1) * TB] = np.asarray(res.results[c]["out"], np.float32)
    kernel.results = res.results
    return out




# version 13
# speedup vs baseline: 1.1678x; 1.0056x over previous
import numpy as np
import ml_dtypes
import concourse.bass as bass
import concourse.mybir as mybir
from concourse.bass_utils import run_bass_kernel_spmd
from contextlib import ExitStack

F32 = mybir.dt.float32
BF16 = mybir.dt.bfloat16
ALU = mybir.AluOpType
AF = mybir.ActivationFunctionType
AX = mybir.AxisListType
BF = ml_dtypes.bfloat16

B, L, D, H = 2, 8192, 1024, 4
DK = DV = 256
CH = 32
NG = 4
GA = L // NG
NTT = GA // 128
NCHG = GA // CH
NGI = NCHG // 4
TB = L // 4
NHB = 2
TBH = TB // NHB
EPS_ID, R_EPS = 0.06, 0.025
NE, FF, FSH = 8, 1408, 2816
CCM = 1024
CCS = 6 * NTT

DBG = False
STAGE = 6
NGB = NG
B_STAGE = 2


class Plan:
    ENG = ("tensor", "vector", "scalar", "gpsimd", "sync")

    def __init__(self):
        self.bucket = {e: [] for e in self.ENG}
        self.sems = {}

    def op(self, eng, fn, waits=(), incs=()):
        for s, k in incs:
            self.sems[s] = self.sems.get(s, 0) + k
        self.bucket[eng].append((fn, tuple(w for w in waits if w is not None), tuple(incs)))

    def tok(self, s):
        return (s, self.sems.get(s, 0))


def sbpp(shape, dtype):
    n = 1
    for d in shape[1:]:
        n *= d
    return n * (2 if dtype == BF16 else 4)


class SB:
    def __init__(self, nc, base):
        self.nc, self.off = nc, base

    def al(self, name, shape, dtype):
        off = (self.off + 31) & ~31
        t = self.nc.alloc_sbuf_tensor_at(name, shape, dtype, offset=off)
        self.off = off + sbpp(shape, dtype)
        return t


def build(nc):
    p = Plan()
    dram = {}

    def din(name, shape, dt=F32):
        dram[name] = nc.dram_tensor(name, shape, dt, kind="ExternalInput")
        return dram[name]

    def dout(name, shape, dt=F32):
        dram[name] = nc.dram_tensor(name, shape, dt, kind="ExternalOutput")
        return dram[name]

    din("x_fm", [D, L])
    din("xb_fm", [D, TB])
    din("wq", [D, DK]); din("wk", [D, DK]); din("wv", [D, DV]); din("wb", [128, 8])
    din("convqd", [8 * 128, 128]); din("convkd", [8 * 128, 128]); din("convvd", [8 * 128, 128])
    din("firsd", [6 * 128, 128]); din("firld", [62 * 128, 128])
    din("wid", [D, H]); din("bid", [H, 1]); din("alpha", [H, 1])
    din("wr1", [1152, 2048]); din("br1", [128, 16]); din("wr2", [2048, 12]); din("br2", [12, 1])
    din("ltg", [1, 2]); din("lth", [1, 4]); din("onwc", [128, 2])
    din("wo", [D, D]); din("gwt", [D, NE])
    din("wg", [NE, D, FF]); din("wu", [NE, D, FF]); din("wd", [NE, FF, D])
    din("swg", [D, FSH]); din("swu", [D, FSH]); din("swd", [FSH, D])
    din("identf", [128, 128]); din("identb", [128, 128], BF16)
    din("eye32s", [128, 32], BF16)
    din("masksl", [128, 32], BF16)
    din("masktu", [128, 32], BF16)
    din("onesf", [1, 128]); din("onesb", [1, 128], BF16)
    din("colb1", [128, 1], BF16); din("colb256", [128, 1], BF16); din("colb1024", [128, 1], BF16)

    dout("out", [TB, D])
    dram["ccm_ia"] = nc.dram_tensor("ccm_ia", [NG * CCM, GA], BF16)
    dram["ccm_oa"] = nc.dram_tensor("ccm_oa", [NG * 4 * CCM, GA], BF16)
    dram["ccs_ia"] = nc.dram_tensor("ccs_ia", [128, NG * CCS], BF16)
    dram["ccs_oa"] = nc.dram_tensor("ccs_oa", [4 * 128, NG * CCS], BF16)

    if DBG:
        dout("d_qfm", [128, 2 * GA], BF16); dout("d_kfm", [128, 2 * GA], BF16)
        dout("d_vfm", [128, 2 * GA], BF16); dout("d_knfm", [128, 2 * GA], BF16)
        dout("d_fs", [128, 2 * GA], BF16); dout("d_fl", [128, 2 * GA], BF16)
        dout("d_otm", [128, NTT * 256], BF16); dout("d_dfm", [128, 2 * GA], BF16)
        dout("d_statc", [128, CCS], BF16); dout("d_scols", [128, NTT * 8])
        dout("d_zb", [128, 8 * TBH], BF16)
        dout("d_ptm", [128, 8 * 12]); dout("d_gl8", [128, 8 * 8], BF16)
        dout("d_ofm", [128, 8 * 512], BF16)

    with ExitStack() as ctx:
        pb = [ctx.enter_context(nc.psum_tensor(f"pb{i}", [128, 512], F32)) for i in range(6)]
        pbt = ctx.enter_context(nc.psum_tensor("pbt", [128, 512], BF16))
        pbt2 = ctx.enter_context(nc.psum_tensor("pbt2", [128, 512], BF16))
        semnames = ["gl", "sl", "sw", "gw", "dw", "cc", "te", "ve", "se",
                    "exl", "ex2", "prw", "crw", "wr1l", "wr1m", "wol", "wom",
                    "wgl", "wul", "wdl"]
        semh = {s: ctx.enter_context(nc.semaphore(s)) for s in semnames}

        base = nc.sbuf_base + 64
        P0 = SB(nc, base)
        identf = P0.al("identf", [128, 128], F32)
        identb = P0.al("identb", [128, 128], BF16)
        eye32s = P0.al("eye32s", [128, 32], BF16)
        masksl = P0.al("masksl", [128, 32], BF16)
        masktu = P0.al("masktu", [128, 32], BF16)
        onesf = P0.al("onesf", [1, 128], F32)
        onesb = P0.al("onesb", [1, 128], BF16)
        colb1 = P0.al("colb1", [128, 1], BF16)
        colb256 = P0.al("colb256", [128, 1], BF16)
        colb1024 = P0.al("colb1024", [128, 1], BF16)
        wqb = P0.al("wqb", [128, 8 * 256], BF16)
        wkb = P0.al("wkb", [128, 8 * 256], BF16)
        wvb = P0.al("wvb", [128, 8 * 256], BF16)
        wbb = P0.al("wbb", [128, 8], BF16)
        PERM_END = P0.off

        A = SB(nc, PERM_END)
        GAp, VAp = GA + 3, GA + 30
        convqd = A.al("convqd", [128, 8 * 128], BF16)
        convkd = A.al("convkd", [128, 8 * 128], BF16)
        convvd = A.al("convvd", [128, 8 * 128], BF16)
        firsd = A.al("firsd", [128, 6 * 128], BF16)
        firld = A.al("firld", [128, 62 * 128], BF16)
        axf_off = (A.off + 31) & ~31
        axf = A.al("axf", [128, 8 * GA], BF16)
        qraw_off = (A.off + 31) & ~31
        qraw = A.al("qraw", [128, 2 * GAp], BF16)
        kraw_off = (A.off + 31) & ~31
        kraw = A.al("kraw", [128, 2 * GAp], BF16)
        vraw_off = (A.off + 31) & ~31
        vraw = A.al("vraw", [128, 2 * GAp], BF16)
        qfm = A.al("qfm", [128, 2 * GA], BF16)
        kfm = A.al("kfm", [128, 2 * GA], BF16)
        vfm = A.al("vfm", [128, 2 * VAp], BF16)
        knfm = A.al("knfm", [128, 2 * GA], BF16)
        fsb = A.al("fsb", [128, 2 * GA], BF16)
        flb = A.al("flb", [128, 2 * GA], BF16)
        sqbuf = A.al("sqbuf", [128, GA], BF16)
        kntm = nc.alloc_sbuf_tensor_at("kntm", [128, NTT * 256], BF16, offset=qraw_off)
        kbntm = nc.alloc_sbuf_tensor_at("kbntm", [128, NTT * 256], BF16, offset=kraw_off)
        vbtm = nc.alloc_sbuf_tensor_at("vbtm", [128, NTT * 256], BF16, offset=vraw_off)
        ubuf = nc.alloc_sbuf_tensor_at("ubuf", [128, NTT * 256], BF16, offset=axf_off)
        wnfm = nc.alloc_sbuf_tensor_at("wnfm", [128, NCHG * 64], BF16, offset=axf_off + 8192)
        otm = nc.alloc_sbuf_tensor_at("otm", [128, NTT * 256], BF16, offset=axf_off + 16384)
        dfm = nc.alloc_sbuf_tensor_at("dfm", [128, 2 * GA], BF16, offset=axf_off + 24576)
        atm = A.al("atm", [128, NGI * 32], BF16)
        Tpk = A.al("Tpk", [128, NGI * 32], BF16)
        TTpk = A.al("TTpk", [128, NGI * 32], BF16)
        Apk = A.al("Apk", [128, NGI * 32], BF16)
        ATpk = A.al("ATpk", [128, NGI * 32], BF16)
        u2sb = A.al("u2sb", [128, 2 * 256], BF16)
        Sbf = A.al("Sbf", [128, 512], BF16)
        S32 = A.al("S32", [128, 512], F32)
        halobuf = A.al("halobuf", [128, 3 * 6], BF16)
        srows = A.al("srows", [8, GA], F32)
        tmpr = A.al("tmpr", [1, GA], F32)
        rkrow = A.al("rkrow", [1, GA], F32)
        rkb = A.al("rkb", [128, 512], F32)
        scols = A.al("scols", [128, NTT * 8], F32)
        rqc = A.al("rqc", [128, NTT], F32)
        statc = A.al("statc", [128, CCS], BF16)
        dmcol = A.al("dmcol", [128, 2 * NTT], F32)
        stdw = A.al("stdw", [128, NTT], F32)
        assert A.off < nc.sbuf_top, f"A region overflow {A.off} > {nc.sbuf_top}"

        Br = SB(nc, PERM_END)
        xz = Br.al("xz", [128, 8 * TBH], F32)
        zb = Br.al("zb", [128, 8 * TBH], BF16)
        wr2b = Br.al("wr2b", [128, 16 * 12], BF16)
        widb = Br.al("widb", [128, 8 * 4], BF16)
        gwtb = Br.al("gwtb", [128, 8 * 8], F32)
        br1c = Br.al("br1c", [128, 16], F32)
        br2c = Br.al("br2c", [12, 1], F32)
        bidc = Br.al("bidc", [4, 1], F32)
        alphac = Br.al("alphac", [4, 1], F32)
        sigac = Br.al("sigac", [4, 1], F32)
        onwcb = Br.al("onwcb", [128, 2], F32)
        ltgr = Br.al("ltgr", [1, 2], F32)
        lthr = Br.al("lthr", [1, 4], F32)
        tau12 = Br.al("tau12", [1, 12], F32)
        tau12r = Br.al("tau12r", [1, 12], F32)
        taub = Br.al("taub", [128, 12], F32)
        logfm = Br.al("logfm", [12, TBH], F32)
        ltm = Br.al("ltm", [128, 8 * 12], F32)
        ptm = Br.al("ptm", [128, 8 * 12], F32)
        p4a = Br.al("p4a", [128, 4], F32)
        pfm12 = Br.al("pfm12", [12, 128], BF16)
        prow_off = (Br.off + 31) & ~31
        prow = Br.al("prow", [1, 12 * 512], BF16)
        irow = Br.al("irow", [1, 4 * 512], BF16)
        idt4 = Br.al("idt4", [4, 512], F32)
        idb4 = Br.al("idb4", [4, 512], BF16)
        combt = Br.al("combt", [128, 8 * 8], BF16)
        comb8 = Br.al("comb8", [8, 128], BF16)
        crow = nc.alloc_sbuf_tensor_at("crow", [1, 8 * TBH], BF16, offset=prow_off)
        g8t = Br.al("g8t", [128, 8], F32)
        gc1 = Br.al("gc1", [128, 1], F32)
        gc2 = Br.al("gc2", [128, 1], F32)
        lnrow = Br.al("lnrow", [1, 2048], F32)
        rmsrow = lnrow
        rbc = Br.al("rbc", [128, 512], F32)
        mrbc = Br.al("mrbc", [128, 512], F32)
        tmpf = Br.al("tmpf", [128, 512], F32)
        obuf = Br.al("obuf", [128, 2 * 512], BF16)
        scr = Br.al("scr", [128, 512], BF16)
        gsil = Br.al("gsil", [128, 512], BF16)
        cbuf = Br.al("cbuf", [128, 5 * 512], BF16)
        ofm = Br.al("ofm", [128, 8 * 512], BF16)
        sstat = Br.al("sstat", [128, 4 * 48], BF16)
        rinstat = Br.al("rinstat", [128, TBH], BF16)
        mucol = Br.al("mucol", [128, 16], F32)
        rcol = Br.al("rcol", [128, 8], F32)
        x2sq = Br.al("x2sq", [128, 1024], F32)
        outb = Br.al("outb", [128, 1024], F32)
        outb2 = x2sq
        SREG = (Br.off + 31) & ~31
        bxf = nc.alloc_sbuf_tensor_at("bxf", [128, 8 * TBH], BF16, offset=SREG)
        hmid = nc.alloc_sbuf_tensor_at("hmid", [128, 16 * 512], BF16, offset=SREG + 16384)
        exch = nc.alloc_sbuf_tensor_at("exch", [128, 8 * 512], BF16, offset=SREG + 16384)
        exch2 = nc.alloc_sbuf_tensor_at("exch2", [128, 8 * 512], BF16, offset=SREG + 24576)
        wr1b = nc.alloc_sbuf_tensor_at("wr1b", [128, 2 * 9 * 128], BF16, offset=SREG + 32768)
        wob = nc.alloc_sbuf_tensor_at("wob", [128, 2 * 8 * 128], BF16, offset=SREG + 37504)
        wgb = nc.alloc_sbuf_tensor_at("wgb", [128, 8 * FF], BF16, offset=SREG)
        wub = nc.alloc_sbuf_tensor_at("wub", [128, 8 * FF], BF16, offset=SREG + 22528)
        wdb = nc.alloc_sbuf_tensor_at("wdb", [128, 11 * 1024], BF16, offset=SREG + 45056)
        gpr = nc.alloc_sbuf_tensor_at("gpr", [128, 11 * 512], BF16, offset=SREG + 67584)
        x2t8 = nc.alloc_sbuf_tensor_at("x2t8", [128, 8 * 1024], F32, offset=SREG)
        assert SREG + 78848 < nc.sbuf_top, f"B overflow {SREG + 78848}"

        def DMA(eng, sem, out, in_, waits=()):
            p.op(eng, lambda e, o=out, i=in_: e.dma_start(out=o, in_=i), waits, [(sem, 16)])
            return p.tok(sem)

        def MM(out, lhsT, rhs, start, stop, tp=None, waits=(), inc=None):
            p.op("tensor",
                 lambda e, o=out, l=lhsT, r=rhs, s1=start, s2=stop, t=tp:
                     e.matmul(o, l, r, start=s1, stop=s2, tile_position=t),
                 waits, [(inc, 1)] if inc else [])
            return p.tok("te") if inc else None

        def TR(out, in_, ident, waits=(), inc=None):
            p.op("tensor",
                 lambda e, o=out, i=in_, d=ident: e.transpose(o, i, d),
                 waits, [(inc, 1)] if inc else [])
            return p.tok("te") if inc else None

        def V(fn, waits=(), inc=None):
            p.op("vector", fn, list(waits) + [p.tok("ve")], [("ve", 1)])
            return p.tok("ve")

        def SC(fn, waits=(), inc=None):
            p.op("scalar", fn, list(waits) + [p.tok("se")], [("se", 1)])
            return p.tok("se")

        bg = {}

        def bgw(key):
            return [bg.get(key)]

        import dataclasses as _dc
        _scoff = {}

        def _scoff_get(e):
            if id(e) not in _scoff:
                _scoff[id(e)] = e.partition_id() % 4
            return _scoff[id(e)]

        def dynoff(e, ap, elems_per_token):
            return _dc.replace(ap, offset=ap.offset + _scoff_get(e) * (TB * elems_per_token))

        x_fm = dram["x_fm"]

        for nm, buf in [("identf", identf), ("onesf", onesf)]:
            DMA("sync", "sl", buf[:, :], dram[nm].ap())
        tok_const_sl = p.tok("sl")
        for nm, buf in [("identb", identb), ("eye32s", eye32s), ("masksl", masksl),
                        ("masktu", masktu), ("onesb", onesb), ("colb1", colb1),
                        ("colb256", colb256), ("colb1024", colb1024)]:
            DMA("gpsimd", "gl", buf[:, :], dram[nm].ap())
        for nm, buf in [("wq", wqb), ("wk", wkb), ("wv", wvb)]:
            w = dram[nm].ap().rearrange("(c p) n -> p c n", p=128)
            DMA("gpsimd", "gl", buf.ap().rearrange("p (c n) -> p c n", c=8), w)
        DMA("gpsimd", "gl", wbb[:, :], dram["wb"].ap())
        for nm, buf, cc in [("convqd", convqd, 8), ("convkd", convkd, 8),
                            ("convvd", convvd, 8), ("firsd", firsd, 6), ("firld", firld, 62)]:
            DMA("gpsimd", "gl", buf.ap().rearrange("p (c k) -> p c k", c=cc),
                dram[nm].ap().rearrange("(c p) k -> p c k", p=128))
        tok_const_gl = p.tok("gl")

        V(lambda e: e.memset(qraw[:, 0:3], 0.0))
        V(lambda e: e.memset(qraw[:, GAp:GAp + 3], 0.0))
        V(lambda e: e.memset(kraw[:, 0:3], 0.0))
        V(lambda e: e.memset(kraw[:, GAp:GAp + 3], 0.0))
        V(lambda e: e.memset(vraw[:, 0:3], 0.0))
        V(lambda e: e.memset(vraw[:, GAp:GAp + 3], 0.0))
        V(lambda e: e.memset(vfm[:, 0:30], 0.0))
        V(lambda e: e.memset(vfm[:, VAp:VAp + 30], 0.0))
        V(lambda e: e.memset(srows[0:8, :], 0.0))
        V(lambda e: e.memset(S32[:, :], 0.0))
        V(lambda e: e.memset(Sbf[:, :], 0.0), inc="ve")

        def a_group(g, bar):
            t0 = g * GA
            p.op("tensor", lambda e: e.nop(), bar, [("te", 1)])
            p.op("vector", lambda e: e.engine_nop(), bar, [("ve", 1)])
            p.op("scalar", lambda e: e.nop(), bar, [("se", 1)])
            p.op("gpsimd", lambda e: e.engine_nop(), bar, [])
            p.op("sync", lambda e: e.nop(), bar, [])
            barve, barse = p.tok("ve"), p.tok("se")

            def fin():
                if DBG and g == 0:
                    wv, ws = p.tok("ve"), p.tok("se")
                    dumps = [("d_qfm", qfm, 2 * GA, 1), ("d_kfm", kfm, 2 * GA, 1),
                             ("d_fs", fsb, 2 * GA, 2), ("d_fl", flb, 2 * GA, 2),
                             ("d_knfm", knfm, 2 * GA, 3), ("d_scols", scols, NTT * 8, 3),
                             ("d_otm", otm, NTT * 256, 5), ("d_dfm", dfm, 2 * GA, 6),
                             ("d_statc", statc, CCS, 6)]
                    for nm, src, w, st in dumps:
                        if STAGE >= st:
                            DMA("sync", "dw", dram[nm].ap(), src[:, 0:w], waits=[wv, ws])
                    if STAGE >= 1:
                        DMA("sync", "dw", dram["d_vfm"].ap(),
                            vfm.ap().rearrange("p (c t) -> p c t", c=2)[:, :, 30:30 + GA],
                            waits=[wv, ws])
                return [p.tok("te"), p.tok("ve"), p.tok("se"), p.tok("sl"), p.tok("gl"),
                        p.tok("sw"), p.tok("gw"), p.tok("dw")]

            xin = x_fm.ap().rearrange("(c p) t -> p c t", p=128)[:, :, t0:t0 + GA]
            xout = axf.ap().rearrange("p (c t) -> p c t", c=8)
            tok_x = DMA("gpsimd", "gl", xout, xin, waits=[tok_const_gl])
            if g > 0 and STAGE >= 6:
                gp = g - 1
                p.op("gpsimd", lambda e, g_=gp: e.collective_compute(
                    "AllGather", ALU.bypass, replica_groups=[[0, 1, 2, 3], [4, 5, 6, 7]],
                    ins=[dram["ccm_ia"].ap()[g_ * CCM:(g_ + 1) * CCM, :].opt()],
                    outs=[dram["ccm_oa"].ap()[g_ * 4 * CCM:(g_ + 1) * 4 * CCM, :].opt()]),
                    [exch_sw_last[gp]], [("cc", 1)])

            if g > 0:
                for bi, buf in enumerate([qraw, kraw, vraw]):
                    V(lambda e, b=buf, k=bi: e.tensor_copy(
                        b.ap().rearrange("p (c t) -> p c t", c=2)[:, :, 0:3],
                        halobuf.ap().rearrange("p (a c t) -> p a c t", a=3, c=2)[:, k]))
                V(lambda e: e.tensor_copy(
                    vfm.ap().rearrange("p (c t) -> p c t", c=2)[:, :, 0:30],
                    vfm.ap().rearrange("p (c t) -> p c t", c=2)[:, :, VAp - 30:VAp]))
            tok_halo = V(lambda e: e.engine_nop(), inc="ve")

            prtoks = {}
            for pi, (wb_, raw) in enumerate([(wqb, qraw), (wkb, kraw), (wvb, vraw)]):
                for mo in range(2):
                    for n in range(4):
                        bank = 1 + (pi * 8 + mo * 4 + n) % 2
                        for ki in range(8):
                            w0 = ([tok_x, tok_const_gl, barve] + bgw(bank)) if ki == 0 else []
                            tmm = MM(pb[bank][:, 0:512],
                                     wb_[:, ki * 256 + mo * 128:ki * 256 + (mo + 1) * 128],
                                     axf[:, ki * GA + n * 512:ki * GA + (n + 1) * 512],
                                     start=(ki == 0), stop=(ki == 7), waits=w0,
                                     inc="te" if ki == 7 else None)
                        dst = raw[:, mo * GAp + 3 + n * 512:mo * GAp + 3 + (n + 1) * 512]
                        ev = V(lambda e, d=dst, b=bank: e.tensor_copy(d, pb[b][:, 0:512]),
                               waits=[tmm], inc="ve")
                        prtoks[(pi, mo, n)] = ev
                        bg[bank] = ev
            for n in range(4):
                for ki in range(8):
                    w0 = ([tok_x] + bgw(3)) if ki == 0 else []
                    tmm = MM(pb[3][0:1, 0:512], wbb[:, ki:ki + 1],
                             axf[:, ki * GA + n * 512:ki * GA + (n + 1) * 512],
                             start=(ki == 0), stop=(ki == 7), waits=w0,
                             inc="te" if ki == 7 else None)
                ev = V(lambda e, n_=n: e.tensor_copy(tmpr[0:1, n_ * 512:(n_ + 1) * 512],
                                                     pb[3][0:1, 0:512]),
                       waits=[tmm], inc="ve")
                bg[3] = ev
            tok_beta = SC(lambda e: e.activation(tmpr[0:1, 0:GA], tmpr[0:1, 0:GA], AF.Sigmoid),
                          waits=[bg[3], barse], inc="se")
            tok_srow_b = DMA("sync", "sl", srows[1:2, 0:GA], tmpr[0:1, 0:GA], waits=[tok_beta])

            if g < NG - 1:
                for bi, buf in enumerate([qraw, kraw, vraw]):
                    V(lambda e, b=buf, k=bi: e.tensor_copy(
                        halobuf.ap().rearrange("p (a c t) -> p a c t", a=3, c=2)[:, k],
                        b.ap().rearrange("p (c t) -> p c t", c=2)[:, :, GAp - 3:GAp]))

            silu_toks = {}
            for pi, (raw, cwd, dst) in enumerate([(qraw, convqd, qfm), (kraw, convkd, kfm),
                                                  (vraw, convvd, vfm)]):
                for mo in range(2):
                    for n in range(4):
                        bank = 1 + (mo * 4 + n) % 2
                        for j in range(4):
                            w0 = ([prtoks[(pi, mo, nn)] for nn in range(4)] + [tok_halo]
                                  + bgw(bank)) if j == 0 else []
                            tmm = MM(pb[bank][:, 0:512],
                                     cwd[:, (mo * 4 + j) * 128:(mo * 4 + j + 1) * 128],
                                     raw[:, mo * GAp + j + n * 512:mo * GAp + j + n * 512 + 512],
                                     start=(j == 0), stop=(j == 3), waits=w0,
                                     inc="te" if j == 3 else None)
                        if pi == 2:
                            od = dst[:, mo * VAp + 30 + n * 512:mo * VAp + 30 + (n + 1) * 512]
                        else:
                            od = dst[:, mo * GA + n * 512:mo * GA + (n + 1) * 512]
                        ev = SC(lambda e, o=od, b=bank: e.activation(o, pb[b][:, 0:512],
                                                                    AF.Silu),
                                waits=[tmm], inc="se")
                        bg[bank] = ev
                    silu_toks[pi * 2 + mo] = p.tok("se")

            if STAGE <= 1:
                return fin()

            fir_ev = {}
            for si, (dstb, wd, ntap) in enumerate([(fsb, firsd, 3), (flb, firld, 31)]):
                for mo in range(2):
                    for n in range(4):
                        bank = 1 + (mo * 4 + n) % 2
                        for j in range(ntap):
                            off = mo * VAp + (30 - ntap + 1) + j + n * 512
                            w0 = ([silu_toks[4 + mo]] + bgw(bank)) if j == 0 else []
                            tmm = MM(pb[bank][:, 0:512],
                                     wd[:, (mo * ntap + j) * 128:(mo * ntap + j + 1) * 128],
                                     vfm[:, off:off + 512],
                                     start=(j == 0), stop=(j == ntap - 1), waits=w0,
                                     inc="te" if j == ntap - 1 else None)
                        ev = V(lambda e, d=dstb, m=mo, n_=n, b=bank: e.tensor_copy(
                            d[:, m * GA + n_ * 512:m * GA + (n_ + 1) * 512], pb[b][:, 0:512]),
                            waits=[tmm], inc="ve")
                        bg[bank] = ev
                fir_ev[si] = p.tok("ve")
            fir_toks = [(fir_ev[0], fir_ev[1]), (fir_ev[0], fir_ev[1])]

            if STAGE <= 2:
                return fin()

            sq_toks = {}
            for ti, src in enumerate([qfm, kfm]):
                for mo in range(2):
                    sq_toks[(ti, mo)] = SC(
                        lambda e, s=src, m=mo: e.square(sqbuf[:, :], s[:, m * GA:(m + 1) * GA]),
                        waits=[silu_toks[ti * 2 + mo], p.tok("te")], inc="se")
                    dstrow = srows if ti == 0 else rkrow
                    for n in range(4):
                        bank = [4, 5, 1, 2][n]
                        w0 = [sq_toks[(ti, mo)]] + (bgw(bank) if mo == 0 else [])
                        tmm = MM(pb[bank][0:1, 0:512], colb1[:, :],
                                 sqbuf[:, n * 512:(n + 1) * 512],
                                 start=(mo == 0), stop=(mo == 1), waits=w0, inc="te")
                        if mo == 1:
                            ev = V(lambda e, n_=n, b=bank, dr=dstrow:
                                   e.tensor_copy(dr[0:1, n_ * 512:(n_ + 1) * 512],
                                                 pb[b][0:1, 0:512]),
                                   waits=[tmm], inc="ve")
                            bg[bank] = ev
            tok_ssq = p.tok("ve")

            if STAGE <= 2.2:
                return fin()

            t_eps = V(lambda e: e.tensor_scalar(rkrow[0:1, :], rkrow[0:1, :], 1e-6, None,
                                                ALU.add), waits=[tok_ssq], inc="ve")
            tok_rk1 = V(lambda e: e.reciprocal(rkrow[0:1, :], rkrow[0:1, :]),
                        waits=[t_eps], inc="ve")
            tok_rk = SC(lambda e: e.activation(rkrow[0:1, :], rkrow[0:1, :], AF.Sqrt),
                        waits=[tok_rk1], inc="se")

            if STAGE <= 2.3:
                return fin()

            prev_tmpr = tok_srow_b
            stat_dma = []
            for si, src in enumerate([fsb, flb]):
                ftk = [fir_toks[0][si], fir_toks[1][si]]
                for qty in range(2):
                    row = 2 + 2 * si + qty
                    evs = []
                    tsq = None
                    for mo in range(2):
                        if qty == 1:
                            tsq = SC(lambda e, s=src, m=mo: e.square(sqbuf[:, :],
                                                                     s[:, m * GA:(m + 1) * GA]),
                                     waits=ftk + [p.tok("te")], inc="se")
                        for n in range(4):
                            bank = [4, 5, 1, 2][n]
                            if qty == 0:
                                rhs = src[:, mo * GA + n * 512:mo * GA + (n + 1) * 512]
                                w0 = (ftk + bgw(bank)) if mo == 0 else []
                            else:
                                rhs = sqbuf[:, n * 512:(n + 1) * 512]
                                w0 = [tsq] + (bgw(bank) if mo == 0 else [])
                            tmm = MM(pb[bank][0:1, 0:512], colb256[:, :], rhs,
                                     start=(mo == 0), stop=(mo == 1), waits=w0, inc="te")
                            if mo == 1:
                                ev = V(lambda e, n_=n, b=bank: e.tensor_copy(
                                    tmpr[0:1, n_ * 512:(n_ + 1) * 512], pb[b][0:1, 0:512]),
                                    waits=[tmm, prev_tmpr], inc="ve")
                                bg[bank] = ev
                                evs.append(ev)
                    prev_tmpr = DMA("sync", "sl", srows[row:row + 1, 0:GA], tmpr[0:1, 0:GA],
                                    waits=evs)
                    stat_dma.append(prev_tmpr)

            if STAGE <= 2.4:
                return fin()

            tok_knfm = {}
            for n in range(4):
                tmm = MM(pb[3][:, 0:512], onesf[:, :], rkrow[0:1, n * 512:(n + 1) * 512],
                         start=True, stop=True, waits=[tok_rk] + bgw(3), inc="te")
                tok_rkb = V(lambda e: e.tensor_copy(rkb[:, :], pb[3][:, 0:512]),
                            waits=[tmm], inc="ve")
                bg[3] = tok_rkb
                for mo in range(2):
                    tok_knfm[(mo, n)] = V(
                        lambda e, m=mo, n_=n: e.tensor_tensor(
                            knfm[:, m * GA + n_ * 512:m * GA + (n_ + 1) * 512],
                            kfm[:, m * GA + n_ * 512:m * GA + (n_ + 1) * 512],
                            rkb[:, :], ALU.mult),
                        waits=[tok_rkb], inc="ve")

            if STAGE <= 2.5:
                return fin()

            tok_scol = {}
            for tt in range(NTT):
                w0 = ([tok_srow_b, tok_ssq] + stat_dma) if tt == 0 else []
                tmm = TR(pb[2][:, 0:8], srows[0:8, tt * 128:(tt + 1) * 128], identf[0:8, 0:8],
                         waits=w0 + bgw(2), inc="te")
                ev = V(lambda e, t=tt: e.tensor_copy(scols[:, t * 8:(t + 1) * 8], pb[2][:, 0:8]),
                       waits=[tmm], inc="ve")
                tok_scol[tt] = ev
                bg[2] = ev
            sc8 = scols.ap().rearrange("p (t c) -> p t c", c=8)
            tok_scall = p.tok("ve")
            t_eps2 = V(lambda e: e.tensor_scalar(rqc[:, :], sc8[:, :, 0], 1e-6, None, ALU.add),
                       waits=[tok_scall], inc="ve")
            V(lambda e: e.reciprocal(rqc[:, :], rqc[:, :]), waits=[t_eps2], inc="ve")
            tok_rqc = SC(lambda e: e.activation(rqc[:, :], rqc[:, :], AF.Sqrt),
                         waits=[p.tok("ve")], inc="se")

            st6 = statc.ap().rearrange("p (t c) -> p t c", c=6)
            for si in range(2):
                mcol = sc8[:, :, 2 + 2 * si]
                qcol = sc8[:, :, 3 + 2 * si]
                V(lambda e, m=mcol, s=si: e.tensor_copy(st6[:, :, 2 * s], m), waits=[tok_scall])
                t1 = V(lambda e, m=mcol: e.tensor_tensor(stdw[:, :], m, m, ALU.mult), inc="ve")
                t2 = V(lambda e, q=qcol: e.tensor_tensor(stdw[:, :], q, stdw[:, :], ALU.subtract),
                       waits=[t1], inc="ve")
                V(lambda e: e.tensor_scalar(stdw[:, :], stdw[:, :], 0.0, None, ALU.max),
                  waits=[t2], inc="ve")
                tv = SC(lambda e: e.activation(stdw[:, :], stdw[:, :], AF.Sqrt),
                        waits=[p.tok("ve")], inc="se")
                V(lambda e, s=si: e.tensor_copy(st6[:, :, 2 * s + 1], stdw[:, :]),
                  waits=[tv], inc="ve")
            tok_statfl = p.tok("ve")

            if STAGE <= 2.7:
                return fin()

            for tt in range(NTT):
                bcol = scols[:, tt * 8 + 1:tt * 8 + 2]
                for mo in range(2):
                    w0 = [tok_knfm[(m, n)] for m in range(2) for n in range(4)] \
                        if (tt == 0 and mo == 0) else []
                    tmm = TR(pbt[:, 0:128], knfm[:, mo * GA + tt * 128:mo * GA + (tt + 1) * 128],
                             identb[:, :], waits=w0 + bgw("t0"), inc="te")
                    V(lambda e, t=tt, m=mo: e.tensor_copy(
                        kntm[:, t * 256 + m * 128:t * 256 + (m + 1) * 128], pbt[:, 0:128]),
                      waits=[tmm], inc="ve")
                    ev = V(lambda e, t=tt, m=mo, b=bcol: e.tensor_scalar(
                        kbntm[:, t * 256 + m * 128:t * 256 + (m + 1) * 128], pbt[:, 0:128],
                        b, -1.0, ALU.mult, ALU.mult),
                        waits=[tok_scol[tt]], inc="ve")
                    bg["t0"] = ev
                    tmm2 = TR(pbt2[:, 0:128],
                              vfm[:, mo * VAp + 30 + tt * 128:mo * VAp + 30 + (tt + 1) * 128],
                              identb[:, :],
                              waits=([silu_toks[4 + mo]] if tt == 0 else []) + bgw("t1"),
                              inc="te")
                    ev2 = V(lambda e, t=tt, m=mo, b=bcol: e.tensor_scalar(
                        vbtm[:, t * 256 + m * 128:t * 256 + (m + 1) * 128], pbt2[:, 0:128],
                        b, None, ALU.mult),
                        waits=[tmm2, tok_scol[tt]], inc="ve")
                    bg["t1"] = ev2
            tok_tm_all = p.tok("ve")

            if STAGE <= 3:
                return fin()

            for gi in range(NGI):
                for ci in range(4):
                    i = gi * 4 + ci
                    for mo in range(2):
                        w0 = (bgw(4) + [tok_knfm[(m, n)] for m in range(2) for n in range(4)]) \
                            if (gi == 0 and ci == 0 and mo == 0) else []
                        tmm = MM(pb[4][32 * ci:32 * ci + 32, gi * 32:(gi + 1) * 32],
                                 knfm[:, mo * GA + i * 32:mo * GA + (i + 1) * 32],
                                 knfm[:, mo * GA + i * 32:mo * GA + (i + 1) * 32],
                                 start=(mo == 0), stop=(mo == 1), tp=(0, 32 * ci),
                                 waits=w0,
                                 inc="te" if (gi == NGI - 1 and ci == 3 and mo == 1) else None)
            tok_Amm = tmm
            evA = []
            for gi in range(NGI):
                bcol = scols[:, gi * 8 + 1:gi * 8 + 2]
                evA.append(V(lambda e, g_=gi, b=bcol: e.scalar_tensor_tensor(
                    Apk[:, g_ * 32:(g_ + 1) * 32], pb[4][:, g_ * 32:(g_ + 1) * 32], b,
                    masksl[:, :], ALU.mult, ALU.mult),
                    waits=[tok_Amm, tok_scol[gi]], inc="ve"))
            bg[4] = evA[-1]
            eyeb = eye32s.ap().rearrange("p (o c) -> p o c", o=1).broadcast_to([128, NGI, 32])
            tok_T = V(lambda e: e.tensor_tensor(
                Tpk.ap().rearrange("p (g c) -> p g c", g=NGI),
                Apk.ap().rearrange("p (g c) -> p g c", g=NGI), eyeb, ALU.add),
                waits=evA, inc="ve")
            for lvl in range(4):
                tok_at = V(lambda e: e.transpose(ATpk[:, :], Apk[:, :]), waits=[tok_T], inc="ve")
                for gi in range(NGI):
                    for ci in range(4):
                        w0 = ([tok_at] + bgw(4)) if (gi == 0 and ci == 0) else []
                        tmm = MM(pb[4][32 * ci:32 * ci + 32, gi * 32:(gi + 1) * 32],
                                 ATpk[32 * ci:32 * ci + 32, gi * 32:(gi + 1) * 32],
                                 Apk[32 * ci:32 * ci + 32, gi * 32:(gi + 1) * 32],
                                 start=True, stop=True, tp=(32 * ci, 32 * ci),
                                 waits=w0, inc="te" if (gi == NGI - 1 and ci == 3) else None)
                tok_A2 = V(lambda e: e.tensor_copy(Apk[:, :], pb[4][:, :]), waits=[tmm], inc="ve")
                bg[4] = tok_A2
                tok_a2t = V(lambda e: e.transpose(ATpk[:, :], Apk[:, :]), waits=[tok_A2], inc="ve")
                for gi in range(NGI):
                    for ci in range(4):
                        w0 = ([tok_a2t] + bgw(5)) if (gi == 0 and ci == 0) else []
                        tmm = MM(pb[5][32 * ci:32 * ci + 32, gi * 32:(gi + 1) * 32],
                                 ATpk[32 * ci:32 * ci + 32, gi * 32:(gi + 1) * 32],
                                 Tpk[32 * ci:32 * ci + 32, gi * 32:(gi + 1) * 32],
                                 start=True, stop=True, tp=(32 * ci, 32 * ci),
                                 waits=w0, inc="te" if (gi == NGI - 1 and ci == 3) else None)
                tok_T = V(lambda e: e.scalar_tensor_tensor(Tpk[:, :], pb[5][:, :], 1.0, Tpk[:, :],
                                                           ALU.mult, ALU.add),
                          waits=[tmm], inc="ve")
                bg[5] = tok_T
            tok_TT = V(lambda e: e.transpose(TTpk[:, :], Tpk[:, :]), waits=[tok_T], inc="ve")

            for gi in range(NGI):
                for ci in range(4):
                    i = gi * 4 + ci
                    w0 = ([tok_TT, tok_tm_all] + bgw(5) + bgw(3)) if ci == 0 else []
                    MM(pb[5][32 * ci:32 * ci + 32, 0:256],
                       TTpk[32 * ci:32 * ci + 32, gi * 32:(gi + 1) * 32],
                       vbtm[32 * ci:32 * ci + 32, gi * 256:(gi + 1) * 256],
                       start=True, stop=True, tp=(32 * ci, 32 * ci), waits=w0)
                    for mo in range(2):
                        MM(pb[3][32 * ci:32 * ci + 32, 256:288],
                           knfm[:, mo * GA + i * 32:mo * GA + (i + 1) * 32],
                           qfm[:, mo * GA + i * 32:mo * GA + (i + 1) * 32],
                           start=(mo == 0), stop=(mo == 1), tp=(0, 32 * ci))
                    for mo in range(2):
                        tmm = MM(pb[3][:, (ci * 2 + mo) * 32:(ci * 2 + mo + 1) * 32],
                                 kbntm[32 * ci:32 * ci + 32,
                                       gi * 256 + mo * 128:gi * 256 + (mo + 1) * 128],
                                 TTpk[32 * ci:32 * ci + 32, gi * 32:(gi + 1) * 32],
                                 start=True, stop=True, tp=(32 * ci, 0),
                                 inc="te" if (ci == 3 and mo == 1) else None)
                e1 = V(lambda e, g_=gi: e.tensor_copy(ubuf[:, g_ * 256:(g_ + 1) * 256],
                                                      pb[5][:, 0:256]), waits=[tmm], inc="ve")
                V(lambda e, g_=gi: e.tensor_tensor(atm[:, g_ * 32:(g_ + 1) * 32],
                                                   pb[3][:, 256:288], masktu[:, :], ALU.mult),
                  inc="ve")
                e3 = V(lambda e, g_=gi: e.tensor_copy(wnfm[:, g_ * 256:(g_ + 1) * 256],
                                                      pb[3][:, 0:256]), inc="ve")
                bg[5], bg[3] = e1, e3
            tok_batch = p.tok("ve")

            if STAGE <= 4:
                return fin()

            prev_sevac = tok_batch
            prev_oevac = {0: None, 1: None, 2: None, 3: None}
            for i in range(NCHG):
                gi, ci = i // 4, i % 4
                gchunk = g * NCHG + i
                slot = (i // 4) % 2
                first = (gchunk == 0)
                w0 = [tok_batch, prev_sevac] + bgw("u2" + str(slot))
                t_u2mm = MM(pb[5][32 * ci:32 * ci + 32, 256:512],
                            eye32s[32 * ci:32 * ci + 32, :],
                            ubuf[32 * ci:32 * ci + 32, gi * 256:(gi + 1) * 256],
                            start=True, stop=first, tp=(32 * ci, 32 * ci), waits=w0,
                            inc="te" if first else None)
                if not first:
                    for mo in range(2):
                        t_u2mm = MM(pb[5][32 * ci:32 * ci + 32, 256:512],
                                    wnfm[:, i * 64 + mo * 32:i * 64 + (mo + 1) * 32],
                                    Sbf[:, mo * 256:(mo + 1) * 256],
                                    start=False, stop=(mo == 1), tp=(0, 32 * ci),
                                    inc="te" if mo == 1 else None)
                t_u2 = SC(lambda e, c=ci, s=slot: e.copy(
                    u2sb[32 * c:32 * c + 32, s * 256:(s + 1) * 256],
                    pb[5][32 * c:32 * c + 32, 256:512]), waits=[t_u2mm], inc="se")
                bg["u2" + str(slot)] = t_u2
                if not first:
                    for mo in range(2):
                        MM(pb[2][32 * ci:32 * ci + 32, 0:256],
                           qfm[:, mo * GA + i * 32:mo * GA + (i + 1) * 32],
                           Sbf[:, mo * 256:(mo + 1) * 256],
                           start=(mo == 0), stop=False, tp=(0, 32 * ci),
                           waits=(([prev_oevac[ci]] if prev_oevac[ci] else bgw(2))
                                  if mo == 0 else ()))
                MM(pb[2][32 * ci:32 * ci + 32, 0:256],
                   atm[32 * ci:32 * ci + 32, gi * 32:(gi + 1) * 32],
                   u2sb[32 * ci:32 * ci + 32, slot * 256:(slot + 1) * 256],
                   start=first, stop=True, tp=(32 * ci, 32 * ci),
                   waits=[t_u2] + (([prev_oevac[ci]] if prev_oevac[ci] else bgw(2))
                                   if first else []))
                for mo in range(2):
                    tmm = MM(pb[0][:, mo * 256:(mo + 1) * 256],
                             kntm[32 * ci:32 * ci + 32,
                                  gi * 256 + mo * 128:gi * 256 + (mo + 1) * 128],
                             u2sb[32 * ci:32 * ci + 32, slot * 256:(slot + 1) * 256],
                             start=True, stop=True, tp=(32 * ci, 0),
                             inc="te" if mo == 1 else None)
                t_smm = tmm
                prev_oevac[ci] = SC(lambda e, c=ci, g_=gi: e.activation(
                    otm[32 * c:32 * c + 32, g_ * 256:(g_ + 1) * 256],
                    pb[2][32 * c:32 * c + 32, 0:256], AF.Copy,
                    scale=rqc[32 * c:32 * c + 32, g_:g_ + 1]),
                    waits=[t_smm] + ([tok_rqc] if i == 0 else []), inc="se")
                V(lambda e: e.tensor_tensor(S32[:, :], pb[0][:, :], S32[:, :], ALU.add),
                  waits=[t_smm])
                prev_sevac = V(lambda e: e.tensor_copy(Sbf[:, :], S32[:, :]), inc="ve")
            tok_scan = p.tok("ve")
            tok_scan_se = p.tok("se")

            if STAGE <= 5:
                return fin()

            dstt = None
            for tt in range(NTT):
                for mo in range(2):
                    w0 = [tok_scan, tok_scan_se] if (tt == 0 and mo == 0) else []
                    tmm = TR(pbt[:, 0:128], otm[:, tt * 256 + mo * 128:tt * 256 + (mo + 1) * 128],
                             identb[:, :], waits=w0 + bgw("t0"), inc="te")
                    ev = V(lambda e, t=tt, m=mo: e.tensor_copy(
                        dfm[:, m * GA + t * 128:m * GA + (t + 1) * 128], pbt[:, 0:128]),
                        waits=[tmm], inc="ve")
                    bg["t0"] = ev
                V(lambda e, t=tt: e.tensor_reduce(dmcol[:, t:t + 1], otm[:, t * 256:(t + 1) * 256],
                                                  AX.X, ALU.add),
                  waits=[tok_scan, tok_scan_se, tok_statfl] if tt == 0 else ())
                dstt = V(lambda e, t=tt: e.scalar_tensor_tensor(
                    sqbuf[:, 0:256], otm[:, t * 256:(t + 1) * 256], 1.0,
                    otm[:, t * 256:(t + 1) * 256], ALU.mult, ALU.mult,
                    accum_out=dmcol[:, NTT + t:NTT + t + 1]), inc="ve")
            td1 = V(lambda e: e.tensor_scalar(dmcol[:, :], dmcol[:, :], 1.0 / 256, None,
                                              ALU.mult), waits=[dstt], inc="ve")
            V(lambda e: e.tensor_copy(st6[:, :, 4], dmcol[:, 0:NTT]), waits=[td1])
            td2 = V(lambda e: e.tensor_tensor(stdw[:, :], dmcol[:, 0:NTT], dmcol[:, 0:NTT],
                                              ALU.mult), waits=[td1], inc="ve")
            td3 = V(lambda e: e.tensor_tensor(stdw[:, :], dmcol[:, NTT:2 * NTT], stdw[:, :],
                                              ALU.subtract), waits=[td2], inc="ve")
            V(lambda e: e.tensor_scalar(stdw[:, :], stdw[:, :], 0.0, None, ALU.max),
              waits=[td3], inc="ve")
            tds = SC(lambda e: e.activation(stdw[:, :], stdw[:, :], AF.Sqrt),
                     waits=[p.tok("ve")], inc="se")
            tok_statc = V(lambda e: e.tensor_copy(st6[:, :, 5], stdw[:, :]),
                          waits=[tds], inc="ve")
            tok_dfm = p.tok("ve")

            ccm = dram["ccm_ia"]
            ccs = dram["ccs_ia"]
            for rowbase, src, pitch, off in [(0, fsb, GA, 0), (256, flb, GA, 0),
                                             (512, dfm, GA, 0), (768, vfm, VAp, 30)]:
                for mo in range(2):
                    exch_wtoks.append(DMA(
                        "sync", "sw",
                        ccm.ap()[g * CCM + rowbase + mo * 128:
                                 g * CCM + rowbase + (mo + 1) * 128, :],
                        src[:, mo * pitch + off:mo * pitch + off + GA],
                        waits=[tok_dfm]))
            exch_wtoks.append(DMA("gpsimd", "gw",
                                  ccs.ap()[:, g * CCS:(g + 1) * CCS],
                                  statc[:, :], waits=[tok_statc]))
            exch_sw_last[g] = p.tok("sw")

            return fin()

        exch_wtoks = []
        exch_sw_last = {}
        bar = [p.tok("ve"), tok_const_sl, tok_const_gl]
        for g in range(NGB):
            bar = a_group(g, bar)
        if STAGE >= 6:
            p.op("gpsimd", lambda e: e.collective_compute(
                "AllGather", ALU.bypass, replica_groups=[[0, 1, 2, 3], [4, 5, 6, 7]],
                ins=[dram["ccs_ia"].ap().opt()], outs=[dram["ccs_oa"].ap().opt()]),
                [p.tok("gw")], [("cc", 1)])
            tok_cc_stats = p.tok("cc")
            gl_ = NGB - 1
            p.op("gpsimd", lambda e, g_=gl_: e.collective_compute(
                "AllGather", ALU.bypass, replica_groups=[[0, 1, 2, 3], [4, 5, 6, 7]],
                ins=[dram["ccm_ia"].ap()[g_ * CCM:(g_ + 1) * CCM, :].opt()],
                outs=[dram["ccm_oa"].ap()[g_ * 4 * CCM:(g_ + 1) * 4 * CCM, :].opt()]),
                [exch_sw_last[gl_]], [("cc", 1)])
        tok_cc_all = p.tok("cc") if p.sems.get("cc") else None
        if not p.sems.get("cc"):
            tok_cc_stats = None

        BSTAGE = B_STAGE
        bstart = bar
        p.op("tensor", lambda e: e.nop(), bstart, [("te", 1)])
        p.op("vector", lambda e: e.engine_nop(), bstart, [("ve", 1)])
        p.op("scalar", lambda e: e.nop(), bstart, [("se", 1)])
        p.op("gpsimd", lambda e: e.engine_nop(), bstart, [])
        p.op("sync", lambda e: e.nop(), bstart, [])

        if BSTAGE == 0 or STAGE < 6:
            tok_z = V(lambda e: e.memset(outb[:, :], 0.0), inc="ve")
            for r in range(TB // 128):
                DMA("sync", "sw", dram["out"].ap()[r * 128:(r + 1) * 128, :], outb[:, :],
                    waits=[tok_z])
        else:
            xb = dram["xb_fm"]
            t_wr2 = DMA("gpsimd", "gl",
                        wr2b.ap().rearrange("p (c n) -> p c n", c=16),
                        dram["wr2"].ap().rearrange("(c p) n -> p c n", p=128))
            t_wid = DMA("gpsimd", "gl",
                        widb.ap().rearrange("p (c n) -> p c n", c=8),
                        dram["wid"].ap().rearrange("(c p) n -> p c n", p=128))
            t_gwt = DMA("sync", "sl",
                        gwtb.ap().rearrange("p (c n) -> p c n", c=8),
                        dram["gwt"].ap().rearrange("(c p) n -> p c n", p=128))
            for nm, buf in [("br1", br1c), ("br2", br2c), ("bid", bidc), ("alpha", alphac),
                            ("onwc", onwcb), ("ltg", ltgr), ("lth", lthr)]:
                DMA("sync", "sl", buf[:, :], dram[nm].ap())
            t_bload = p.tok("sl")
            t_gload = p.tok("gl")
            t_sa = SC(lambda e: e.activation(sigac[:, :], alphac[:, :], AF.Sigmoid),
                      waits=[t_bload], inc="se")
            t_eg = SC(lambda e: e.activation(ltgr[:, :], ltgr[:, :], AF.Exp), inc="se")
            t_eh = SC(lambda e: e.activation(lthr[:, :], lthr[:, :], AF.Exp), inc="se")
            t_t1 = V(lambda e: e.tensor_copy(
                tau12.ap().rearrange("o (g x) -> o g x", g=2),
                ltgr.ap().rearrange("o (g x) -> o g x", x=1).broadcast_to([1, 2, 6])),
                waits=[t_eg], inc="ve")
            t_t2 = V(lambda e: e.scalar_tensor_tensor(
                tau12.ap().rearrange("o (g x) -> o g x", g=4),
                lthr.ap().rearrange("o (g x) -> o g x", x=1).broadcast_to([1, 4, 3]),
                0.0, tau12.ap().rearrange("o (g x) -> o g x", g=4), ALU.mult, ALU.add),
                waits=[t_eh, t_t1], inc="ve")
            t_t3 = V(lambda e: e.reciprocal(tau12r[:, :], tau12[:, :]), waits=[t_t2], inc="ve")
            tmm = MM(pb[5][:, 0:12], onesf[:, :], tau12r[0:1, 0:12], start=True, stop=True,
                     waits=[t_t3] + bgw(5), inc="te")
            t_taub = V(lambda e: e.tensor_copy(taub[:, :], pb[5][:, 0:12]), waits=[tmm], inc="ve")
            bg[5] = t_taub

            wr1c = [0]
            woc = [0]
            ec = [0]
            wr1guard = {0: None, 1: None}
            woguard = {0: None, 1: None}

            def b_half(hb):
                thc = hb * TBH
                t_bxf = DMA("gpsimd", "gl",
                            bxf.ap().rearrange("p (c t) -> p c t", c=8),
                            xb.ap().rearrange("(c p) t -> p c t", p=128)[:, :, thc:thc + TBH],
                            waits=[p.tok("ve"), t_gload])
                t_xz = DMA("sync", "sl",
                           xz.ap().rearrange("p (c t) -> p c t", c=8),
                           xb.ap().rearrange("(c p) t -> p c t", p=128)[:, :, thc:thc + TBH],
                           waits=[p.tok("ve")])
                sst_src = dram["ccs_oa"].ap().rearrange("(j p) c -> p j c", p=128)[
                    :, :, hb * 48:(hb + 1) * 48]
                p.op("sync",
                     lambda e, o=sstat.ap().rearrange("p (j c) -> p j c", j=4), i=sst_src:
                         e.dma_start(out=o, in_=_dc.replace(
                             i, offset=i.offset + _scoff_get(e) * (NG * CCS // 4) * 4
                             if False else i.offset + _scoff_get(e) * CCS)),
                     [t_xz, tok_cc_stats], [("sl", 16)])
                t_sst = p.tok("sl")
                t_rz = V(lambda e: e.memset(rinstat[:, :], 0.0), waits=[p.tok("te")], inc="ve")
                for j in range(4):
                    for tt in range(8):
                        w0 = [t_sst, t_rz] + bgw("t0")
                        tmm = TR(pbt[0:6, 0:128],
                                 sstat[:, j * 48 + tt * 6:j * 48 + (tt + 1) * 6],
                                 identb[:, :], waits=w0, inc="te")
                        ev = V(lambda e, j_=j, t=tt: e.tensor_copy(
                            rinstat[32 * j_:32 * j_ + 6, t * 128:(t + 1) * 128],
                            pbt[0:6, 0:128]), waits=[tmm], inc="ve")
                        bg["t0"] = ev
                tok_rin = p.tok("ve")

                for sg in range(2):
                    for mo in range(16):
                        slot = mo % 2
                        wr1c[0] += 1
                        t_w = DMA("gpsimd", "wr1l" if slot == 0 else "wr1m",
                                  wr1b.ap()[:, slot * 1152:(slot + 1) * 1152]
                                  .rearrange("p (c n) -> p c n", c=9),
                                  dram["wr1"].ap().rearrange("(c p) n -> p c n", p=128)
                                  [:, :, mo * 128:(mo + 1) * 128],
                                  waits=[wr1guard[slot]])
                        bank = 1 + mo % 2
                        for ki in range(9):
                            rhs = bxf[:, ki * TBH + sg * 512:ki * TBH + (sg + 1) * 512]                                 if ki < 8 else rinstat[:, sg * 512:(sg + 1) * 512]
                            w0 = ([t_w, t_bxf, tok_rin] + bgw(bank)) if ki == 0 else []
                            tmm = MM(pb[bank][:, 0:512],
                                     wr1b[:, slot * 1152 + ki * 128:slot * 1152 + (ki + 1) * 128],
                                     rhs, start=(ki == 0), stop=(ki == 8), waits=w0,
                                     inc="te" if ki == 8 else None)
                        wr1guard[slot] = tmm
                        ev = SC(lambda e, m=mo, s=sg, b=bank: e.activation(
                            hmid[:, m * 512:(m + 1) * 512], pb[b][:, 0:512], AF.Gelu,
                            bias=br1c[:, m:m + 1]), waits=[tmm, t_bload], inc="se")
                        bg[bank] = ev
                    t_hm = p.tok("se")
                    for ki in range(16):
                        w0 = ([t_hm, t_gload] + bgw(3)) if ki == 0 else []
                        tmm = MM(pb[3][0:12, 0:512], wr2b[:, ki * 12:(ki + 1) * 12],
                                 hmid[:, ki * 512:(ki + 1) * 512],
                                 start=(ki == 0), stop=(ki == 15), waits=w0,
                                 inc="te" if ki == 15 else None)
                    ev = V(lambda e, s=sg: e.tensor_scalar(
                        logfm[0:12, s * 512:(s + 1) * 512], pb[3][0:12, 0:512],
                        br2c[0:12, 0:1], None, ALU.add), waits=[tmm, t_bload], inc="ve")
                    bg[3] = ev
                tok_log = p.tok("ve")

                for tt in range(8):
                    w0 = ([tok_log] if tt == 0 else []) + bgw(2)
                    tmm = TR(pb[2][:, 0:12], logfm[0:12, tt * 128:(tt + 1) * 128],
                             identf[0:12, 0:12], waits=w0, inc="te")
                    ev = V(lambda e, t=tt: e.tensor_copy(ltm[:, t * 12:(t + 1) * 12],
                                                         pb[2][:, 0:12]), waits=[tmm], inc="ve")
                    bg[2] = ev
                for tt in range(8):
                    pv = ptm.ap().rearrange("p (t c) -> p t c", c=12)[:, tt]
                    pv3 = ptm.ap().rearrange("p (t h c) -> p t h c", t=8, h=4)[:, tt]
                    t1 = V(lambda e, t=tt: e.tensor_tensor(
                        ptm[:, t * 12:(t + 1) * 12], ltm[:, t * 12:(t + 1) * 12],
                        taub[:, :], ALU.mult), waits=[t_taub], inc="ve")
                    t2 = V(lambda e, v3=pv3: e.tensor_reduce(p4a[:, :], v3, AX.X, ALU.max,
                                                             negate=True),
                           waits=[t1], inc="ve")
                    t3 = V(lambda e, v3=pv3: e.tensor_tensor(
                        v3, v3, p4a.ap().rearrange("p (h x) -> p h x", x=1)
                        .broadcast_to([128, 4, 3]), ALU.add), waits=[t2], inc="ve")
                    t4 = SC(lambda e, t=tt: e.activation(ptm[:, t * 12:(t + 1) * 12],
                                                         ptm[:, t * 12:(t + 1) * 12], AF.Exp),
                            waits=[t3], inc="se")
                    t5 = V(lambda e, v3=pv3: e.tensor_reduce(p4a[:, :], v3, AX.X, ALU.add),
                           waits=[t4], inc="ve")
                    t6 = V(lambda e: e.reciprocal(p4a[:, :], p4a[:, :]), waits=[t5], inc="ve")
                    t7 = V(lambda e, v3=pv3: e.tensor_tensor(
                        v3, v3, p4a.ap().rearrange("p (h x) -> p h x", x=1)
                        .broadcast_to([128, 4, 3]), ALU.mult), waits=[t6], inc="ve")
                    t8 = V(lambda e, t=tt: e.tensor_scalar(
                        ptm[:, t * 12:(t + 1) * 12], ptm[:, t * 12:(t + 1) * 12],
                        1.0 - 3.0 * R_EPS, R_EPS, ALU.mult, ALU.add), waits=[t7], inc="ve")
                tok_pfm = p.tok("ve")
                if DBG and hb == 0:
                    bg["dwc"] = DMA("sync", "dw", dram["d_ptm"].ap(), ptm[:, :],
                                    waits=[tok_pfm, bg.get("dwc")])

                exj = [exch, exch2]
                t_exl = {}
                prev_exl = [None, None]
                for sg in range(2):
                    rowdma = []
                    for tl in range(4):
                        tt = sg * 4 + tl
                        w0 = ([tok_pfm] if tl == 0 else []) + bgw(2)
                        tmm = TR(pb[2][0:12, 0:128], ptm[:, tt * 12:(tt + 1) * 12],
                                 identf[0:128, 0:128], waits=w0, inc="te")
                        ev = V(lambda e: e.tensor_copy(pfm12[0:12, 0:128], pb[2][0:12, 0:128]),
                               waits=[tmm, rowdma[-1] if rowdma else None], inc="ve")
                        bg[2] = ev
                        rowdma.append(DMA(
                            "sync", "prw",
                            prow.ap().rearrange("o (r t) -> o r t", r=12)[:, :, tl * 128:(tl + 1) * 128],
                            pfm12.ap()[0:12, 0:128], waits=[ev]))
                    for ki in range(8):
                        w0 = ([t_gload, t_bxf] + bgw(4)) if ki == 0 else []
                        tmm = MM(pb[4][0:4, 0:512], widb[:, ki * 4:(ki + 1) * 4],
                                 bxf[:, ki * TBH + sg * 512:ki * TBH + (sg + 1) * 512],
                                 start=(ki == 0), stop=(ki == 7), waits=w0,
                                 inc="te" if ki == 7 else None)
                    t_id1 = SC(lambda e: e.activation(idt4[:, :], pb[4][0:4, 0:512], AF.Sigmoid,
                                                      bias=bidc[0:4, 0:1]),
                               waits=[tmm, t_bload, t_sa], inc="se")
                    ev = V(lambda e: e.tensor_scalar(
                        idb4[0:4, 0:512], idt4[:, :], sigac[0:4, 0:1],
                        EPS_ID, ALU.mult, ALU.add), waits=[t_id1], inc="ve")
                    bg[4] = ev
                    rowdma.append(DMA("sync", "prw",
                                      irow.ap().rearrange("o (r t) -> o r t", r=4),
                                      idb4.ap()[0:4, 0:512], waits=[ev]))
                    tok_rows = p.tok("prw")
                    for j in range(4):
                        eb = exj[j % 2]
                        guard = bg.get("ex" + str(j % 2))
                        src_ap = dram["ccm_oa"].ap().rearrange("(c p) t -> p c t", p=128)[
                            :, 8 * j:8 * j + 8,
                            hb * TBH + sg * 512:hb * TBH + (sg + 1) * 512]
                        qp = j % 2
                        p.op("sync" if qp == 0 else "scalar",
                             lambda e, o=eb.ap().rearrange("p (c t) -> p c t", c=8), i=src_ap:
                                 e.dma_start(out=o, in_=dynoff(e, i, 4 * CCM)),
                             [guard, prev_exl[qp], tok_cc_all,
                              tok_pfm if (sg == 0 and j <= 1) else None],
                             [("exl" if qp == 0 else "ex2", 16)])
                        t_exl[(sg, j)] = p.tok("exl" if qp == 0 else "ex2")
                        prev_exl[qp] = t_exl[(sg, j)]
                        cbev = []
                        for r in range(3):
                            tmm = MM(pb[3][:, 0:512], onesb[:, :],
                                     prow[0:1, (3 * j + r) * 512:(3 * j + r + 1) * 512],
                                     start=True, stop=True,
                                     waits=[tok_rows] + bgw(3), inc="te")
                            ev = V(lambda e, r_=r: e.tensor_copy(
                                cbuf[:, r_ * 512:(r_ + 1) * 512], pb[3][:, 0:512]),
                                waits=[tmm], inc="ve")
                            bg[3] = ev
                            cbev.append(ev)
                        tmm = MM(pb[4][:, 0:512], onesb[:, :],
                                 irow[0:1, j * 512:(j + 1) * 512],
                                 start=True, stop=True, waits=[tok_rows] + bgw(4), inc="te")
                        ev = V(lambda e: e.tensor_copy(cbuf[:, 3 * 512:4 * 512], pb[4][:, 0:512]),
                               waits=[tmm], inc="ve")
                        bg[4] = ev
                        cbev.append(ev)
                        for mo in range(2):
                            ts0 = V(lambda e, b=eb, m=mo: e.tensor_tensor(
                                obuf[:, m * 512:(m + 1) * 512],
                                b[:, (0 + m) * 512:(1 + m) * 512], cbuf[:, 0:512], ALU.mult),
                                waits=cbev + [t_exl[(sg, j)]], inc="ve")
                            for t_i, cb_i in [(2, 1), (4, 2), (6, 3)]:
                                ts1 = V(lambda e, b=eb, m=mo, ti=t_i, ci=cb_i: e.tensor_tensor(
                                    scr[:, 0:512], b[:, (ti + m) * 512:(ti + 1 + m) * 512],
                                    cbuf[:, ci * 512:(ci + 1) * 512], ALU.mult),
                                    inc="ve")
                                ts0 = V(lambda e, m=mo: e.tensor_tensor(
                                    obuf[:, m * 512:(m + 1) * 512],
                                    obuf[:, m * 512:(m + 1) * 512], scr[:, 0:512], ALU.add),
                                    waits=[ts1, ts0], inc="ve")
                        tok_o = ts0
                        for mo in range(2):
                            tsq = SC(lambda e, m=mo: e.square(scr[:, 0:512],
                                                              obuf[:, m * 512:(m + 1) * 512]),
                                     waits=[tok_o, p.tok("te")], inc="se")
                            tmm = MM(pb[5][0:1, 0:512], colb256[:, :], scr[:, 0:512],
                                     start=(mo == 0), stop=(mo == 1),
                                     waits=[tsq] + (bgw(5) if mo == 0 else []), inc="te")
                        tr1 = V(lambda e: e.tensor_scalar(rmsrow[0:1, 0:512], pb[5][0:1, 0:512],
                                                          1e-5, None, ALU.add),
                                waits=[tmm], inc="ve")
                        bg[5] = tr1
                        tr2 = V(lambda e: e.reciprocal(rmsrow[0:1, 0:512], rmsrow[0:1, 0:512]),
                                waits=[tr1], inc="ve")
                        tr3 = SC(lambda e: e.activation(rmsrow[0:1, 0:512], rmsrow[0:1, 0:512], AF.Sqrt),
                                 waits=[tr2], inc="se")
                        tmm = MM(pb[5][:, 0:512], onesf[:, :], rmsrow[0:1, 0:512],
                                 start=True, stop=True, waits=[tr3] + bgw(5), inc="te")
                        tr4 = V(lambda e: e.tensor_copy(rbc[:, :], pb[5][:, 0:512]),
                                waits=[tmm], inc="ve")
                        bg[5] = tr4
                        for mo in range(2):
                            ev = V(lambda e, j_=j, m=mo: e.scalar_tensor_tensor(
                                ofm[:, (j_ * 2 + m) * 512:(j_ * 2 + m + 1) * 512],
                                obuf[:, m * 512:(m + 1) * 512], onwcb[:, m:m + 1],
                                rbc[:, :], ALU.mult, ALU.mult),
                                waits=[tr4, bg.get("ofmdump")], inc="ve")
                        bg["ex" + str(j % 2)] = ev
                    tok_ofm = p.tok("ve")
                    if DBG and hb == 0 and sg == 0:
                        bg["ofmdump"] = bg["dwc"] = DMA(
                            "sync", "dw", dram["d_ofm"].ap(), ofm[:, :],
                            waits=[tok_ofm, bg.get("dwc")])
                    for mo in range(8):
                        slot = mo % 2
                        woc[0] += 1
                        t_w = DMA("gpsimd", "wol" if slot == 0 else "wom",
                                  wob.ap()[:, slot * 1024:(slot + 1) * 1024]
                                  .rearrange("p (c n) -> p c n", c=8),
                                  dram["wo"].ap().rearrange("(c p) n -> p c n", p=128)
                                  [:, :, mo * 128:(mo + 1) * 128],
                                  waits=[woguard[slot]])
                        bank = 1 + mo % 2
                        for ki in range(8):
                            w0 = ([t_w, tok_ofm] + bgw(bank)) if ki == 0 else []
                            tmm = MM(pb[bank][:, 0:512],
                                     wob[:, slot * 1024 + ki * 128:slot * 1024 + (ki + 1) * 128],
                                     ofm[:, ki * 512:(ki + 1) * 512],
                                     start=(ki == 0), stop=(ki == 7), waits=w0,
                                     inc="te" if ki == 7 else None)
                        woguard[slot] = tmm
                        ev = V(lambda e, m=mo, s=sg, b=bank: e.tensor_tensor(
                            xz[:, m * TBH + s * 512:m * TBH + (s + 1) * 512],
                            pb[b][:, 0:512],
                            xz[:, m * TBH + s * 512:m * TBH + (s + 1) * 512], ALU.add),
                            waits=[tmm, t_xz], inc="ve")
                        bg[bank] = ev
                tok_x1 = p.tok("ve")

                for sg in range(2):
                    cts = []
                    for ki in range(8):
                        cts.append(V(lambda e, k=ki, s=sg: e.tensor_copy(
                            bxf[:, k * TBH + s * 512:k * TBH + (s + 1) * 512],
                            xz[:, k * TBH + s * 512:k * TBH + (s + 1) * 512]),
                            waits=[tok_x1] if ki == 0 else [], inc="ve"))
                    for ki in range(8):
                        w0m = ([cts[-1]] + bgw(4)) if ki == 0 else []
                        tmm_mu = MM(pb[4][0:1, 0:512], colb1024[:, :],
                                    bxf[:, ki * TBH + sg * 512:ki * TBH + (sg + 1) * 512],
                                    start=(ki == 0), stop=(ki == 7), waits=w0m,
                                    inc="te" if ki == 7 else None)
                    for ki in range(8):
                        tsq = SC(lambda e, k=ki, s=sg: e.square(
                            scr[:, 0:512], bxf[:, k * TBH + s * 512:k * TBH + (s + 1) * 512]),
                            waits=[cts[-1], p.tok("te")], inc="se")
                        tmm_m2 = MM(pb[5][0:1, 0:512], colb1024[:, :], scr[:, 0:512],
                                    start=(ki == 0), stop=(ki == 7),
                                    waits=[tsq] + (bgw(5) if ki == 0 else []),
                                    inc="te")
                    l0 = V(lambda e: e.tensor_copy(lnrow[0:1, 0:512], pb[4][0:1, 0:512]),
                           waits=[tmm_mu, tmm_m2], inc="ve")
                    bg[4] = l0
                    l1 = V(lambda e: e.tensor_copy(lnrow[0:1, 512:1024], pb[5][0:1, 0:512]),
                           waits=[tmm_m2], inc="ve")
                    bg[5] = l1
                    l2 = V(lambda e: e.tensor_tensor(lnrow[0:1, 1024:1536], lnrow[0:1, 0:512],
                                                     lnrow[0:1, 0:512], ALU.mult),
                           waits=[l0, l1], inc="ve")
                    l3 = V(lambda e: e.tensor_tensor(lnrow[0:1, 1024:1536],
                                                     lnrow[0:1, 512:1024],
                                                     lnrow[0:1, 1024:1536], ALU.subtract),
                           waits=[l2], inc="ve")
                    l4 = V(lambda e: e.tensor_scalar(lnrow[0:1, 1024:1536],
                                                     lnrow[0:1, 1024:1536], 1e-5, None, ALU.add),
                           waits=[l3], inc="ve")
                    l5 = V(lambda e: e.reciprocal(lnrow[0:1, 1024:1536], lnrow[0:1, 1024:1536]),
                           waits=[l4], inc="ve")
                    l6 = SC(lambda e: e.activation(lnrow[0:1, 1024:1536], lnrow[0:1, 1024:1536],
                                                   AF.Sqrt), waits=[l5], inc="se")
                    l7 = V(lambda e: e.tensor_scalar(lnrow[0:1, 1536:2048], lnrow[0:1, 0:512],
                                                     -1.0, None, ALU.mult), waits=[l6], inc="ve")
                    tmm = MM(pb[4][:, 0:512], onesf[:, :], lnrow[0:1, 1024:1536],
                             start=True, stop=True, waits=[l6] + bgw(4), inc="te")
                    b1 = V(lambda e: e.tensor_copy(rbc[:, :], pb[4][:, 0:512]),
                           waits=[tmm], inc="ve")
                    bg[4] = b1
                    tmm = MM(pb[5][:, 0:512], onesf[:, :], lnrow[0:1, 1536:2048],
                             start=True, stop=True, waits=[l7] + bgw(5), inc="te")
                    b2 = V(lambda e: e.tensor_copy(mrbc[:, :], pb[5][:, 0:512]),
                           waits=[tmm], inc="ve")
                    bg[5] = b2
                    for ki in range(8):
                        a1 = V(lambda e, k=ki, s=sg: e.tensor_tensor(
                            tmpf[:, :], xz[:, k * TBH + s * 512:k * TBH + (s + 1) * 512],
                            mrbc[:, :], ALU.add), waits=[b1, b2], inc="ve")
                        a2 = V(lambda e, k=ki, s=sg: e.tensor_tensor(
                            xz[:, k * TBH + s * 512:k * TBH + (s + 1) * 512], tmpf[:, :],
                            rbc[:, :], ALU.mult), waits=[a1], inc="ve")
                        V(lambda e, k=ki, s=sg: e.tensor_copy(
                            zb[:, k * TBH + s * 512:k * TBH + (s + 1) * 512],
                            xz[:, k * TBH + s * 512:k * TBH + (s + 1) * 512]),
                            waits=[a2, bg.get("zbdump")], inc="ve")
                tok_z = p.tok("ve")
                if DBG and hb == 0:
                    bg["zbdump"] = bg["dwc"] = DMA("sync", "dw", dram["d_zb"].ap(), zb[:, :],
                                                   waits=[tok_z, bg.get("dwc")])

                for tt in range(8):
                    for ki in range(8):
                        w0 = ([tok_z, t_bload] + bgw(2)) if ki == 0 else []
                        tmm = MM(pb[2][:, 0:8],
                                 xz[:, ki * TBH + tt * 128:ki * TBH + (tt + 1) * 128],
                                 gwtb[:, ki * 8:(ki + 1) * 8],
                                 start=(ki == 0), stop=(ki == 7), waits=w0,
                                 inc="te" if ki == 7 else None)
                    gl = V(lambda e: e.tensor_copy(g8t[:, :], pb[2][:, 0:8]),
                           waits=[tmm], inc="ve")
                    bg[2] = gl
                    s1 = V(lambda e: e.tensor_reduce(gc1[:, :], g8t[:, :], AX.X, ALU.max,
                                                     negate=True), waits=[gl], inc="ve")
                    s2 = V(lambda e: e.tensor_scalar(g8t[:, :], g8t[:, :], gc1[:, 0:1], None,
                                                     ALU.add), waits=[s1], inc="ve")
                    s3 = SC(lambda e: e.activation(g8t[:, :], g8t[:, :], AF.Exp),
                            waits=[s2], inc="se")
                    s4 = V(lambda e: e.tensor_reduce(gc1[:, :], g8t[:, :], AX.X, ALU.add),
                           waits=[s3], inc="ve")
                    s5 = V(lambda e: e.reciprocal(gc1[:, :], gc1[:, :]), waits=[s4], inc="ve")
                    s6 = V(lambda e: e.tensor_scalar(g8t[:, :], g8t[:, :], gc1[:, 0:1], None,
                                                     ALU.mult), waits=[s5], inc="ve")
                    s7 = V(lambda e: e.tensor_reduce(gc1[:, :], g8t[:, :], AX.X, ALU.max),
                           waits=[s6], inc="ve")
                    s8 = V(lambda e: e.tensor_scalar(tmpf[:, 0:8], g8t[:, :], gc1[:, 0:1],
                                                     -1e30, ALU.is_ge, ALU.mult),
                           waits=[s7], inc="ve")
                    s9 = V(lambda e: e.tensor_tensor(tmpf[:, 0:8], g8t[:, :], tmpf[:, 0:8],
                                                     ALU.add), waits=[s8], inc="ve")
                    sa_ = V(lambda e: e.tensor_reduce(gc2[:, :], tmpf[:, 0:8], AX.X, ALU.max),
                            waits=[s9], inc="ve")
                    sb_ = V(lambda e: e.tensor_scalar(tmpf[:, 0:8], g8t[:, :], gc2[:, 0:1],
                                                      None, ALU.is_ge), waits=[sa_], inc="ve")
                    sc_ = V(lambda e: e.tensor_tensor(gc1[:, :], gc1[:, :], gc2[:, :], ALU.add),
                            waits=[sa_], inc="ve")
                    sd_ = V(lambda e: e.tensor_scalar(gc1[:, :], gc1[:, :], 1e-20, None,
                                                      ALU.add), waits=[sc_], inc="ve")
                    se_ = V(lambda e: e.reciprocal(gc1[:, :], gc1[:, :]), waits=[sd_], inc="ve")
                    sf_ = V(lambda e: e.tensor_scalar(gc1[:, :], gc1[:, :], 2.5, None, ALU.mult),
                            waits=[se_], inc="ve")
                    sg_ = V(lambda e: e.tensor_tensor(tmpf[:, 0:8], g8t[:, :], tmpf[:, 0:8],
                                                      ALU.mult), waits=[sb_, sf_], inc="ve")
                    sh_ = V(lambda e, t=tt: e.tensor_scalar(combt[:, t * 8:(t + 1) * 8],
                                                            tmpf[:, 0:8], gc1[:, 0:1], None,
                                                            ALU.mult),
                            waits=[sg_, bg.get("dwc") if hb == 1 else None], inc="ve")
                    if DBG and hb == 0:
                        bg["dwc"] = DMA("sync", "dw",
                                        dram["d_gl8"].ap()[:, tt * 8:(tt + 1) * 8],
                                        combt[:, tt * 8:(tt + 1) * 8],
                                        waits=[sh_, bg.get("dwc")])
                    tmm = TR(pbt[0:8, 0:128], combt[:, tt * 8:(tt + 1) * 8], identb[:, :],
                             waits=[sh_] + bgw("t0"), inc="te")
                    ev = V(lambda e: e.tensor_copy(comb8[0:8, 0:128], pbt[0:8, 0:128]),
                           waits=[tmm, bg.get("c8")], inc="ve")
                    bg["t0"] = ev
                    bg["c8"] = DMA("sync", "crw",
                                   crow.ap().rearrange("o (r t) -> o r t", r=8)
                                   [:, :, tt * 128:(tt + 1) * 128],
                                   comb8.ap()[0:8, 0:128], waits=[ev])
                tok_comb = p.tok("ve")
                tok_crow = p.tok("crw")

                if BSTAGE < 2:
                    tok_zz = V(lambda e: e.memset(outb[:, :], 0.0), waits=[tok_comb], inc="ve")
                    for r in range(TBH // 128):
                        DMA("sync", "sw",
                            dram["out"].ap()[hb * TBH + r * 128:hb * TBH + (r + 1) * 128, :],
                            outb[:, :], waits=[tok_zz])
                    return

                def wsrc(e_, which):
                    if which == 0:
                        return dram["wg"].ap()[e_] if e_ < 8 else                             dram["swg"].ap()[:, (e_ - 8) * FF:(e_ - 7) * FF]
                    if which == 1:
                        return dram["wu"].ap()[e_] if e_ < 8 else                             dram["swu"].ap()[:, (e_ - 8) * FF:(e_ - 7) * FF]
                    return dram["wd"].ap()[e_] if e_ < 8 else                         dram["swd"].ap()[(e_ - 8) * FF:(e_ - 7) * FF, :]

                wguard = {"g": None, "u": None, "d": None}
                for e_ in range(10):
                    ec[0] += 1
                    fw = [tok_comb, p.tok("te"), p.tok("se")] if e_ == 0 else []
                    t_wg = DMA("gpsimd", "wgl",
                               wgb.ap().rearrange("p (c f) -> p c f", c=8),
                               wsrc(e_, 0).rearrange("(c p) f -> p c f", p=128),
                               waits=[wguard["g"]] + fw)
                    t_wu = DMA("gpsimd", "wul",
                               wub.ap().rearrange("p (c f) -> p c f", c=8),
                               wsrc(e_, 1).rearrange("(c p) f -> p c f", p=128),
                               waits=[wguard["u"]] + fw)
                    t_wd = DMA("gpsimd", "wdl",
                               wdb.ap().rearrange("p (c f) -> p c f", c=11),
                               wsrc(e_, 2).rearrange("(c p) f -> p c f", p=128),
                               waits=[wguard["d"]] + fw)
                    for sg in range(2):
                        if e_ < 8:
                            tmm = MM(pb[3][:, 0:512], onesb[:, :],
                                     crow[0:1, e_ * TBH + sg * 512:e_ * TBH + (sg + 1) * 512],
                                     start=True, stop=True,
                                     waits=[tok_comb, tok_crow] + bgw(3), inc="te")
                            cb = V(lambda e: e.tensor_copy(cbuf[:, 4 * 512:5 * 512],
                                                           pb[3][:, 0:512]),
                                   waits=[tmm], inc="ve")
                            bg[3] = cb
                        lastg = None
                        for fo in range(11):
                            gbank = [1, 0][fo % 2]
                            ubank = [2, 5][fo % 2]
                            for ki in range(8):
                                w0 = ([t_wg, tok_z] + bgw(gbank)) if ki == 0 else []
                                tmm = MM(pb[gbank][:, 0:512],
                                         wgb[:, ki * FF + fo * 128:ki * FF + (fo + 1) * 128],
                                         zb[:, ki * TBH + sg * 512:ki * TBH + (sg + 1) * 512],
                                         start=(ki == 0), stop=(ki == 7), waits=w0,
                                         inc="te" if ki == 7 else None)
                            if sg == 1 and fo == 10:
                                wguard["g"] = tmm
                            tsil = SC(lambda e, b=gbank, f=fo: e.activation(
                                gsil[:, (f % 2) * 256:(f % 2) * 256 + 512]
                                if False else gsil[:, 0:512],
                                pb[b][:, 0:512], AF.Silu),
                                waits=[tmm, bg.get("gsilr")], inc="se")
                            bg[gbank] = tsil
                            for ki in range(8):
                                w0 = ([t_wu] + bgw(ubank)) if ki == 0 else []
                                tmm = MM(pb[ubank][:, 0:512],
                                         wub[:, ki * FF + fo * 128:ki * FF + (fo + 1) * 128],
                                         zb[:, ki * TBH + sg * 512:ki * TBH + (sg + 1) * 512],
                                         start=(ki == 0), stop=(ki == 7), waits=w0,
                                         inc="te" if ki == 7 else None)
                            if sg == 1 and fo == 10:
                                wguard["u"] = tmm
                            tg1 = V(lambda e, b=ubank: e.tensor_tensor(
                                scr[:, 0:512], gsil[:, 0:512], pb[b][:, 0:512], ALU.mult),
                                waits=[tmm, tsil], inc="ve")
                            bg[ubank] = tg1
                            bg["gsilr"] = tg1
                            if e_ < 8:
                                lastg = V(lambda e, f=fo: e.tensor_tensor(
                                    gpr[:, f * 512:(f + 1) * 512], scr[:, 0:512],
                                    cbuf[:, 4 * 512:5 * 512], ALU.mult),
                                    waits=[tg1, cb], inc="ve")
                            else:
                                lastg = V(lambda e, f=fo: e.tensor_copy(
                                    gpr[:, f * 512:(f + 1) * 512], scr[:, 0:512]),
                                    waits=[tg1], inc="ve")
                        for dm in range(8):
                            bank = 3 + dm % 2
                            for ki in range(11):
                                w0 = ([t_wd, lastg] + bgw(bank)) if ki == 0 else []
                                tmm = MM(pb[bank][:, 0:512],
                                         wdb[:, ki * 1024 + dm * 128:ki * 1024 + (dm + 1) * 128],
                                         gpr[:, ki * 512:(ki + 1) * 512],
                                         start=(ki == 0), stop=(ki == 10), waits=w0,
                                         inc="te" if ki == 10 else None)
                            if sg == 1 and dm == 7:
                                wguard["d"] = tmm
                            ev = V(lambda e, d=dm, s=sg, b=bank: e.tensor_tensor(
                                xz[:, d * TBH + s * 512:d * TBH + (s + 1) * 512],
                                pb[b][:, 0:512],
                                xz[:, d * TBH + s * 512:d * TBH + (s + 1) * 512], ALU.add),
                                waits=[tmm], inc="ve")
                            bg[bank] = ev
                tok_ffn = p.tok("ve")

                for tt in range(8):
                    for mo8 in range(8):
                        w0 = ([tok_ffn] if (tt == 0 and mo8 == 0) else []) + bgw(5)
                        tmm = TR(pb[5][:, 0:128],
                                 xz[:, mo8 * TBH + tt * 128:mo8 * TBH + (tt + 1) * 128],
                                 identf[:, :], waits=w0, inc="te")
                        ev = V(lambda e, t=tt, m=mo8: e.tensor_copy(
                            x2t8[:, t * 1024 + m * 128:t * 1024 + (m + 1) * 128],
                            pb[5][:, 0:128]), waits=[tmm], inc="ve")
                        bg[5] = ev
                    r1 = V(lambda e, t=tt: e.tensor_reduce(
                        mucol[:, t:t + 1], x2t8[:, t * 1024:(t + 1) * 1024], AX.X, ALU.add),
                        inc="ve")
                    V(lambda e, t=tt: e.scalar_tensor_tensor(
                        x2sq[:, :], x2t8[:, t * 1024:(t + 1) * 1024], 1.0,
                        x2t8[:, t * 1024:(t + 1) * 1024], ALU.mult, ALU.mult,
                        accum_out=mucol[:, 8 + t:8 + t + 1]),
                      waits=[r1] + ([bg.get("outd1")] if tt == 0 else []), inc="ve")
                m0 = V(lambda e: e.tensor_scalar(mucol[:, :], mucol[:, :], 1.0 / 1024, None,
                                                 ALU.mult), inc="ve")
                m1 = V(lambda e: e.tensor_tensor(rcol[:, :], mucol[:, 0:8], mucol[:, 0:8],
                                                 ALU.mult), waits=[m0], inc="ve")
                m2_ = V(lambda e: e.tensor_tensor(rcol[:, :], mucol[:, 8:16], rcol[:, :],
                                                  ALU.subtract), waits=[m1], inc="ve")
                m3 = V(lambda e: e.tensor_scalar(rcol[:, :], rcol[:, :], 1e-5, None, ALU.add),
                       waits=[m2_], inc="ve")
                m4 = V(lambda e: e.reciprocal(rcol[:, :], rcol[:, :]), waits=[m3], inc="ve")
                m5 = SC(lambda e: e.activation(rcol[:, :], rcol[:, :], AF.Sqrt),
                        waits=[m4], inc="se")
                for tt in range(8):
                    ob = [outb, outb2][tt % 2]
                    gk = "outd" + str(tt % 2)
                    ap_ = V(lambda e, t=tt, o=ob: e.tensor_scalar(
                        o[:, :], x2t8[:, t * 1024:(t + 1) * 1024],
                        mucol[:, t:t + 1], rcol[:, t:t + 1], ALU.subtract, ALU.mult),
                        waits=[m5, bg.get(gk)], inc="ve")
                    bg[gk] = DMA(
                        "sync" if tt % 2 == 0 else "gpsimd",
                        "sw" if tt % 2 == 0 else "gw",
                        dram["out"].ap()[hb * TBH + tt * 128:hb * TBH + (tt + 1) * 128, :],
                        ob[:, :], waits=[ap_])

            for hb in range(NHB):
                b_half(hb)

        p.op("sync", lambda e: e.nop(),
             [p.tok("sw"), p.tok("gw"), p.tok("dw"), p.tok("cc")], [])

        block = ctx.enter_context(nc.Block())

        def make_runner(name):
            prog = p.bucket[name]

            def runner(eng):
                waited = {}
                for fn, waits, incs in prog:
                    need = {}
                    for s, v in waits:
                        if v > need.get(s, 0):
                            need[s] = v
                    for s, v in need.items():
                        if v > 0 and waited.get(s, 0) < v:
                            eng.wait_ge(semh[s], v)
                            waited[s] = v
                    inst = fn(eng)
                    for s, k in incs:
                        inst.then_inc(semh[s], k)
            return runner

        block.tensor(make_runner("tensor"))
        block.vector(make_runner("vector"))
        block.scalar(make_runner("scalar"))
        block.gpsimd(make_runner("gpsimd"))
        block.sync(make_runner("sync"))

    return nc


def prep_inputs(inp):
    f32 = np.float32
    x = np.asarray(inp["L0_attn_hidden_states"], f32)
    g = lambda n: np.asarray(inp[n], f32)
    Wr1 = g("L0_attn_Wr1")
    wr1p = np.zeros((1152, 2048), f32)
    wr1p[:1024] = Wr1[:1024]
    for h in range(H):
        for j in range(6):
            wr1p[1024 + h * 32 + j] = Wr1[1024 + j * 4 + h]
    consts = dict(
        identf=np.eye(128, dtype=f32),
        identb=np.eye(128, dtype=f32).astype(BF),
        eye32s=np.tile(np.eye(32, dtype=f32), (4, 1)).astype(BF),
        masksl=np.tile(-np.tril(np.ones((32, 32), f32), -1), (4, 1)).astype(BF),
        masktu=np.tile(np.triu(np.ones((32, 32), f32), 0), (4, 1)).astype(BF),
        onesf=np.ones((1, 128), f32),
        onesb=np.ones((1, 128), f32).astype(BF),
        colb1=np.ones((128, 1), f32).astype(BF),
        colb256=np.full((128, 1), 1.0 / 256, f32).astype(BF),
        colb1024=np.full((128, 1), 1.0 / 1024, f32).astype(BF),
    )
    shared = dict(
        wid=g("L0_attn_Wid"), bid=g("L0_attn_bid").reshape(H, 1),
        alpha=g("L0_attn_alpha_id").reshape(H, 1),
        wr1=wr1p, br1=np.ascontiguousarray(g("L0_attn_br1").reshape(16, 128).T),
        wr2=g("L0_attn_Wr2"), br2=g("L0_attn_br2").reshape(12, 1),
        ltg=g("L0_attn_log_tau_group").reshape(1, 2),
        lth=g("L0_attn_log_tau_head").reshape(1, 4),
        onwc=np.ascontiguousarray(g("L0_attn_o_norm_w").reshape(2, 128).T),
        wo=g("L0_attn_Wo"),
        gwt=np.ascontiguousarray(g("L0_ffn_gate_w").T),
        wg=g("L0_ffn_w_gate"), wu=g("L0_ffn_w_up"), wd=g("L0_ffn_w_down"),
        swg=g("L0_ffn_sw_gate"), swu=g("L0_ffn_sw_up"), swd=g("L0_ffn_sw_down"),
        **consts,
    )
    xT = [np.ascontiguousarray(x[b_].T) for b_ in range(B)]
    cores = []
    for c in range(8):
        b_, h_ = c // 4, c % 4
        cs, ce = h_ * DK, (h_ + 1) * DK
        d = dict(shared)
        d["x_fm"] = xT[b_]
        d["xb_fm"] = np.ascontiguousarray(xT[b_][:, h_ * TB:(h_ + 1) * TB])
        d["wq"] = np.ascontiguousarray(g("L0_attn_Wq")[:, cs:ce])
        d["wk"] = np.ascontiguousarray(g("L0_attn_Wk")[:, cs:ce])
        d["wv"] = np.ascontiguousarray(g("L0_attn_Wv")[:, cs:ce])
        d["wb"] = np.ascontiguousarray(g("L0_attn_Wb")[:, h_].reshape(8, 128).T)
        def diag_expand(w):
            K = w.shape[1]
            out = np.zeros((2 * K * 128, 128), f32)
            for mo in range(2):
                for j in range(K):
                    blk = (mo * K + j) * 128
                    np.fill_diagonal(out[blk:blk + 128], w[mo * 128:(mo + 1) * 128, j])
            return out
        d["convqd"] = diag_expand(g("L0_attn_conv_q")[cs:ce])
        d["convkd"] = diag_expand(g("L0_attn_conv_k")[cs:ce])
        d["convvd"] = diag_expand(g("L0_attn_conv_v")[cs:ce])
        d["firsd"] = diag_expand(g("L0_attn_fir_short")[h_])
        d["firld"] = diag_expand(g("L0_attn_fir_long")[h_])
        cores.append(d)
    return cores


_CACHE = {}


def kernel(**inputs):
    cores = prep_inputs(inputs)
    if "nc" not in _CACHE:
        nc = bass.Bass("TRN2", target_bir_lowering=False)
        _CACHE["nc"] = build(nc)
    nc = _CACHE["nc"]
    res = run_bass_kernel_spmd(nc, cores, core_ids=list(range(8)))
    out = np.zeros((B, L, D), np.float32)
    for c in range(8):
        b_, s_ = c // 4, c % 4
        out[b_, s_ * TB:(s_ + 1) * TB] = np.asarray(res.results[c]["out"], np.float32)
    kernel.results = res.results
    return out


